# version 04
# baseline (speedup 1.0000x reference)
import math
from contextlib import ExitStack

import numpy as np
import concourse.bass as bass
import concourse.mybir as mybir

F32 = mybir.dt.float32
BF16 = mybir.dt.bfloat16
AF = mybir.ActivationFunctionType
OP = mybir.AluOpType
AX = mybir.AxisListType

D = 2048
HD = 128
FOWN = 256
NCORE = 8
EPS = 1e-5
QS = np.float32(1.0 / math.sqrt(HD))
ESHIFT = np.float32(-20.0)
f32 = np.float32




class _AD:
    _OPS = {"memset", "tensor_copy", "tensor_tensor", "tensor_add",
            "tensor_sub", "tensor_mul", "tensor_max", "tensor_scalar",
            "tensor_single_scalar", "tensor_scalar_add", "tensor_scalar_sub",
            "tensor_scalar_mul", "tensor_scalar_max", "tensor_scalar_min",
            "reduce_sum", "reduce_max", "tensor_reduce",
            "scalar_tensor_tensor", "reciprocal", "activation", "copy",
            "select", "transpose"}

    def __init__(self, eng):
        self._e = eng
        self._wmax = {}

    def wait_ge(self, sem, val):
        key = id(sem)
        if val <= self._wmax.get(key, -1):
            return
        self._wmax[key] = val
        self._e.wait_ge(sem, val)

    def __getattr__(self, n):
        f = getattr(self._e, n)
        if n in _AD._OPS:
            def wrapped(*a, **k):
                r = f(*a, **k)
                self._e.drain()
                return r
            return wrapped
        return f


def build(T=8192, DFF=8192, phases=(1, 2, 3), dbg=()):
    TOK = T // NCORE
    NT = T // 128
    NG = NT // 2
    QR = T // 512
    NP = 2 * QR
    NFT = DFF // 128
    NSLAB = DFF // 512
    TOKT = TOK // 128
    TN = min(512, TOK)
    NTH = TOK // TN
    W1S = DFF // 8
    W2S = D // 8
    QW = min(4096, DFF)
    QBW = max(D, QW)
    assert T % 1024 == 0 and DFF % 512 == 0 and TOK % 128 == 0

    nc = bass.Bass("TRN2", target_bir_lowering=False)

    x_full = nc.dram_tensor("x_full", [T, D], F32, kind="ExternalInput").ap()
    x_own = nc.dram_tensor("x_own", [TOK, D], F32, kind="ExternalInput").ap()
    wq_in = nc.dram_tensor("wq_in", [D, FOWN], F32, kind="ExternalInput").ap()
    wk_in = nc.dram_tensor("wk_in", [D, FOWN], F32, kind="ExternalInput").ap()
    wv_in = nc.dram_tensor("wv_in", [D, FOWN], F32, kind="ExternalInput").ap()
    bq_in = nc.dram_tensor("bq_in", [128, 2], F32, kind="ExternalInput").ap()
    bk_in = nc.dram_tensor("bk_in", [128, 2], F32, kind="ExternalInput").ap()
    bv_in = nc.dram_tensor("bv_in", [128, FOWN], F32, kind="ExternalInput").ap()
    wp_in = nc.dram_tensor("wp_in", [FOWN, D], F32, kind="ExternalInput").ap()
    bp_in = nc.dram_tensor("bp_in", [128, D], F32, kind="ExternalInput").ap()
    b2_in = nc.dram_tensor("b2_in", [128, D], F32, kind="ExternalInput").ap()
    b1_in = nc.dram_tensor("b1_in", [128, NFT], F32, kind="ExternalInput").ap()
    w1_sh = nc.dram_tensor("w1_sh", [W1S, D], F32, kind="ExternalInput").ap()
    w2_sh = nc.dram_tensor("w2_sh", [W2S, DFF], F32, kind="ExternalInput").ap()
    id_in = nc.dram_tensor("id_in", [128, 128], BF16, kind="ExternalInput").ap()
    mk_in = nc.dram_tensor("mk_in", [4, 128, 512], F32, kind="ExternalInput").ap()

    out_own = nc.dram_tensor("out_own", [TOK, D], F32, kind="ExternalOutput").ap()

    qT_hi = nc.dram_tensor("qT_hi", [FOWN, T], BF16).ap()
    qT_lo = nc.dram_tensor("qT_lo", [FOWN, T], BF16).ap()
    kT_hi = nc.dram_tensor("kT_hi", [FOWN, T], BF16).ap()
    kT_lo = nc.dram_tensor("kT_lo", [FOWN, T], BF16).ap()
    v_sh = nc.dram_tensor("v_sh", [T, FOWN], BF16).ap()
    v_sl = nc.dram_tensor("v_sl", [T, FOWN], BF16).ap()
    proj_part = nc.dram_tensor("proj_part", [T, D], F32).ap()
    rs_out = nc.dram_tensor("rs_out", [TOK, D], F32).ap()
    ffn_acc = nc.dram_tensor("ffn_acc", [TOK, D], F32).ap()
    sc_part = nc.dram_tensor("sc_part", [128, 2], F32).ap()
    sc_sum = nc.dram_tensor("sc_sum", [128, 2], F32).ap()
    wq1_shd = nc.dram_tensor("wq1_shd", [D, W1S], BF16).ap()
    wq2_shd = nc.dram_tensor("wq2_shd", [DFF, W2S], BF16).ap()
    wq1_all = nc.dram_tensor("wq1_all", [8 * D, W1S], BF16).ap()
    wq2_all = nc.dram_tensor("wq2_all", [8 * DFF, W2S], BF16).ap()

    dbg_map = {"qT_hi": qT_hi, "qT_lo": qT_lo, "kT_hi": kT_hi, "kT_lo": kT_lo,
               "v_sh": v_sh, "v_sl": v_sl, "proj_part": proj_part,
               "rs_out": rs_out, "wq1_all": wq1_all, "wq2_all": wq2_all,
               "sc_sum": sc_sum, "ffn_acc": ffn_acc}
    dbg_outs = []
    for name in dbg:
        src = dbg_map[name]
        dst = nc.dram_tensor("dbg_" + name, list(src.shape), src.dtype,
                             kind="ExternalOutput").ap()
        dbg_outs.append((dst, src))

    qtiles = []
    for i in range(W1S // 128):
        dstT = bass.AP(wq1_shd.tensor, i * 128,
                       [[W1S, 128], [128 * W1S, D // 128], [1, 128]])
        qtiles.append((w1_sh[i * 128:(i + 1) * 128, :], dstT, D, 0))
    for i in range(W2S // 128):
        for hf in range(DFF // QW):
            dstT = bass.AP(wq2_shd.tensor, hf * QW * W2S + i * 128,
                           [[W2S, 128], [128 * W2S, QW // 128], [1, 128]])
            qtiles.append((w2_sh[i * 128:(i + 1) * 128, hf * QW:(hf + 1) * QW],
                           dstT, QW, 1))
    NQT = len(qtiles)
    qe_base = [0]
    for qi in range(NQT):
        qe_base.append(qe_base[-1] + qtiles[qi][2] // 1024)

    kt_cnt = [4 * (p // 2 + 1) for p in range(NP)]
    kt_base = [0]
    for p in range(NP):
        kt_base.append(kt_base[-1] + kt_cnt[p])
    NKT = kt_base[-1]
    qk_cnt = [2 + 4 * (p // 2 + 1) for p in range(NP)]
    qk_base = [0]
    for p in range(NP):
        qk_base.append(qk_base[-1] + qk_cnt[p])

    st = ExitStack()
    with st:
        semd = {}
        for n in ["ld", "pq", "pt", "cc", "qd",
                  "x", "a", "b", "c", "d", "e", "f", "tp", "te",
                  "mm", "ev", "hi", "lo", "dq",
                  "qk", "Sf", "Pf", "Pv", "Vf", "Of", "dn", "rr",
                  "rb", "rc", "no", "oh", "pj", "pe", "pd",
                  "xo", "la", "lb", "lc", "ldd", "le", "lf", "t3", "e3",
                  "w1", "w2", "u1", "f1", "g1", "h", "f2", "fe", "f2d",
                  "r3", "r3b", "o3", "stq", "qA", "qB", "kA", "kB", "w1b", "w2b", "qt", "qe"]:
            semd[n] = st.enter_context(nc.semaphore("s_" + n))
        S = semd

        ident = st.enter_context(nc.sbuf_tensor("ident", [128, 128], BF16))
        ones_c = st.enter_context(nc.sbuf_tensor("ones_c", [128, 1], BF16))
        ones_sq = st.enter_context(nc.sbuf_tensor("ones_sq", [128, 128], BF16))
        ones_k1 = st.enter_context(nc.sbuf_tensor("ones_k1", [1, 128], BF16))
        bqs = st.enter_context(nc.sbuf_tensor("bqs", [128, 2], F32))
        bk_sb = st.enter_context(nc.sbuf_tensor("bk_sb", [128, 2], F32))
        bv_sb = st.enter_context(nc.sbuf_tensor("bv_sb", [128, FOWN], F32))
        thrp = st.enter_context(nc.sbuf_tensor("thrp", [128, 2], F32))
        thrn = st.enter_context(nc.sbuf_tensor("thrn", [128, 2], F32))

        with ExitStack() as _esw:
            wq_h = _esw.enter_context(nc.sbuf_tensor("wq_h", [128, 16 * FOWN], BF16))
            wq_l = _esw.enter_context(nc.sbuf_tensor("wq_l", [128, 16 * FOWN], BF16))
            wk_h = _esw.enter_context(nc.sbuf_tensor("wk_h", [128, 16 * FOWN], BF16))
            wk_l = _esw.enter_context(nc.sbuf_tensor("wk_l", [128, 16 * FOWN], BF16))
            wv_h = _esw.enter_context(nc.sbuf_tensor("wv_h", [128, 16 * FOWN], BF16))
            wv_l = _esw.enter_context(nc.sbuf_tensor("wv_l", [128, 16 * FOWN], BF16))
            with ExitStack() as _es:
                wbig = _es.enter_context(nc.sbuf_tensor("wbig", [128, 4096], F32))
                qbuf = _es.enter_context(nc.sbuf_tensor("qbuf", [128, QBW], F32))
                qtmp = _es.enter_context(nc.sbuf_tensor("qtmp", [128, QBW], BF16))
                qout = _es.enter_context(nc.sbuf_tensor("qout", [128, QBW], BF16))
                qstage = _es.enter_context(nc.sbuf_tensor("qstage", [128, QBW], BF16))
                tpq0 = _es.enter_context(nc.psum_tensor("tpq0", [128, 1024], BF16))
                tpq1 = _es.enter_context(nc.psum_tensor("tpq1", [128, 1024], BF16))
                acc2 = _es.enter_context(nc.sbuf_tensor("acc2", [128, 2], F32))
                rtmp = _es.enter_context(nc.sbuf_tensor("rtmp", [128, 1], F32))
                ac_h = _es.enter_context(nc.sbuf_tensor("ac_h", [128, 2], BF16))
                ac_l = _es.enter_context(nc.sbuf_tensor("ac_l", [128, 2], BF16))
                pp0 = _es.enter_context(nc.psum_tensor("pp0", [128, 512], F32))
                b0 = _es.enter_context(nc.Block())
                @b0.sync
                def _(s):
                    s = _AD(s)
                    s.dma_start(out=ident[:, :], in_=id_in[:, :]).then_inc(S["ld"], 16)
                    s.dma_start(out=bqs[:, :], in_=bq_in[:, :]).then_inc(S["ld"], 16)
                    s.dma_start(out=bk_sb[:, :], in_=bk_in[:, :]).then_inc(S["ld"], 16)
                    s.dma_start(out=bv_sb[:, :], in_=bv_in[:, :]).then_inc(S["ld"], 16)
                    for wi, win in enumerate([wq_in, wk_in, wv_in]):
                        s.wait_ge(S["ld"], 16 * (4 + wi))
                        s.wait_ge(S["pq"], 2 * wi)
                        s.dma_start(
                            out=wbig[:, 0:16 * FOWN],
                            in_=bass.AP(win.tensor, 0,
                                        [[FOWN, 128], [128 * FOWN, 16], [1, FOWN]]),
                        ).then_inc(S["ld"], 16)

                @b0.vector
                def _(v):
                    v = _AD(v)
                    v.memset(ones_c[:, :], 1.0)
                    v.memset(ones_sq[:, :], 1.0)
                    v.memset(ones_k1[:, :], 1.0)
                    v.memset(acc2[:, :], 0.0)
                    v.wait_ge(S["ld"], 16 * 4)
                    v.tensor_scalar_mul(bqs[:, :], bqs[:, :], float(QS))
                    for wi, (dh, dl) in enumerate([(wq_h, wq_l), (wk_h, wk_l),
                                                   (wv_h, wv_l)]):
                        v.wait_ge(S["ld"], 16 * (5 + wi))
                        v.tensor_copy(dh[:, :], wbig[:, 0:16 * FOWN])
                        v.tensor_sub(dl[:, :], wbig[:, 0:16 * FOWN],
                                     dh[:, :]).then_inc(S["pq"], 2)
                    for qi in range(NQT):
                        w, col = qtiles[qi][2], qtiles[qi][3]
                        v.wait_ge(S["qd"], 16 * (qi + 1))
                        v.reduce_sum(rtmp[:, :], qbuf[:, 0:w], axis=AX.X,
                                     apply_absolute_value=True)
                        v.tensor_add(acc2[:, col:col + 1], acc2[:, col:col + 1],
                                     rtmp[:, :]).then_inc(S["pq"], 1)
                    v.tensor_copy(ac_h[:, :], acc2[:, :])
                    v.tensor_sub(ac_l[:, :], acc2[:, :],
                                 ac_h[:, :]).then_inc(S["pq"], 1)
                    v.wait_ge(S["qd"], 16 * (NQT + 2))
                    v.tensor_scalar_mul(thrp[:, :], thrp[:, :],
                                        float(0.7 / (DFF * D)))
                    v.tensor_scalar_mul(thrn[:, :], thrp[:, :],
                                        -1.0).then_inc(S["pq"], 1)
                    for qi in range(NQT):
                        w, col = qtiles[qi][2], qtiles[qi][3]
                        v.wait_ge(S["qd"], 16 * (NQT + 3 + 2 * qi))
                        v.tensor_scalar(qtmp[:, 0:w], qbuf[:, 0:w],
                                        thrp[:, col:col + 1], None, OP.is_ge)
                        v.tensor_scalar(qout[:, 0:w], qbuf[:, 0:w],
                                        thrn[:, col:col + 1], None, OP.is_le)
                        v.tensor_sub(qout[:, 0:w], qtmp[:, 0:w],
                                     qout[:, 0:w]).then_inc(S["pq"], 1)

                @b0.tensor
                def _(t):
                    t = _AD(t)
                    t.wait_ge(S["ld"], 64)
                    t.wait_ge(S["pq"], 6 + NQT + 1)
                    t.matmul(pp0[:, 0:2], ones_sq[:, :], ac_h[:, :],
                             start=True, stop=False)
                    t.matmul(pp0[:, 0:2], ones_sq[:, :], ac_l[:, :],
                             start=False, stop=True).then_inc(S["pt"], 1)
                    fl = 0
                    for qi in range(NQT):
                        w = qtiles[qi][2]
                        t.wait_ge(S["pq"], 6 + NQT + 2 + qi + 1)
                        for f in range(w // 1024):
                            if fl >= 2:
                                t.wait_ge(S["qe"], fl - 1)
                            bank = [tpq0, tpq1][fl % 2]
                            for ii in range(8):
                                k = f * 8 + ii
                                mmq = t.transpose(
                                    bank[:, ii * 128:(ii + 1) * 128],
                                    qout[:, k * 128:(k + 1) * 128], ident[:, :])
                            mmq.then_inc(S["qt"], 1)
                            fl += 1

                @b0.scalar
                def _(sc):
                    sc = _AD(sc)
                    sc.wait_ge(S["pt"], 1)
                    sc.copy(acc2[:, :], pp0[:, 0:2]).then_inc(S["pt"], 1)
                    fl = 0
                    for qi in range(NQT):
                        w = qtiles[qi][2]
                        sc.wait_ge(S["qd"], 16 * (NQT + 3 + 2 * qi))
                        for f in range(w // 1024):
                            sc.wait_ge(S["qt"], fl + 1)
                            sc.copy(qstage[:, f * 1024:(f + 1) * 1024],
                                    [tpq0, tpq1][fl % 2][:, :]).then_inc(S["qe"], 1)
                            fl += 1

                @b0.gpsimd
                def _(g):
                    g = _AD(g)
                    for qi in range(NQT):
                        w = qtiles[qi][2]
                        g.wait_ge(S["qd"], 16 * qi)
                        if qi > 0:
                            g.wait_ge(S["pq"], 6 + qi)
                        g.dma_start(out=qbuf[:, 0:w],
                                    in_=qtiles[qi][0]).then_inc(S["qd"], 16)
                    g.wait_ge(S["qd"], 16 * NQT)
                    g.wait_ge(S["pt"], 2)
                    g.dma_start(out=sc_part[:, :], in_=acc2[:, :]).then_inc(S["qd"], 16)
                    g.wait_ge(S["qd"], 16 * (NQT + 1))
                    g.collective_compute(
                        "AllReduce", OP.add,
                        replica_groups=[list(range(NCORE))],
                        ins=[sc_part.tensor.ap().opt()],
                        outs=[sc_sum.tensor.ap().opt()]).then_inc(S["cc"], 1)
                    g.wait_ge(S["cc"], 1)
                    g.dma_start(out=thrp[:, :], in_=sc_sum[:, :]).then_inc(S["qd"], 16)
                    for qi in range(NQT):
                        w = qtiles[qi][2]
                        g.wait_ge(S["qd"], 16 * (NQT + 2 + 2 * qi))
                        g.wait_ge(S["pq"], 6 + NQT + 2 + qi)
                        g.dma_start(out=qbuf[:, 0:w],
                                    in_=qtiles[qi][0]).then_inc(S["qd"], 16)
                        g.wait_ge(S["qd"], 16 * (NQT + 3 + 2 * qi))
                        g.wait_ge(S["qe"], qe_base[qi + 1])
                        g.dma_start(out=qtiles[qi][1],
                                    in_=qstage[:, 0:w]).then_inc(S["qd"], 16)
                    g.wait_ge(S["qd"], 16 * (NQT + 2 + 2 * NQT))
                    g.collective_compute(
                        "AllGather", OP.bypass,
                        replica_groups=[list(range(NCORE))],
                        ins=[wq1_shd.tensor.ap().opt()],
                        outs=[wq1_all.tensor.ap().opt()]).then_inc(S["cc"], 1)
                    g.wait_ge(S["cc"], 2)
                    g.collective_compute(
                        "AllGather", OP.bypass,
                        replica_groups=[list(range(NCORE))],
                        ins=[wq2_shd.tensor.ap().opt()],
                        outs=[wq2_all.tensor.ap().opt()]).then_inc(S["cc"], 1)

            if 1 in phases:
                with ExitStack() as _es:
                    xg = _es.enter_context(nc.sbuf_tensor("xg", [128, 3 * D], F32))
                    sqb = _es.enter_context(nc.sbuf_tensor("sqb", [128, D], F32))
                    xn32 = _es.enter_context(nc.sbuf_tensor("xn32", [128, 2 * D], F32))
                    xnh = _es.enter_context(nc.sbuf_tensor("xnh", [128, 2 * D], BF16))
                    xnl = _es.enter_context(nc.sbuf_tensor("xnl", [128, 2 * D], BF16))
                    s1 = _es.enter_context(nc.sbuf_tensor("s1", [128, 2], F32))
                    s2 = _es.enter_context(nc.sbuf_tensor("s2", [128, 2], F32))
                    nm = _es.enter_context(nc.sbuf_tensor("nm", [128, 2], F32))
                    vp = _es.enter_context(nc.sbuf_tensor("vp", [128, 2], F32))
                    sr = _es.enter_context(nc.sbuf_tensor("sr", [128, 2], F32))
                    rt1 = _es.enter_context(nc.sbuf_tensor("rt1", [128, 2], F32))
                    rstd = _es.enter_context(nc.sbuf_tensor("rstd", [128, 2], F32))
                    bcol = _es.enter_context(nc.sbuf_tensor("bcol", [128, 2], F32))
                    stg = _es.enter_context(nc.sbuf_tensor("stg", [128, 2 * 16 * 512], BF16))
                    evt = _es.enter_context(nc.sbuf_tensor("evt", [128, 2 * FOWN], F32))
                    ohb = _es.enter_context(nc.sbuf_tensor("ohb", [128, 2 * FOWN], BF16))
                    olb = _es.enter_context(nc.sbuf_tensor("olb", [128, 2 * FOWN], BF16))
                    tp0 = _es.enter_context(nc.psum_tensor("tp0", [128, 1024], BF16))
                    tp1 = _es.enter_context(nc.psum_tensor("tp1", [128, 1024], BF16))
                    mp0 = _es.enter_context(nc.psum_tensor("mp0", [128, 512], F32))
                    mp1 = _es.enter_context(nc.psum_tensor("mp1", [128, 512], F32))
                    b1 = _es.enter_context(nc.Block())
                    tpb = [tp0, tp1]
                    mpb = [mp0, mp1]

                    @b1.sync
                    def _(s):
                        s = _AD(s)
                        for t in range(NT):
                            if t >= 1:
                                s.wait_ge(S["x"], 16 * t)
                            if t >= 3:
                                s.wait_ge(S["a"], t - 2)
                                s.wait_ge(S["d"], t - 2)
                            s.dma_start(
                                out=xg[:, (t % 3) * D:(t % 3 + 1) * D],
                                in_=x_full[t * 128:(t + 1) * 128, :],
                            ).then_inc(S["x"], 16)

                    def fill_dst(m):
                        gq, slot = divmod(m, 6)
                        if slot < 4:
                            ft = slot
                            dsth = (qT_hi if ft < 2 else kT_hi)
                            dstl = (qT_lo if ft < 2 else kT_lo)
                            r0 = (ft % 2) * 128
                            return (dsth[r0:r0 + 128, gq * 256:(gq + 1) * 256],
                                    dstl[r0:r0 + 128, gq * 256:(gq + 1) * 256],
                                    FOWN)
                        tt = slot - 4
                        r0 = (2 * gq + tt) * 128
                        return (v_sh[r0:r0 + 128, :], v_sl[r0:r0 + 128, :], FOWN)

                    @b1.scalar
                    def _(sc):
                        sc = _AD(sc)
                        for i in range(NT + 4):
                            t = i
                            if t < NT:
                                sc.wait_ge(S["x"], 16 * (t + 1))
                                if t >= 2:
                                    sc.wait_ge(S["b"], t - 1)
                                sc.activation(sqb[:, :], xg[:, (t % 3) * D:(t % 3 + 1) * D],
                                              AF.Identity, bias=f32(0.0),
                                              accum_out=s1[:, t % 2:t % 2 + 1])
                                sc.activation(sqb[:, :], xg[:, (t % 3) * D:(t % 3 + 1) * D],
                                              AF.Square, bias=f32(0.0),
                                              accum_out=s2[:, t % 2:t % 2 + 1]
                                              ).then_inc(S["a"], 1)
                            t = i - 1
                            if 0 <= t < NT:
                                sc.wait_ge(S["b"], t + 1)
                                if t >= 2:
                                    sc.wait_ge(S["d"], t - 1)
                                sc.activation(sr[:, t % 2:t % 2 + 1],
                                              vp[:, t % 2:t % 2 + 1],
                                              AF.Sqrt, bias=f32(0.0)
                                              ).then_inc(S["c"], 1)
                            t = i - 2
                            if 0 <= t < NT:
                                sc.wait_ge(S["d"], t + 1)
                                if t >= 2:
                                    sc.wait_ge(S["tp"], 4 * (t - 1))
                                sc.copy(xnh[:, (t % 2) * D:(t % 2 + 1) * D],
                                        xn32[:, (t % 2) * D:(t % 2 + 1) * D]
                                        ).then_inc(S["e"], 1)
                            t = i - 3
                            if 0 <= t < NT:
                                gq = t // 2
                                for b4 in range(4):
                                    fl = 4 * t + b4
                                    sc.wait_ge(S["tp"], fl + 1)
                                    if gq >= 2:
                                        sc.wait_ge(S["mm"], 6 * (gq - 1))
                                    dst = bass.AP(
                                        stg,
                                        (gq % 2) * (16 * 512) + b4 * 4 * 512 + (t % 2) * 128,
                                        [[2 * 16 * 512, 128], [512, 4], [256, 2], [1, 128]],
                                    )
                                    sc.copy(dst, tpb[fl % 2][:, :]).then_inc(S["te"], 1)
                            j = i - 4
                            if 0 <= j < NT:
                                gq, half = divmod(j, 2)
                                for m in range(6 * gq + 3 * half, 6 * gq + 3 * half + 3):
                                    sc.wait_ge(S["ev"], m + 1)
                                    if m >= 2:
                                        sc.wait_ge(S["dq"], 32 * (m - 1))
                                    sc.copy(ohb[:, (m % 2) * FOWN:(m % 2 + 1) * FOWN],
                                            evt[:, (m % 2) * FOWN:(m % 2 + 1) * FOWN]
                                            ).then_inc(S["hi"], 1)

                    @b1.vector
                    def _(v):
                        v = _AD(v)
                        for i in range(NT + 4):
                            t = i
                            if t < NT:
                                v.wait_ge(S["a"], t + 1)
                                if t >= 2:
                                    v.wait_ge(S["c"], t - 1)
                                ts = slice(t % 2, t % 2 + 1)
                                v.tensor_scalar_mul(nm[:, ts], s1[:, ts], -1.0 / D)
                                v.tensor_scalar_mul(vp[:, ts], s2[:, ts], 1.0 / D)
                                v.tensor_mul(rt1[:, ts], nm[:, ts], nm[:, ts])
                                v.tensor_sub(vp[:, ts], vp[:, ts], rt1[:, ts])
                                v.tensor_scalar_add(vp[:, ts], vp[:, ts],
                                                    float(EPS)).then_inc(S["b"], 1)
                            t = i - 1
                            if 0 <= t < NT:
                                v.wait_ge(S["c"], t + 1)
                                if t >= 2:
                                    v.wait_ge(S["e"], t - 1)
                                ts = slice(t % 2, t % 2 + 1)
                                v.reciprocal(rstd[:, ts], sr[:, ts])
                                v.tensor_mul(rt1[:, ts], rstd[:, ts], rstd[:, ts])
                                v.tensor_mul(rt1[:, ts], rt1[:, ts], vp[:, ts])
                                v.tensor_scalar(rt1[:, ts], rt1[:, ts],
                                                -0.5, 1.5, OP.mult, OP.add)
                                v.tensor_mul(rstd[:, ts], rstd[:, ts], rt1[:, ts])
                                v.tensor_mul(bcol[:, ts], nm[:, ts], rstd[:, ts])
                                v.scalar_tensor_tensor(
                                    xn32[:, (t % 2) * D:(t % 2 + 1) * D],
                                    xg[:, (t % 3) * D:(t % 3 + 1) * D],
                                    rstd[:, ts],
                                    bcol[:, ts].broadcast_to([128, D]),
                                    OP.mult, OP.add).then_inc(S["d"], 1)
                            t = i - 2
                            if 0 <= t < NT:
                                v.wait_ge(S["e"], t + 1)
                                if t >= 2:
                                    v.wait_ge(S["tp"], 4 * (t - 1))
                                v.tensor_sub(xnl[:, (t % 2) * D:(t % 2 + 1) * D],
                                             xn32[:, (t % 2) * D:(t % 2 + 1) * D],
                                             xnh[:, (t % 2) * D:(t % 2 + 1) * D]
                                             ).then_inc(S["f"], 1)
                            j = i - 4
                            if 0 <= j < NT:
                                gq, half = divmod(j, 2)
                                for m in range(6 * gq + 3 * half, 6 * gq + 3 * half + 3):
                                    slot = m % 6
                                    v.wait_ge(S["mm"], m + 1)
                                    if m >= 2:
                                        v.wait_ge(S["hi"], m - 1)
                                    if slot < 2:
                                        v.scalar_tensor_tensor(
                                            evt[:, (m % 2) * FOWN:(m % 2 + 1) * FOWN],
                                            mpb[m % 2][:, 0:FOWN], float(QS),
                                            bqs[:, slot:slot + 1].broadcast_to([128, FOWN]),
                                            OP.mult, OP.add).then_inc(S["ev"], 1)
                                    elif slot < 4:
                                        v.scalar_tensor_tensor(
                                            evt[:, (m % 2) * FOWN:(m % 2 + 1) * FOWN],
                                            mpb[m % 2][:, 0:FOWN], 1.0,
                                            bk_sb[:, slot - 2:slot - 1].broadcast_to([128, FOWN]),
                                            OP.mult, OP.add).then_inc(S["ev"], 1)
                                    else:
                                        v.tensor_add(
                                            evt[:, (m % 2) * FOWN:(m % 2 + 1) * FOWN],
                                            mpb[m % 2][:, 0:FOWN],
                                            bv_sb[:, :]).then_inc(S["ev"], 1)
                                    v.wait_ge(S["hi"], m + 1)
                                    if m >= 2:
                                        v.wait_ge(S["dq"], 32 * (m - 1))
                                    v.tensor_sub(
                                        olb[:, (m % 2) * FOWN:(m % 2 + 1) * FOWN],
                                        evt[:, (m % 2) * FOWN:(m % 2 + 1) * FOWN],
                                        ohb[:, (m % 2) * FOWN:(m % 2 + 1) * FOWN]
                                        ).then_inc(S["lo"], 1)

                    @b1.tensor
                    def _(t_):
                        t_ = _AD(t_)
                        for i in range(NT):
                            t = i
                            t_.wait_ge(S["f"], t + 1)
                            for b4 in range(4):
                                fl = 4 * t + b4
                                if fl >= 2:
                                    t_.wait_ge(S["te"], fl - 1)
                                for ii in range(8):
                                    idx = b4 * 8 + ii
                                    k, hl = divmod(idx, 2)
                                    src = (xnh if hl == 0 else xnl)
                                    mm = t_.transpose(
                                        tpb[fl % 2][:, ii * 128:(ii + 1) * 128],
                                        src[:, (t % 2) * D + k * 128:(t % 2) * D + (k + 1) * 128],
                                        ident[:, :])
                                mm.then_inc(S["tp"], 1)
                            if t % 2 == 1:
                                gq = t // 2
                                t_.wait_ge(S["te"], 8 * gq + 8)
                                for slot in range(6):
                                    m = 6 * gq + slot
                                    if m >= 2:
                                        t_.wait_ge(S["ev"], m - 1)
                                    ps = mpb[m % 2]
                                    if slot < 4:
                                        wsh = wq_h if slot < 2 else wk_h
                                        wsl = wq_l if slot < 2 else wk_l
                                        jj = (slot % 2) * 128
                                        for k in range(16):
                                            sh = wsh[:, k * FOWN + jj:k * FOWN + jj + 128]
                                            sl = wsl[:, k * FOWN + jj:k * FOWN + jj + 128]
                                            xh = stg[:, (gq % 2) * 16 * 512 + k * 512:
                                                     (gq % 2) * 16 * 512 + k * 512 + 256]
                                            xl = stg[:, (gq % 2) * 16 * 512 + k * 512 + 256:
                                                     (gq % 2) * 16 * 512 + k * 512 + 512]
                                            t_.matmul(ps[:, 0:FOWN], sh, xh,
                                                      start=(k == 0), stop=False)
                                            t_.matmul(ps[:, 0:FOWN], sl, xh,
                                                      start=False, stop=False)
                                            mm = t_.matmul(ps[:, 0:FOWN], sh, xl,
                                                           start=False, stop=(k == 15))
                                        mm.then_inc(S["mm"], 1)
                                    else:
                                        tt = slot - 4
                                        for k in range(16):
                                            base = (gq % 2) * 16 * 512 + k * 512
                                            sh = stg[:, base + tt * 128:base + tt * 128 + 128]
                                            sl = stg[:, base + 256 + tt * 128:base + 256 + tt * 128 + 128]
                                            mh = wv_h[:, k * FOWN:(k + 1) * FOWN]
                                            ml = wv_l[:, k * FOWN:(k + 1) * FOWN]
                                            t_.matmul(ps[:, 0:FOWN], sh, mh,
                                                      start=(k == 0), stop=False)
                                            t_.matmul(ps[:, 0:FOWN], sl, mh,
                                                      start=False, stop=False)
                                            mm = t_.matmul(ps[:, 0:FOWN], sh, ml,
                                                           start=False, stop=(k == 15))
                                        mm.then_inc(S["mm"], 1)

                    @b1.gpsimd
                    def _(g):
                        g = _AD(g)
                        for m in range(6 * NG):
                            dsth, dstl, wdt = fill_dst(m)
                            if m >= 1:
                                g.wait_ge(S["dq"], 32 * m)
                            g.wait_ge(S["hi"], m + 1)
                            g.dma_start(out=dsth,
                                        in_=ohb[:, (m % 2) * FOWN:(m % 2 + 1) * FOWN]
                                        ).then_inc(S["dq"], 16)
                            g.wait_ge(S["lo"], m + 1)
                            g.dma_start(out=dstl,
                                        in_=olb[:, (m % 2) * FOWN:(m % 2 + 1) * FOWN]
                                        ).then_inc(S["dq"], 16)

        if 2 in phases:
            _phase2(nc, S, locals())
        if 3 in phases:
            _phase3(nc, S, locals())

        if dbg_outs:
            totals = {"qd": 16 * (3 * NQT + 2), "cc": 3}
            if 1 in phases:
                totals["dq"] = 16 * 12 * NG
            if 2 in phases:
                totals["pd"] = 16 * 16 * QR
                totals["cc"] = 4
            if 3 in phases:
                totals["f2d"] = 16 * (TOKT * 4) * NSLAB
                totals["stq"] = 16 * TOKT
            with nc.Block() as bd:
                @bd.gpsimd
                def _(g):
                    for nm_, tot in totals.items():
                        g.wait_ge(S[nm_], tot)
                    n0 = 0
                    for dst, src_ in dbg_outs:
                        g.dma_start(out=dst[...], in_=src_[...]).then_inc(S["qd"], 16)
                        n0 += 16
                    g.wait_ge(S["qd"], totals["qd"] + n0)

    return nc


def _phase2(nc, S, env):
    T, QR, NP = env["T"], env["QR"], env["NP"]
    NG = env["NG"]
    kt_base = env["kt_base"]; kt_cnt = env["kt_cnt"]
    qk_base = env["qk_base"]
    qT_hi, qT_lo, kT_hi, kT_lo = env["qT_hi"], env["qT_lo"], env["kT_hi"], env["kT_lo"]
    v_sh, v_sl = env["v_sh"], env["v_sl"]
    wp_in, mk_in = env["wp_in"], env["mk_in"]
    proj_part, rs_out = env["proj_part"], env["rs_out"]
    ones_c, ones_k1 = env["ones_c"], env["ones_k1"]
    phases = env["phases"]
    dq_tot = 16 * 12 * NG if 1 in phases else 0

    with ExitStack() as _es:
        masks = _es.enter_context(nc.sbuf_tensor("masks", [128, 4 * 512], F32))
        wpb = _es.enter_context(nc.sbuf_tensor("wpb", [128, 2 * D], F32))
        wp_h = _es.enter_context(nc.sbuf_tensor("wp_h", [128, 2 * D], BF16))
        wp_l = _es.enter_context(nc.sbuf_tensor("wp_l", [128, 2 * D], BF16))
        qc_h = _es.enter_context(nc.sbuf_tensor("qc_h", [128, 2 * 512], BF16))
        qc_l = _es.enter_context(nc.sbuf_tensor("qc_l", [128, 2 * 512], BF16))
        kc_h = _es.enter_context(nc.sbuf_tensor("kc_h", [128, 2 * 512], BF16))
        kc_l = _es.enter_context(nc.sbuf_tensor("kc_l", [128, 2 * 512], BF16))
        vc_h = _es.enter_context(nc.sbuf_tensor("vc_h", [128, 2 * 512], BF16))
        vc_l = _es.enter_context(nc.sbuf_tensor("vc_l", [128, 2 * 512], BF16))
        P32 = _es.enter_context(nc.sbuf_tensor("P32", [128, 2 * 512], F32))
        Ph = _es.enter_context(nc.sbuf_tensor("Ph", [128, 2 * 512], BF16))
        Pl = _es.enter_context(nc.sbuf_tensor("Pl", [128, 2 * 512], BF16))
        rb_sb = _es.enter_context(nc.sbuf_tensor("rb_sb", [128, 2 * 512], F32))
        r32 = _es.enter_context(nc.sbuf_tensor("r32", [1, 2 * 512], F32))
        r_h = _es.enter_context(nc.sbuf_tensor("r_h", [1, 2 * 512], BF16))
        r_l = _es.enter_context(nc.sbuf_tensor("r_l", [1, 2 * 512], BF16))
        On32 = _es.enter_context(nc.sbuf_tensor("On32", [128, 2 * 512], F32))
        Oh = _es.enter_context(nc.sbuf_tensor("Oh", [128, 2 * 512], BF16))
        Ol = _es.enter_context(nc.sbuf_tensor("Ol", [128, 2 * 512], BF16))
        pj_sb = _es.enter_context(nc.sbuf_tensor("pj_sb", [128, 2 * 512], F32))
        sp0 = _es.enter_context(nc.psum_tensor("sp0", [128, 512], F32))
        sp1 = _es.enter_context(nc.psum_tensor("sp1", [128, 512], F32))
        Op0 = _es.enter_context(nc.psum_tensor("Op0", [128, 512], F32))
        Op1 = _es.enter_context(nc.psum_tensor("Op1", [128, 512], F32))
        den = _es.enter_context(nc.psum_tensor("den", [1, 512], F32))
        pj0 = _es.enter_context(nc.psum_tensor("pj0", [128, 512], F32))
        pj1 = _es.enter_context(nc.psum_tensor("pj1", [128, 512], F32))
        b2 = _es.enter_context(nc.Block())
        spb = [sp0, sp1]
        Opb = [Op0, Op1]
        pjb = [pj0, pj1]

        qslot_use = {}
        kslot_use = {}
        kslot_prev_end = {}
        _qcnt = [0, 0]
        _kcnt = [0, 0]
        _klast = [None, None]
        for p in range(NP):
            _qcnt[p % 2] += 1
            qslot_use[p] = _qcnt[p % 2]
            for ck in range(p // 2 + 1):
                _kcnt[ck % 2] += 1
                kslot_use[(p, ck)] = _kcnt[ck % 2]
                kslot_prev_end[(p, ck)] = _klast[ck % 2]
                _klast[ck % 2] = kt_base[p] + 4 * ck + 4
        QSL = [S["qA"], S["qB"]]
        KSL = [S["kA"], S["kB"]]

        @b2.sync
        def _(s):
            s = _AD(s)
            s.wait_ge(S["dq"], dq_tot)
            for j in range(4):
                s.dma_start(out=masks[:, j * 512:(j + 1) * 512],
                            in_=mk_in[j, :, :]).then_inc(S["ld"], 16)
            s.dma_start(out=wpb[:, :],
                        in_=bass.AP(wp_in.tensor, 0,
                                    [[D, 128], [128 * D, 2], [1, D]])
                        ).then_inc(S["ld"], 16)
            s.wait_ge(S["ld"], 16 * 12)
            for p in range(NP):
                qr, h = divmod(p, 2)
                r0 = h * 128
                qs = QSL[p % 2]
                if qslot_use[p] >= 2:
                    s.wait_ge(qs, 32 * (qslot_use[p] - 1))
                if p >= 2:
                    s.wait_ge(S["Sf"], kt_base[p - 1])
                s.dma_start(out=qc_h[:, (p % 2) * 512:(p % 2 + 1) * 512],
                            in_=qT_hi[r0:r0 + 128, qr * 512:(qr + 1) * 512]
                            ).then_inc(qs, 16)
                s.dma_start(out=qc_l[:, (p % 2) * 512:(p % 2 + 1) * 512],
                            in_=qT_lo[r0:r0 + 128, qr * 512:(qr + 1) * 512]
                            ).then_inc(qs, 16)
                for ck in range(qr + 1):
                    ks = KSL[ck % 2]
                    if kslot_use[(p, ck)] >= 2:
                        s.wait_ge(ks, 64 * (kslot_use[(p, ck)] - 1))
                    if kslot_prev_end[(p, ck)] is not None:
                        s.wait_ge(S["Vf"], kslot_prev_end[(p, ck)])
                    c2 = ck % 2
                    s.dma_start(out=kc_h[:, c2 * 512:(c2 + 1) * 512],
                                in_=kT_hi[r0:r0 + 128, ck * 512:(ck + 1) * 512]
                                ).then_inc(ks, 16)
                    s.dma_start(out=kc_l[:, c2 * 512:(c2 + 1) * 512],
                                in_=kT_lo[r0:r0 + 128, ck * 512:(ck + 1) * 512]
                                ).then_inc(ks, 16)
                    vap = [[FOWN, 128], [128 * FOWN, 4], [1, 128]]
                    voff = ck * 512 * FOWN + r0
                    s.dma_start(out=vc_h[:, c2 * 512:(c2 + 1) * 512],
                                in_=bass.AP(v_sh.tensor, voff, list(vap))
                                ).then_inc(ks, 16)
                    s.dma_start(out=vc_l[:, c2 * 512:(c2 + 1) * 512],
                                in_=bass.AP(v_sl.tensor, voff, list(vap))
                                ).then_inc(ks, 16)

        LD2 = 16 * 12
        @b2.vector
        def _(v):
            v = _AD(v)
            v.wait_ge(S["ld"], LD2)
            v.tensor_copy(wp_h[:, :], wpb[:, :])
            v.tensor_sub(wp_l[:, :], wpb[:, :], wp_h[:, :]).then_inc(S["rr"], 1)
            for p in range(NP):
                qr, h = divmod(p, 2)
                KT = kt_cnt[p]
                for j in range(KT):
                    g = kt_base[p] + j
                    g2 = g % 2
                    v.wait_ge(S["Pf"], g + 1)
                    if g >= 2:
                        v.wait_ge(S["Vf"], g - 1)
                    if j >= 4 * qr:
                        v.wait_ge(S["ld"], LD2)
                        jj = j - 4 * qr
                        v.tensor_mul(P32[:, g2 * 512:(g2 + 1) * 512],
                                     P32[:, g2 * 512:(g2 + 1) * 512],
                                     masks[:, jj * 512:(jj + 1) * 512])
                    v.tensor_copy(Ph[:, g2 * 512:(g2 + 1) * 512],
                                  P32[:, g2 * 512:(g2 + 1) * 512])
                    v.tensor_sub(Pl[:, g2 * 512:(g2 + 1) * 512],
                                 P32[:, g2 * 512:(g2 + 1) * 512],
                                 Ph[:, g2 * 512:(g2 + 1) * 512]).then_inc(S["Pv"], 1)
                h5 = h * 512
                if p >= 2:
                    v.wait_ge(S["rb"], p - 1)
                v.wait_ge(S["Of"], p + 1)
                v.reciprocal(r32[:, h5:h5 + 512], den[:, :]).then_inc(S["dn"], 1)
                v.tensor_copy(r_h[:, h5:h5 + 512], r32[:, h5:h5 + 512])
                v.tensor_sub(r_l[:, h5:h5 + 512], r32[:, h5:h5 + 512],
                             r_h[:, h5:h5 + 512]).then_inc(S["rr"], 1)
                v.wait_ge(S["rc"], p + 1)
                if qr >= 1:
                    v.wait_ge(S["pj"], 16 * qr)
                v.tensor_mul(On32[:, h5:h5 + 512], Opb[h][:, :],
                             rb_sb[:, h5:h5 + 512]).then_inc(S["no"], 1)
                v.wait_ge(S["oh"], p + 1)
                v.tensor_sub(Ol[:, h5:h5 + 512], On32[:, h5:h5 + 512],
                             Oh[:, h5:h5 + 512]).then_inc(S["no"], 1)
                if h == 1:
                    for f in range(16):
                        pf = 16 * qr + f
                        v.wait_ge(S["pj"], pf + 1)
                        if pf >= 2:
                            v.wait_ge(S["pd"], 16 * (pf - 1))
                        v.tensor_copy(pj_sb[:, (pf % 2) * 512:(pf % 2 + 1) * 512],
                                      pjb[pf % 2][:, :]).then_inc(S["pe"], 1)

        @b2.scalar
        def _(sc):
            sc = _AD(sc)
            for p in range(NP):
                qr, h = divmod(p, 2)
                KT = kt_cnt[p]
                for j in range(KT):
                    g = kt_base[p] + j
                    g2 = g % 2
                    sc.wait_ge(S["Sf"], g + 1)
                    if g >= 2:
                        sc.wait_ge(S["Pv"], g - 1)
                    sc.activation(P32[:, g2 * 512:(g2 + 1) * 512],
                                  spb[g2][:, :], AF.Exp,
                                  bias=ESHIFT).then_inc(S["Pf"], 1)
                h5 = h * 512
                if p >= 2:
                    sc.wait_ge(S["no"], 2 * (p - 2) + 1)
                sc.wait_ge(S["rb"], p + 1)
                sc.copy(rb_sb[:, h5:h5 + 512], pjb[h][:, :]).then_inc(S["rc"], 1)
                sc.wait_ge(S["no"], 2 * p + 1)
                sc.copy(Oh[:, h5:h5 + 512],
                        On32[:, h5:h5 + 512]).then_inc(S["oh"], 1)

        @b2.tensor
        def _(t_):
            t_ = _AD(t_)
            for p in range(NP):
                qr, h = divmod(p, 2)
                KT = kt_cnt[p]

                def s_tile(j):
                    g = kt_base[p] + j
                    g2 = g % 2
                    ck, kk = divmod(j, 4)
                    c2 = ck % 2
                    t_.wait_ge(QSL[p % 2], 32 * qslot_use[p])
                    t_.wait_ge(KSL[ck % 2], 64 * kslot_use[(p, ck)])
                    if g >= 2:
                        t_.wait_ge(S["Pf"], g - 1)
                    kh = kc_h[:, c2 * 512 + kk * 128:c2 * 512 + (kk + 1) * 128]
                    kl = kc_l[:, c2 * 512 + kk * 128:c2 * 512 + (kk + 1) * 128]
                    qh = qc_h[:, (p % 2) * 512:(p % 2 + 1) * 512]
                    ql = qc_l[:, (p % 2) * 512:(p % 2 + 1) * 512]
                    t_.matmul(spb[g2][:, :], kh, qh, start=True, stop=False)
                    t_.matmul(spb[g2][:, :], kl, qh, start=False, stop=False)
                    t_.matmul(spb[g2][:, :], kh, ql, start=False,
                              stop=True).then_inc(S["Sf"], 1)

                def pv_tile(j):
                    g = kt_base[p] + j
                    g2 = g % 2
                    ck, kk = divmod(j, 4)
                    c2 = ck % 2
                    t_.wait_ge(S["Pv"], g + 1)
                    first = (j == 0)
                    last = (j == KT - 1)
                    if first:
                        t_.wait_ge(S["no"], max(0, 2 * p - 3))
                        if p >= 1:
                            t_.wait_ge(S["dn"], p)
                    t_.wait_ge(KSL[ck % 2], 64 * kslot_use[(p, ck)])
                    vh = vc_h[:, c2 * 512 + kk * 128:c2 * 512 + (kk + 1) * 128]
                    vl = vc_l[:, c2 * 512 + kk * 128:c2 * 512 + (kk + 1) * 128]
                    ph = Ph[:, g2 * 512:(g2 + 1) * 512]
                    pl = Pl[:, g2 * 512:(g2 + 1) * 512]
                    t_.matmul(Opb[h][:, :], vh, ph, start=first, stop=False)
                    t_.matmul(Opb[h][:, :], vh, pl, start=False, stop=False)
                    t_.matmul(Opb[h][:, :], vl, ph, start=False, stop=last
                              ).then_inc(S["Vf"], 1)
                    t_.matmul(den[:, :], ones_c[:, :], ph, start=first, stop=False)
                    mm = t_.matmul(den[:, :], ones_c[:, :], pl, start=False,
                                   stop=last)
                    if last:
                        mm.then_inc(S["Of"], 1)

                for j in range(KT):
                    s_tile(j)
                    if j > 0:
                        pv_tile(j - 1)
                pv_tile(KT - 1)
                t_.wait_ge(S["rr"], p + 2)
                if qr >= 1:
                    t_.wait_ge(S["pe"], 16 * qr)
                h5 = h * 512
                t_.matmul(pjb[h][:, :], ones_k1[:, :], r_h[:, h5:h5 + 512],
                          start=True, stop=False)
                t_.matmul(pjb[h][:, :], ones_k1[:, :], r_l[:, h5:h5 + 512],
                          start=False, stop=True).then_inc(S["rb"], 1)
                if h == 1:
                    for f in range(16):
                        tt, dg = divmod(f, 4)
                        pf = 16 * qr + f
                        t_.wait_ge(S["no"], 2 * p + 2)
                        if f >= 2:
                            t_.wait_ge(S["pe"], pf - 1)
                        else:
                            t_.wait_ge(S["rc"], p + 1)
                        ps = pjb[pf % 2]
                        for ch in range(2):
                            oh = Oh[:, ch * 512 + tt * 128:ch * 512 + (tt + 1) * 128]
                            ol = Ol[:, ch * 512 + tt * 128:ch * 512 + (tt + 1) * 128]
                            mh = wp_h[:, ch * D + dg * 512:ch * D + (dg + 1) * 512]
                            ml = wp_l[:, ch * D + dg * 512:ch * D + (dg + 1) * 512]
                            t_.matmul(ps[:, :], oh, mh, start=(ch == 0), stop=False)
                            t_.matmul(ps[:, :], ol, mh, start=False, stop=False)
                            mmp = t_.matmul(ps[:, :], oh, ml, start=False,
                                            stop=(ch == 1))
                            if ch == 1:
                                mmp.then_inc(S["pj"], 1)

        @b2.gpsimd
        def _(g):
            g = _AD(g)
            for qr in range(QR):
                for f in range(16):
                    tt, dg = divmod(f, 4)
                    pf = 16 * qr + f
                    if pf >= 1:
                        g.wait_ge(S["pd"], 16 * pf)
                    g.wait_ge(S["pe"], pf + 1)
                    g.dma_start(
                        out=proj_part[qr * 512 + tt * 128:qr * 512 + (tt + 1) * 128,
                                      dg * 512:(dg + 1) * 512],
                        in_=pj_sb[:, (pf % 2) * 512:(pf % 2 + 1) * 512]
                    ).then_inc(S["pd"], 16)
            g.wait_ge(S["pd"], 16 * 16 * QR)
            g.wait_ge(S["cc"], 3)
            g.collective_compute(
                "ReduceScatter", OP.add, replica_groups=[list(range(NCORE))],
                ins=[proj_part.tensor.ap().opt()],
                outs=[rs_out.tensor.ap().opt()]).then_inc(S["cc"], 1)


def _phase3(nc, S, env):
    T, DFF = env["T"], env["DFF"]
    TOK, TOKT = env["TOK"], env["TOKT"]
    NFT, NSLAB = env["NFT"], env["NSLAB"]
    TN, NTH = env["TN"], env["NTH"]
    W1S, W2S = env["W1S"], env["W2S"]
    x_own, b1_in, bp_in, b2_in = env["x_own"], env["b1_in"], env["bp_in"], env["b2_in"]
    wq1_all, wq2_all = env["wq1_all"], env["wq2_all"]
    ffn_acc, rs_out, out_own = env["ffn_acc"], env["rs_out"], env["out_own"]
    ident = env["ident"]
    phases = env["phases"]
    FPS = 4
    NFILL = TOKT * 4

    with ExitStack() as _esx:
        xnT_h = _esx.enter_context(nc.sbuf_tensor("xnT_h", [128, 16 * TOK], BF16))
        xnT_l = _esx.enter_context(nc.sbuf_tensor("xnT_l", [128, 16 * TOK], BF16))
        b1_sb = _esx.enter_context(nc.sbuf_tensor("b1_sb", [128, NFT], F32))
        with ExitStack() as _es:
            xg3 = _es.enter_context(nc.sbuf_tensor("xg3", [128, 3 * D], F32))
            sqb3 = _es.enter_context(nc.sbuf_tensor("sqb3", [128, D], F32))
            xn33 = _es.enter_context(nc.sbuf_tensor("xn33", [128, 2 * D], F32))
            xnh3 = _es.enter_context(nc.sbuf_tensor("xnh3", [128, 2 * D], BF16))
            xnl3 = _es.enter_context(nc.sbuf_tensor("xnl3", [128, 2 * D], BF16))
            sm3 = _es.enter_context(nc.sbuf_tensor("sm3", [128, 16], F32))
            tq0 = _es.enter_context(nc.psum_tensor("tq0", [128, 1024], BF16))
            tq1 = _es.enter_context(nc.psum_tensor("tq1", [128, 1024], BF16))
            b3l = _es.enter_context(nc.Block())
            tqb = [tq0, tq1]
            @b3l.sync
            def _(s):
                s = _AD(s)
                s.dma_start(out=b1_sb[:, :], in_=b1_in[:, :]).then_inc(S["xo"], 16)
                for t in range(TOKT):
                    if t >= 1:
                        s.wait_ge(S["xo"], 16 * (t + 1))
                    if t >= 3:
                        s.wait_ge(S["la"], t - 2)
                        s.wait_ge(S["ldd"], t - 2)
                    s.dma_start(out=xg3[:, (t % 3) * D:(t % 3 + 1) * D],
                                in_=x_own[t * 128:(t + 1) * 128, :]
                                ).then_inc(S["xo"], 16)

            @b3l.scalar
            def _(sc):
                sc = _AD(sc)
                for i in range(TOKT + 4):
                    t = i
                    if t < TOKT:
                        sc.wait_ge(S["xo"], 16 * (t + 2))
                        if t >= 2:
                            sc.wait_ge(S["lb"], t - 1)
                        sc.activation(sqb3[:, :], xg3[:, (t % 3) * D:(t % 3 + 1) * D],
                                      AF.Identity, bias=f32(0.0),
                                      accum_out=sm3[:, t % 2:t % 2 + 1])
                        sc.activation(sqb3[:, :], xg3[:, (t % 3) * D:(t % 3 + 1) * D],
                                      AF.Square, bias=f32(0.0),
                                      accum_out=sm3[:, 2 + t % 2:3 + t % 2]
                                      ).then_inc(S["la"], 1)
                    t = i - 1
                    if 0 <= t < TOKT:
                        sc.wait_ge(S["lb"], t + 1)
                        if t >= 2:
                            sc.wait_ge(S["ldd"], t - 1)
                        sc.activation(sm3[:, 8 + t % 2:9 + t % 2],
                                      sm3[:, 6 + t % 2:7 + t % 2],
                                      AF.Sqrt, bias=f32(0.0)).then_inc(S["lc"], 1)
                    t = i - 2
                    if 0 <= t < TOKT:
                        sc.wait_ge(S["ldd"], t + 1)
                        if t >= 2:
                            sc.wait_ge(S["t3"], 4 * (t - 1))
                        sc.copy(xnh3[:, (t % 2) * D:(t % 2 + 1) * D],
                                xn33[:, (t % 2) * D:(t % 2 + 1) * D]
                                ).then_inc(S["le"], 1)
                    t = i - 3
                    if 0 <= t < TOKT:
                        for b4 in range(4):
                            fl = 4 * t + b4
                            sc.wait_ge(S["t3"], fl + 1)
                            k0 = b4 * 4
                            dsth = bass.AP(xnT_h, k0 * TOK + t * 128,
                                           [[16 * TOK, 128], [TOK, 4], [1, 128]])
                            dstl = bass.AP(xnT_l, k0 * TOK + t * 128,
                                           [[16 * TOK, 128], [TOK, 4], [1, 128]])
                            src = tqb[fl % 2]
                            sph = bass.AP(src, 0, [[1024, 128], [256, 4], [1, 128]])
                            spl = bass.AP(src, 128, [[1024, 128], [256, 4], [1, 128]])
                            sc.copy(dsth, sph)
                            sc.copy(dstl, spl).then_inc(S["e3"], 1)

            @b3l.vector
            def _(v):
                v = _AD(v)
                for i in range(TOKT + 4):
                    t = i
                    if t < TOKT:
                        v.wait_ge(S["la"], t + 1)
                        if t >= 2:
                            v.wait_ge(S["lc"], t - 1)
                        a, b_, c, dcol = t % 2, 2 + t % 2, 4 + t % 2, 6 + t % 2
                        v.tensor_scalar_mul(sm3[:, c:c + 1], sm3[:, a:a + 1],
                                            -1.0 / D)
                        v.tensor_scalar_mul(sm3[:, dcol:dcol + 1],
                                            sm3[:, b_:b_ + 1], 1.0 / D)
                        v.tensor_mul(sm3[:, 12:13], sm3[:, c:c + 1], sm3[:, c:c + 1])
                        v.tensor_sub(sm3[:, dcol:dcol + 1], sm3[:, dcol:dcol + 1],
                                     sm3[:, 12:13])
                        v.tensor_scalar_add(sm3[:, dcol:dcol + 1],
                                            sm3[:, dcol:dcol + 1],
                                            float(EPS)).then_inc(S["lb"], 1)
                    t = i - 1
                    if 0 <= t < TOKT:
                        v.wait_ge(S["lc"], t + 1)
                        if t >= 2:
                            v.wait_ge(S["le"], t - 1)
                        r, c, dcol = 10 + t % 2, 4 + t % 2, 6 + t % 2
                        v.reciprocal(sm3[:, r:r + 1], sm3[:, 8 + t % 2:9 + t % 2])
                        v.tensor_mul(sm3[:, 12:13], sm3[:, r:r + 1], sm3[:, r:r + 1])
                        v.tensor_mul(sm3[:, 12:13], sm3[:, 12:13],
                                     sm3[:, dcol:dcol + 1])
                        v.tensor_scalar(sm3[:, 12:13], sm3[:, 12:13], -0.5,
                                        1.5, OP.mult, OP.add)
                        v.tensor_mul(sm3[:, r:r + 1], sm3[:, r:r + 1], sm3[:, 12:13])
                        v.tensor_mul(sm3[:, 13 + t % 2:14 + t % 2],
                                     sm3[:, c:c + 1], sm3[:, r:r + 1])
                        v.scalar_tensor_tensor(
                            xn33[:, (t % 2) * D:(t % 2 + 1) * D],
                            xg3[:, (t % 3) * D:(t % 3 + 1) * D],
                            sm3[:, r:r + 1],
                            sm3[:, 13 + t % 2:14 + t % 2].broadcast_to([128, D]),
                            OP.mult, OP.add).then_inc(S["ldd"], 1)
                    t = i - 2
                    if 0 <= t < TOKT:
                        v.wait_ge(S["le"], t + 1)
                        if t >= 2:
                            v.wait_ge(S["t3"], 4 * (t - 1))
                        v.tensor_sub(xnl3[:, (t % 2) * D:(t % 2 + 1) * D],
                                     xn33[:, (t % 2) * D:(t % 2 + 1) * D],
                                     xnh3[:, (t % 2) * D:(t % 2 + 1) * D]
                                     ).then_inc(S["lf"], 1)

            @b3l.tensor
            def _(t_):
                t_ = _AD(t_)
                for t in range(TOKT):
                    t_.wait_ge(S["lf"], t + 1)
                    for b4 in range(4):
                        fl = 4 * t + b4
                        if fl >= 2:
                            t_.wait_ge(S["e3"], fl - 1)
                        for ii in range(8):
                            idx = b4 * 8 + ii
                            k, hl = divmod(idx, 2)
                            src = (xnh3 if hl == 0 else xnl3)
                            mm = t_.transpose(
                                tqb[fl % 2][:, ii * 128:(ii + 1) * 128],
                                src[:, (t % 2) * D + k * 128:(t % 2) * D + (k + 1) * 128],
                                ident[:, :])
                        mm.then_inc(S["t3"], 1)

        E3TOT = 4 * TOKT
        with ExitStack() as _es:
            w1s = _es.enter_context(nc.sbuf_tensor("w1s", [128, 2 * 16 * 512], BF16))
            w2s = _es.enter_context(nc.sbuf_tensor("w2s", [128, 8 * D], BF16))
            hsl_h = _es.enter_context(nc.sbuf_tensor("hsl", [128, 2 * FPS * TOK], BF16))
            hsl_l = _es.enter_context(nc.sbuf_tensor("hsll", [128, 2 * FPS * TOK], BF16))
            h32 = _es.enter_context(nc.sbuf_tensor("h32", [128, 2 * 512], F32))
            f2sb = _es.enter_context(nc.sbuf_tensor("f2sb", [128, 2 * 512], F32))
            ff0 = _es.enter_context(nc.psum_tensor("ff0", [128, 512], F32))
            ff1 = _es.enter_context(nc.psum_tensor("ff1", [128, 512], F32))
            ff2 = _es.enter_context(nc.psum_tensor("ff2", [128, 512], F32))
            ff3 = _es.enter_context(nc.psum_tensor("ff3", [128, 512], F32))
            ff4 = _es.enter_context(nc.psum_tensor("ff4", [128, 512], F32))
            ff5 = _es.enter_context(nc.psum_tensor("ff5", [128, 512], F32))
            ff6 = _es.enter_context(nc.psum_tensor("ff6", [128, 512], F32))
            ff7 = _es.enter_context(nc.psum_tensor("ff7", [128, 512], F32))
            b3a = _es.enter_context(nc.Block())
            ffb = [ff0, ff1, ff2, ff3, ff4, ff5, ff6, ff7]

            W1SL = [S["w1"], S["w1b"]]
            W2SL = [S["w2"], S["w2b"]]

            @b3a.sync
            def _(s):
                s = _AD(s)
                s.wait_ge(S["cc"], 3)
                for sl in range(NSLAB):
                    if sl >= 2:
                        s.wait_ge(W1SL[sl % 2], 16 * 16 * (sl // 2))
                        s.wait_ge(S["f1"], 4 * NTH * (sl - 1))
                    for k in range(16):
                        if W1S >= 512:
                            blk, off = divmod(sl * 512, W1S)
                            ap1 = bass.AP(wq1_all.tensor,
                                          blk * D * W1S + (k * 128) * W1S + off,
                                          [[W1S, 128], [1, 512]])
                        else:
                            nb = 512 // W1S
                            blk = sl * 512 // W1S
                            ap1 = bass.AP(wq1_all.tensor,
                                          blk * D * W1S + (k * 128) * W1S,
                                          [[W1S, 128], [D * W1S, nb], [1, W1S]])
                        s.dma_start(
                            out=w1s[:, (sl % 2) * (16 * 512) + k * 512:
                                    (sl % 2) * (16 * 512) + (k + 1) * 512],
                            in_=ap1).then_inc(W1SL[sl % 2], 16)
                    if sl >= 2:
                        s.wait_ge(W2SL[sl % 2], 16 * 4 * (sl // 2))
                        s.wait_ge(S["f2"], NFILL * (sl - 1))
                    for jj in range(FPS):
                        fc = FPS * sl + jj
                        ap2 = bass.AP(wq2_all.tensor, (fc * 128) * W2S,
                                      [[W2S, 128], [DFF * W2S, 8], [1, W2S]])
                        s.dma_start(
                            out=w2s[:, ((sl % 2) * 4 + jj) * D:((sl % 2) * 4 + jj + 1) * D],
                            in_=ap2).then_inc(W2SL[sl % 2], 16)

            def _ffn2_slab(t_, sl):
                hb = (sl % 2) * FPS * TOK
                for fi in range(NFILL):
                    tt, dg = divmod(fi, 4)
                    ff = NFILL * sl + fi
                    t_.wait_ge(S["h"], 2 * FPS * NTH * (sl + 1))
                    t_.wait_ge(S["g1"], min(4 * NTH * (sl + 2), 4 * NTH * NSLAB))
                    if ff >= 2:
                        t_.wait_ge(S["fe"], ff - 1)
                    ps = ffb[ff % 2]
                    for jj in range(FPS):
                        fc = FPS * sl + jj
                        t_.wait_ge(W2SL[sl % 2], 16 * 4 * (sl // 2 + 1))
                        th, trem = divmod(tt * 128, TN)
                        sh = hsl_h[:, hb + (jj * NTH + th) * TN + trem:
                                   hb + (jj * NTH + th) * TN + trem + 128]
                        sll = hsl_l[:, hb + (jj * NTH + th) * TN + trem:
                                    hb + (jj * NTH + th) * TN + trem + 128]
                        mv = w2s[:, ((sl % 2) * 4 + jj) * D + dg * 512:
                                 ((sl % 2) * 4 + jj) * D + (dg + 1) * 512]
                        t_.matmul(ps[:, :], sh, mv, start=(jj == 0), stop=False)
                        mm2 = t_.matmul(ps[:, :], sll, mv, start=False,
                                        stop=(jj == FPS - 1))
                        if jj == FPS - 1:
                            mm2.then_inc(S["f2"], 1)

            @b3a.tensor
            def _(t_):
                t_ = _AD(t_)
                t_.wait_ge(S["e3"], E3TOT)
                for sl in range(NSLAB):
                    for th in range(NTH):
                        for k in range(16):
                            t_.wait_ge(W1SL[sl % 2], 16 * 16 * (sl // 2 + 1))
                            for jj in range(4):
                                bk = ((sl * NTH + th) % 2) * 4 + jj
                                prev = (sl * NTH + th - 2)
                                if prev >= 0:
                                    t_.wait_ge(S["g1"], 4 * prev + jj + 1)
                                if sl >= 2 and th == 0 and bk <= 1:
                                    t_.wait_ge(S["fe"], NFILL * (sl - 1))
                                st_h = w1s[:, (sl % 2) * (16 * 512) + k * 512 + jj * 128:
                                           (sl % 2) * (16 * 512) + k * 512 + (jj + 1) * 128]
                                xh = xnT_h[:, k * TOK + th * TN:k * TOK + (th + 1) * TN]
                                xl = xnT_l[:, k * TOK + th * TN:k * TOK + (th + 1) * TN]
                                t_.matmul(ffb[bk][:, 0:TN], st_h, xh,
                                          start=(k == 0), stop=False)
                                mm = t_.matmul(ffb[bk][:, 0:TN], st_h, xl,
                                               start=False, stop=(k == 15))
                                if k == 15:
                                    mm.then_inc(S["f1"], 1)
                    if sl >= 1:
                        _ffn2_slab(t_, sl - 1)
                _ffn2_slab(t_, NSLAB - 1)

            @b3a.scalar
            def _(sc):
                sc = _AD(sc)
                sc.wait_ge(S["xo"], 32)
                for sl in range(NSLAB):
                    for th in range(NTH):
                        for jj in range(4):
                            gi = 4 * (sl * NTH + th) + jj
                            bk = ((sl * NTH + th) % 2) * 4 + jj
                            sc.wait_ge(S["f1"], gi + 1)
                            if gi >= 2:
                                sc.wait_ge(S["h"], 2 * (gi - 1))
                            ft = 4 * sl + jj
                            sc.activation(h32[:, (gi % 2) * 512:(gi % 2) * 512 + TN],
                                          ffb[bk][:, 0:TN], AF.Gelu,
                                          bias=b1_sb[:, ft:ft + 1]
                                          ).then_inc(S["g1"], 1)

            @b3a.vector
            def _(v):
                v = _AD(v)
                for sl in range(NSLAB):
                    hb = (sl % 2) * FPS * TOK
                    for th in range(NTH):
                        for jj in range(4):
                            gi = 4 * (sl * NTH + th) + jj
                            if sl >= 2 and th == 0 and jj == 0:
                                v.wait_ge(S["f2"], NFILL * (sl - 1))
                            v.wait_ge(S["g1"], gi + 1)
                            dsth = hsl_h[:, hb + (jj * NTH + th) * TN:
                                         hb + (jj * NTH + th + 1) * TN]
                            dstl = hsl_l[:, hb + (jj * NTH + th) * TN:
                                         hb + (jj * NTH + th + 1) * TN]
                            v.tensor_copy(dsth, h32[:, (gi % 2) * 512:(gi % 2) * 512 + TN])
                            v.tensor_sub(dstl, h32[:, (gi % 2) * 512:(gi % 2) * 512 + TN],
                                         dsth).then_inc(S["h"], 2)
                    if sl >= 1:
                        for fi in range(NFILL):
                            ff = NFILL * (sl - 1) + fi
                            v.wait_ge(S["f2"], ff + 1)
                            if ff >= 2:
                                v.wait_ge(S["f2d"], 16 * (ff - 1))
                            v.tensor_copy(f2sb[:, (ff % 2) * 512:(ff % 2 + 1) * 512],
                                          ffb[ff % 2][:, :]).then_inc(S["fe"], 1)
                for fi in range(NFILL):
                    ff = NFILL * (NSLAB - 1) + fi
                    v.wait_ge(S["f2"], ff + 1)
                    if ff >= 2:
                        v.wait_ge(S["f2d"], 16 * (ff - 1))
                    v.tensor_copy(f2sb[:, (ff % 2) * 512:(ff % 2 + 1) * 512],
                                  ffb[ff % 2][:, :]).then_inc(S["fe"], 1)

            @b3a.gpsimd
            def _(g):
                g = _AD(g)
                for sl in range(NSLAB):
                    for fi in range(NFILL):
                        tt, dg = divmod(fi, 4)
                        ff = NFILL * sl + fi
                        if ff >= 1:
                            g.wait_ge(S["f2d"], 16 * ff)
                        g.wait_ge(S["fe"], ff + 1)
                        g.dma_start(
                            out=ffn_acc[tt * 128:(tt + 1) * 128, dg * 512:(dg + 1) * 512],
                            in_=f2sb[:, (ff % 2) * 512:(ff % 2 + 1) * 512],
                            accum_op=(OP.add if sl >= 1 else OP.bypass)
                        ).then_inc(S["f2d"], 16)
                g.wait_ge(S["f2d"], 16 * NFILL * NSLAB)

        with ExitStack() as _es:
            rs_sb = _es.enter_context(nc.sbuf_tensor("rs_sb", [128, 2 * D], F32))
            xf_sb = _es.enter_context(nc.sbuf_tensor("xf_sb", [128, 2 * D], F32))
            fa_sb = _es.enter_context(nc.sbuf_tensor("fa_sb", [128, 2 * D], F32))
            bp_sb = _es.enter_context(nc.sbuf_tensor("bp_sb", [128, D], F32))
            b22_sb = _es.enter_context(nc.sbuf_tensor("b22_sb", [128, D], F32))
            cmb = _es.enter_context(nc.sbuf_tensor("cmb", [128, 2 * D], F32))
            b3b = _es.enter_context(nc.Block())
            R3SL = [S["r3"], S["r3b"]]

            @b3b.sync
            def _(s):
                s = _AD(s)
                s.wait_ge(S["cc"], 4)
                s.wait_ge(S["ld"], 16 * 12)
                s.dma_start(out=bp_sb[:, :], in_=bp_in[:, :]).then_inc(S["ld"], 16)
                s.dma_start(out=b22_sb[:, :], in_=b2_in[:, :]).then_inc(S["ld"], 16)
                s.wait_ge(S["ld"], 16 * 14)
                s.wait_ge(S["f2d"], 16 * NFILL * NSLAB)
                for t in range(TOKT):
                    if t >= 2:
                        s.wait_ge(R3SL[t % 2], 48 * (t // 2))
                        s.wait_ge(S["o3"], t - 1)
                    s.dma_start(out=rs_sb[:, (t % 2) * D:(t % 2 + 1) * D],
                                in_=rs_out[t * 128:(t + 1) * 128, :]).then_inc(R3SL[t % 2], 16)
                    s.dma_start(out=xf_sb[:, (t % 2) * D:(t % 2 + 1) * D],
                                in_=x_own[t * 128:(t + 1) * 128, :]).then_inc(R3SL[t % 2], 16)
                    s.dma_start(out=fa_sb[:, (t % 2) * D:(t % 2 + 1) * D],
                                in_=ffn_acc[t * 128:(t + 1) * 128, :]).then_inc(R3SL[t % 2], 16)

            @b3b.vector
            def _(v):
                v = _AD(v)
                v.wait_ge(S["ld"], 16 * 14)
                v.tensor_add(bp_sb[:, :], bp_sb[:, :], b22_sb[:, :])
                for t in range(TOKT):
                    v.wait_ge(R3SL[t % 2], 48 * (t // 2 + 1))
                    if t >= 2:
                        v.wait_ge(S["stq"], 16 * (t - 1))
                    sl_ = slice((t % 2) * D, (t % 2 + 1) * D)
                    v.tensor_add(cmb[:, sl_], rs_sb[:, sl_], fa_sb[:, sl_])
                    v.tensor_add(cmb[:, sl_], cmb[:, sl_], xf_sb[:, sl_])
                    v.tensor_add(cmb[:, sl_], cmb[:, sl_],
                                 bp_sb[:, :]).then_inc(S["o3"], 1)

            @b3b.gpsimd
            def _(g):
                g = _AD(g)
                for t in range(TOKT):
                    g.wait_ge(S["o3"], t + 1)
                    g.dma_start(out=out_own[t * 128:(t + 1) * 128, :],
                                in_=cmb[:, (t % 2) * D:(t % 2 + 1) * D]
                                ).then_inc(S["stq"], 16)
                g.wait_ge(S["stq"], 16 * TOKT)


def stage_inputs(x, W_qkv, b_qkv, W_proj, b_proj, w1, b1, w2, b2,
                 T=8192, DFF=8192):
    import ml_dtypes
    TOK = T // NCORE
    NFT = DFF // 128
    x2 = np.ascontiguousarray(np.asarray(x, dtype=np.float32).reshape(T, D))
    W_qkv = np.asarray(W_qkv, np.float32)
    b_qkv = np.asarray(b_qkv, np.float32)
    W_proj = np.asarray(W_proj, np.float32)
    b_proj = np.asarray(b_proj, np.float32)
    w1 = np.asarray(w1, np.float32)
    b1 = np.asarray(b1, np.float32)
    w2 = np.asarray(w2, np.float32)
    b2 = np.asarray(b2, np.float32)
    NQ = W_qkv.shape[1] // 3
    ident = np.eye(128, dtype=ml_dtypes.bfloat16)
    masks = np.zeros((4, 128, 512), np.float32)
    for j in range(4):
        kk = np.arange(128)[:, None] + j * 128
        qq = np.arange(512)[None, :]
        masks[j] = (qq >= kk).astype(np.float32)
    in_maps = []
    for c in range(NCORE):
        fo = c * FOWN
        d = {
            "x_full": x2,
            "x_own": np.ascontiguousarray(x2[c * TOK:(c + 1) * TOK]),
            "wq_in": np.ascontiguousarray(W_qkv[:, fo:fo + FOWN]),
            "wk_in": np.ascontiguousarray(W_qkv[:, NQ + fo:NQ + fo + FOWN]),
            "wv_in": np.ascontiguousarray(W_qkv[:, 2 * NQ + fo:2 * NQ + fo + FOWN]),
            "bq_in": np.ascontiguousarray(
                b_qkv[fo:fo + FOWN].reshape(2, 128).T),
            "bk_in": np.ascontiguousarray(
                b_qkv[NQ + fo:NQ + fo + FOWN].reshape(2, 128).T),
            "bv_in": np.ascontiguousarray(np.broadcast_to(
                b_qkv[2 * NQ + fo:2 * NQ + fo + FOWN][None, :], (128, FOWN))),
            "wp_in": np.ascontiguousarray(W_proj[fo:fo + FOWN, :]),
            "bp_in": np.ascontiguousarray(
                np.broadcast_to(b_proj[None, :], (128, D))),
            "b2_in": np.ascontiguousarray(
                np.broadcast_to(b2[None, :], (128, D))),
            "b1_in": np.ascontiguousarray(b1.reshape(NFT, 128).T),
            "w1_sh": np.ascontiguousarray(
                w1[c * (DFF // 8):(c + 1) * (DFF // 8), :]),
            "w2_sh": np.ascontiguousarray(
                w2[c * (D // 8):(c + 1) * (D // 8), :]),
            "id_in": ident,
            "mk_in": masks,
        }
        in_maps.append(d)
    return in_maps


def gather_output(results, T=8192):
    TOK = T // NCORE
    out = np.concatenate([np.asarray(r["out_own"]) for r in results], axis=0)
    return out.reshape(1, T, D)


from concourse.bass_utils import run_bass_kernel_spmd

_NC_CACHE = {}


def kernel(L0_attn_x, L0_attn_W_qkv, L0_attn_b_qkv, L0_attn_W_proj,
           L0_attn_b_proj, L0_ffn_w1, L0_ffn_b1, L0_ffn_w2, L0_ffn_b2):
    T, DFF = 8192, 8192
    if "nc" not in _NC_CACHE:
        _NC_CACHE["nc"] = build(T=T, DFF=DFF)
    nc = _NC_CACHE["nc"]
    in_maps = stage_inputs(L0_attn_x, L0_attn_W_qkv, L0_attn_b_qkv,
                           L0_attn_W_proj, L0_attn_b_proj,
                           L0_ffn_w1, L0_ffn_b1, L0_ffn_w2, L0_ffn_b2,
                           T=T, DFF=DFF)
    res = run_bass_kernel_spmd(nc, in_maps, core_ids=list(range(NCORE)))
    return gather_output(res.results, T=T).astype(np.float32)


# version 08
# speedup vs baseline: 1.1877x; 1.1877x over previous
import math
from contextlib import ExitStack

import numpy as np
import concourse.bass as bass
import concourse.mybir as mybir

F32 = mybir.dt.float32
BF16 = mybir.dt.bfloat16
AF = mybir.ActivationFunctionType
OP = mybir.AluOpType
AX = mybir.AxisListType

D = 2048
HD = 128
FOWN = 256
NCORE = 8
EPS = 1e-5
QS = np.float32(1.0 / math.sqrt(HD))
ESHIFT = np.float32(-20.0)
f32 = np.float32




class _AD:
    _OPS = {"memset", "tensor_copy", "tensor_tensor", "tensor_add",
            "tensor_sub", "tensor_mul", "tensor_max", "tensor_scalar",
            "tensor_single_scalar", "tensor_scalar_add", "tensor_scalar_sub",
            "tensor_scalar_mul", "tensor_scalar_max", "tensor_scalar_min",
            "reduce_sum", "reduce_max", "tensor_reduce",
            "scalar_tensor_tensor", "reciprocal", "activation", "copy",
            "select", "transpose"}

    def __init__(self, eng):
        self._e = eng
        self._wmax = {}

    def wait_ge(self, sem, val):
        key = id(sem)
        if val <= self._wmax.get(key, -1):
            return
        self._wmax[key] = val
        self._e.wait_ge(sem, val)

    def __getattr__(self, n):
        f = getattr(self._e, n)
        if n in _AD._OPS:
            def wrapped(*a, **k):
                r = f(*a, **k)
                self._e.drain()
                return r
            return wrapped
        return f


def build(T=8192, DFF=8192, phases=(1, 2, 3), dbg=()):
    TOK = T // NCORE
    NT = T // 128
    NG = NT // 2
    QR = T // 512
    NP = 2 * QR
    NFT = DFF // 128
    NSLAB = DFF // 512
    TOKT = TOK // 128
    TN = min(512, TOK)
    NTH = TOK // TN
    W1S = DFF // 8
    W2S = D // 8
    QW = min(4096, DFF)
    QBW = max(D, QW)
    assert T % 1024 == 0 and DFF % 512 == 0 and TOK % 128 == 0

    nc = bass.Bass("TRN2", target_bir_lowering=False)

    x_full = nc.dram_tensor("x_full", [T, D], F32, kind="ExternalInput").ap()
    x_own = nc.dram_tensor("x_own", [TOK, D], F32, kind="ExternalInput").ap()
    wq_in = nc.dram_tensor("wq_in", [D, FOWN], F32, kind="ExternalInput").ap()
    wk_in = nc.dram_tensor("wk_in", [D, FOWN], F32, kind="ExternalInput").ap()
    wv_in = nc.dram_tensor("wv_in", [D, FOWN], F32, kind="ExternalInput").ap()
    bq_in = nc.dram_tensor("bq_in", [128, 2], F32, kind="ExternalInput").ap()
    bk_in = nc.dram_tensor("bk_in", [128, 2], F32, kind="ExternalInput").ap()
    bv_in = nc.dram_tensor("bv_in", [128, FOWN], F32, kind="ExternalInput").ap()
    wp_in = nc.dram_tensor("wp_in", [FOWN, D], F32, kind="ExternalInput").ap()
    bp_in = nc.dram_tensor("bp_in", [128, D], F32, kind="ExternalInput").ap()
    b2_in = nc.dram_tensor("b2_in", [128, D], F32, kind="ExternalInput").ap()
    b1_in = nc.dram_tensor("b1_in", [128, NFT], F32, kind="ExternalInput").ap()
    w1_sh = nc.dram_tensor("w1_sh", [W1S, D], F32, kind="ExternalInput").ap()
    w2_sh = nc.dram_tensor("w2_sh", [W2S, DFF], F32, kind="ExternalInput").ap()
    id_in = nc.dram_tensor("id_in", [128, 128], BF16, kind="ExternalInput").ap()
    mk_in = nc.dram_tensor("mk_in", [4, 128, 512], F32, kind="ExternalInput").ap()

    out_own = nc.dram_tensor("out_own", [TOK, D], F32, kind="ExternalOutput").ap()

    qT_hi = nc.dram_tensor("qT_hi", [FOWN, T], BF16).ap()
    qT_lo = nc.dram_tensor("qT_lo", [FOWN, T], BF16).ap()
    kT_hi = nc.dram_tensor("kT_hi", [FOWN, T], BF16).ap()
    kT_lo = nc.dram_tensor("kT_lo", [FOWN, T], BF16).ap()
    v_sh = nc.dram_tensor("v_sh", [T, FOWN], BF16).ap()
    v_sl = nc.dram_tensor("v_sl", [T, FOWN], BF16).ap()
    proj_part = nc.dram_tensor("proj_part", [T, D], F32).ap()
    rs_out = nc.dram_tensor("rs_out", [TOK, D], F32).ap()
    ffn_acc = nc.dram_tensor("ffn_acc", [TOK, D], F32).ap()
    sc_part = nc.dram_tensor("sc_part", [128, 2], F32).ap()
    sc_sum = nc.dram_tensor("sc_sum", [128, 2], F32).ap()
    wq1_shd = nc.dram_tensor("wq1_shd", [D, W1S], BF16).ap()
    wq2_shd = nc.dram_tensor("wq2_shd", [DFF, W2S], BF16).ap()
    wq1_all = nc.dram_tensor("wq1_all", [8 * D, W1S], BF16).ap()
    wq2_all = nc.dram_tensor("wq2_all", [8 * DFF, W2S], BF16).ap()

    dbg_map = {"qT_hi": qT_hi, "qT_lo": qT_lo, "kT_hi": kT_hi, "kT_lo": kT_lo,
               "v_sh": v_sh, "v_sl": v_sl, "proj_part": proj_part,
               "rs_out": rs_out, "wq1_all": wq1_all, "wq2_all": wq2_all,
               "sc_sum": sc_sum, "ffn_acc": ffn_acc}
    dbg_outs = []
    for name in dbg:
        src = dbg_map[name]
        dst = nc.dram_tensor("dbg_" + name, list(src.shape), src.dtype,
                             kind="ExternalOutput").ap()
        dbg_outs.append((dst, src))

    qtiles = []
    for i in range(W1S // 128):
        dstT = bass.AP(wq1_shd.tensor, i * 128,
                       [[W1S, 128], [128 * W1S, D // 128], [1, 128]])
        qtiles.append((w1_sh[i * 128:(i + 1) * 128, :], dstT, D, 0))
    for i in range(W2S // 128):
        for hf in range(DFF // QW):
            dstT = bass.AP(wq2_shd.tensor, hf * QW * W2S + i * 128,
                           [[W2S, 128], [128 * W2S, QW // 128], [1, 128]])
            qtiles.append((w2_sh[i * 128:(i + 1) * 128, hf * QW:(hf + 1) * QW],
                           dstT, QW, 1))
    NQT = len(qtiles)
    qe_base = [0]
    for qi in range(NQT):
        qe_base.append(qe_base[-1] + qtiles[qi][2] // 1024)
    qt_base = qe_base

    kt_cnt = [4 * (p // 2 + 1) for p in range(NP)]
    kt_base = [0]
    for p in range(NP):
        kt_base.append(kt_base[-1] + kt_cnt[p])
    NKT = kt_base[-1]
    qk_cnt = [2 + 4 * (p // 2 + 1) for p in range(NP)]
    qk_base = [0]
    for p in range(NP):
        qk_base.append(qk_base[-1] + qk_cnt[p])

    st = ExitStack()
    with st:
        semd = {}
        for n in ["ld", "pq", "pt", "cc", "qd",
                  "x", "a", "b", "c", "d", "e", "f", "tp", "te",
                  "mm", "ev", "hi", "lo", "dq",
                  "qk", "Sf", "Pf", "Pv", "Vf", "Of", "dn", "rr",
                  "rb", "rc", "no", "oh", "pj", "pe", "pd",
                  "xo", "la", "lb", "lc", "ldd", "le", "lf", "t3", "e3",
                  "w1", "w2", "u1", "f1", "g1", "h", "f2", "fe", "f2d",
                  "f2d0", "f2d1", "f2d2", "f2d3", "f2d4", "f2d5", "f2d6", "f2d7",
                  "r3", "r3b", "o3", "stq", "qA", "qB", "kA", "kB", "w1b", "w2b", "qt", "qe",
                  "qla", "qlb", "qsa", "qsb"]:
            semd[n] = st.enter_context(nc.semaphore("s_" + n))
        S = semd
        QLS = [S["qla"], S["qlb"]]
        QSS = [S["qsa"], S["qsb"]]
        QAT = [(NQT + 1) // 2, NQT // 2]

        ident = st.enter_context(nc.sbuf_tensor("ident", [128, 128], BF16))
        ones_c = st.enter_context(nc.sbuf_tensor("ones_c", [128, 1], BF16))
        ones_sq = st.enter_context(nc.sbuf_tensor("ones_sq", [128, 128], BF16))
        ones_k1 = st.enter_context(nc.sbuf_tensor("ones_k1", [1, 128], BF16))
        bqs = st.enter_context(nc.sbuf_tensor("bqs", [128, 2], F32))
        bk_sb = st.enter_context(nc.sbuf_tensor("bk_sb", [128, 2], F32))
        bv_sb = st.enter_context(nc.sbuf_tensor("bv_sb", [128, FOWN], F32))
        thrp = st.enter_context(nc.sbuf_tensor("thrp", [128, 2], F32))
        thrn = st.enter_context(nc.sbuf_tensor("thrn", [128, 2], F32))

        with ExitStack() as _esw:
            wq_h = _esw.enter_context(nc.sbuf_tensor("wq_h", [128, 16 * FOWN], BF16))
            wq_l = _esw.enter_context(nc.sbuf_tensor("wq_l", [128, 16 * FOWN], BF16))
            wk_h = _esw.enter_context(nc.sbuf_tensor("wk_h", [128, 16 * FOWN], BF16))
            wk_l = _esw.enter_context(nc.sbuf_tensor("wk_l", [128, 16 * FOWN], BF16))
            wv_h = _esw.enter_context(nc.sbuf_tensor("wv_h", [128, 16 * FOWN], BF16))
            wv_l = _esw.enter_context(nc.sbuf_tensor("wv_l", [128, 16 * FOWN], BF16))
            with ExitStack() as _es:
                wbig = _es.enter_context(nc.sbuf_tensor("wbig", [128, 4096], F32))
                qbuf = _es.enter_context(nc.sbuf_tensor("qbuf", [128, 2 * QBW], F32))
                qtmp = _es.enter_context(nc.sbuf_tensor("qtmp", [128, QBW], BF16))
                qout = _es.enter_context(nc.sbuf_tensor("qout", [128, 2 * QBW], BF16))
                qstage = _es.enter_context(nc.sbuf_tensor("qstage", [128, 2 * QBW], BF16))
                acc2 = _es.enter_context(nc.sbuf_tensor("acc2", [128, 2], F32))
                rtmp = _es.enter_context(nc.sbuf_tensor("rtmp", [128, 1], F32))
                ac_h = _es.enter_context(nc.sbuf_tensor("ac_h", [128, 2], BF16))
                ac_l = _es.enter_context(nc.sbuf_tensor("ac_l", [128, 2], BF16))
                tpq0 = _es.enter_context(nc.psum_tensor("tpq0", [128, 1024], BF16))
                tpq1 = _es.enter_context(nc.psum_tensor("tpq1", [128, 1024], BF16))
                pp0 = _es.enter_context(nc.psum_tensor("pp0", [128, 512], F32))
                b0 = _es.enter_context(nc.Block())
                @b0.sync
                def _(s):
                    s = _AD(s)
                    s.dma_start(out=ident[:, :], in_=id_in[:, :]).then_inc(S["ld"], 16)
                    s.dma_start(out=bqs[:, :], in_=bq_in[:, :]).then_inc(S["ld"], 16)
                    s.dma_start(out=bk_sb[:, :], in_=bk_in[:, :]).then_inc(S["ld"], 16)
                    s.dma_start(out=bv_sb[:, :], in_=bv_in[:, :]).then_inc(S["ld"], 16)
                    for wi, win in enumerate([wq_in, wk_in, wv_in]):
                        s.wait_ge(S["ld"], 16 * (4 + wi))
                        s.wait_ge(S["pq"], 2 * wi)
                        s.dma_start(
                            out=wbig[:, 0:16 * FOWN],
                            in_=bass.AP(win.tensor, 0,
                                        [[FOWN, 128], [128 * FOWN, 16], [1, FOWN]]),
                        ).then_inc(S["ld"], 16)

                @b0.vector
                def _(v):
                    v = _AD(v)
                    v.memset(ones_c[:, :], 1.0)
                    v.memset(ones_sq[:, :], 1.0)
                    v.memset(ones_k1[:, :], 1.0)
                    v.memset(acc2[:, :], 0.0)
                    v.wait_ge(S["ld"], 16 * 4)
                    v.tensor_scalar_mul(bqs[:, :], bqs[:, :], float(QS))
                    for wi, (dh, dl) in enumerate([(wq_h, wq_l), (wk_h, wk_l),
                                                   (wv_h, wv_l)]):
                        v.wait_ge(S["ld"], 16 * (5 + wi))
                        v.tensor_copy(dh[:, :], wbig[:, 0:16 * FOWN])
                        v.tensor_sub(dl[:, :], wbig[:, 0:16 * FOWN],
                                     dh[:, :]).then_inc(S["pq"], 2)
                    for qi in range(NQT):
                        w, col = qtiles[qi][2], qtiles[qi][3]
                        v.wait_ge(QLS[qi % 2], 16 * (qi // 2 + 1))
                        v.reduce_sum(rtmp[:, :], qbuf[:, (qi % 2) * QBW:(qi % 2) * QBW + w],
                                     axis=AX.X, apply_absolute_value=True)
                        v.tensor_add(acc2[:, col:col + 1], acc2[:, col:col + 1],
                                     rtmp[:, :]).then_inc(S["pq"], 1)
                    v.tensor_copy(ac_h[:, :], acc2[:, :])
                    v.tensor_sub(ac_l[:, :], acc2[:, :],
                                 ac_h[:, :]).then_inc(S["pq"], 1)
                    v.wait_ge(S["qd"], 16 * 2)
                    v.tensor_scalar_mul(thrp[:, :], thrp[:, :],
                                        float(0.7 / (DFF * D)))
                    v.tensor_scalar_mul(thrn[:, :], thrp[:, :],
                                        -1.0).then_inc(S["pq"], 1)
                    for qi in range(NQT):
                        w, col = qtiles[qi][2], qtiles[qi][3]
                        v.wait_ge(QLS[qi % 2], 16 * (QAT[qi % 2] + qi // 2 + 1))
                        if qi >= 2:
                            v.wait_ge(S["qt"], qt_base[qi - 1])
                        qb = qbuf[:, (qi % 2) * QBW:(qi % 2) * QBW + w]
                        qo = qout[:, (qi % 2) * QBW:(qi % 2) * QBW + w]
                        v.tensor_scalar(qtmp[:, 0:w], qb,
                                        thrp[:, col:col + 1], None, OP.is_ge)
                        v.tensor_scalar(qo, qb,
                                        thrn[:, col:col + 1], None, OP.is_le)
                        v.tensor_sub(qo, qtmp[:, 0:w], qo).then_inc(S["pq"], 1)

                @b0.tensor
                def _(t):
                    t = _AD(t)
                    t.wait_ge(S["ld"], 64)
                    t.wait_ge(S["pq"], 6 + NQT + 1)
                    t.matmul(pp0[:, 0:2], ones_sq[:, :], ac_h[:, :],
                             start=True, stop=False)
                    t.matmul(pp0[:, 0:2], ones_sq[:, :], ac_l[:, :],
                             start=False, stop=True).then_inc(S["pt"], 1)
                    fl = 0
                    for qi in range(NQT):
                        w = qtiles[qi][2]
                        t.wait_ge(S["pq"], 6 + NQT + 2 + qi + 1)
                        for f in range(w // 1024):
                            if fl >= 2:
                                t.wait_ge(S["qe"], fl - 1)
                            bank = [tpq0, tpq1][fl % 2]
                            for ii in range(8):
                                k = f * 8 + ii
                                mmq = t.transpose(
                                    bank[:, ii * 128:(ii + 1) * 128],
                                    qout[:, (qi % 2) * QBW + k * 128:
                                         (qi % 2) * QBW + (k + 1) * 128],
                                    ident[:, :])
                            mmq.then_inc(S["qt"], 1)
                            fl += 1

                @b0.scalar
                def _(sc):
                    sc = _AD(sc)
                    sc.wait_ge(S["pt"], 1)
                    sc.copy(acc2[:, :], pp0[:, 0:2]).then_inc(S["pt"], 1)
                    fl = 0
                    for qi in range(NQT):
                        w = qtiles[qi][2]
                        if qi >= 2:
                            sc.wait_ge(QSS[qi % 2], 16 * (qi // 2))
                        for f in range(w // 1024):
                            sc.wait_ge(S["qt"], fl + 1)
                            sc.copy(qstage[:, (qi % 2) * QBW + f * 1024:
                                            (qi % 2) * QBW + (f + 1) * 1024],
                                    [tpq0, tpq1][fl % 2][:, :]).then_inc(S["qe"], 1)
                            fl += 1

                @b0.gpsimd
                def _(g):
                    g = _AD(g)
                    for qi in range(NQT):
                        w = qtiles[qi][2]
                        if qi >= 2:
                            g.wait_ge(QLS[qi % 2], 16 * (qi // 2))
                            g.wait_ge(S["pq"], 6 + qi - 1)
                        g.dma_start(out=qbuf[:, (qi % 2) * QBW:(qi % 2) * QBW + w],
                                    in_=qtiles[qi][0]).then_inc(QLS[qi % 2], 16)
                    g.wait_ge(S["pt"], 2)
                    g.dma_start(out=sc_part[:, :], in_=acc2[:, :]).then_inc(S["qd"], 16)
                    g.wait_ge(S["qd"], 16)
                    g.collective_compute(
                        "AllReduce", OP.add, replica_groups=[list(range(NCORE))],
                        ins=[sc_part.tensor.ap().opt()],
                        outs=[sc_sum.tensor.ap().opt()]).then_inc(S["cc"], 1)
                    g.wait_ge(S["cc"], 1)
                    g.dma_start(out=thrp[:, :], in_=sc_sum[:, :]).then_inc(S["qd"], 16)
                    def _storeB(g, qi):
                        w = qtiles[qi][2]
                        if qi >= 2:
                            g.wait_ge(QSS[qi % 2], 16 * (qi // 2))
                        g.wait_ge(S["qe"], qe_base[qi + 1])
                        g.dma_start(out=qtiles[qi][1],
                                    in_=qstage[:, (qi % 2) * QBW:(qi % 2) * QBW + w]
                                    ).then_inc(QSS[qi % 2], 16)
                    for qi in range(NQT):
                        w = qtiles[qi][2]
                        g.wait_ge(QLS[qi % 2], 16 * (QAT[qi % 2] + qi // 2))
                        g.wait_ge(S["pq"], 6 + NQT + 2 + qi)
                        g.dma_start(out=qbuf[:, (qi % 2) * QBW:(qi % 2) * QBW + w],
                                    in_=qtiles[qi][0]).then_inc(QLS[qi % 2], 16)
                        if qi >= 2:
                            _storeB(g, qi - 2)
                    _storeB(g, NQT - 2)
                    _storeB(g, NQT - 1)
                    g.wait_ge(QSS[0], 16 * ((NQT + 1) // 2))
                    g.wait_ge(QSS[1], 16 * (NQT // 2))
                    g.collective_compute(
                        "AllGather", OP.bypass, replica_groups=[list(range(NCORE))],
                        ins=[wq1_shd.tensor.ap().opt()],
                        outs=[wq1_all.tensor.ap().opt()]).then_inc(S["cc"], 1)
                    g.wait_ge(S["cc"], 2)
                    g.collective_compute(
                        "AllGather", OP.bypass, replica_groups=[list(range(NCORE))],
                        ins=[wq2_shd.tensor.ap().opt()],
                        outs=[wq2_all.tensor.ap().opt()]).then_inc(S["cc"], 1)

            if 1 in phases:
                with ExitStack() as _es:
                    xg = _es.enter_context(nc.sbuf_tensor("xg", [128, 3 * D], F32))
                    sqb = _es.enter_context(nc.sbuf_tensor("sqb", [128, D], F32))
                    xn32 = _es.enter_context(nc.sbuf_tensor("xn32", [128, 2 * D], F32))
                    xnh = _es.enter_context(nc.sbuf_tensor("xnh", [128, 2 * D], BF16))
                    xnl = _es.enter_context(nc.sbuf_tensor("xnl", [128, 2 * D], BF16))
                    s1 = _es.enter_context(nc.sbuf_tensor("s1", [128, 2], F32))
                    s2 = _es.enter_context(nc.sbuf_tensor("s2", [128, 2], F32))
                    nm = _es.enter_context(nc.sbuf_tensor("nm", [128, 2], F32))
                    vp = _es.enter_context(nc.sbuf_tensor("vp", [128, 2], F32))
                    sr = _es.enter_context(nc.sbuf_tensor("sr", [128, 2], F32))
                    rt1 = _es.enter_context(nc.sbuf_tensor("rt1", [128, 2], F32))
                    rstd = _es.enter_context(nc.sbuf_tensor("rstd", [128, 2], F32))
                    bcol = _es.enter_context(nc.sbuf_tensor("bcol", [128, 2], F32))
                    stg = _es.enter_context(nc.sbuf_tensor("stg", [128, 2 * 16 * 512], BF16))
                    evt = _es.enter_context(nc.sbuf_tensor("evt", [128, 2 * FOWN], F32))
                    ohb = _es.enter_context(nc.sbuf_tensor("ohb", [128, 2 * FOWN], BF16))
                    olb = _es.enter_context(nc.sbuf_tensor("olb", [128, 2 * FOWN], BF16))
                    tp0 = _es.enter_context(nc.psum_tensor("tp0", [128, 1024], BF16))
                    tp1 = _es.enter_context(nc.psum_tensor("tp1", [128, 1024], BF16))
                    mp0 = _es.enter_context(nc.psum_tensor("mp0", [128, 512], F32))
                    mp1 = _es.enter_context(nc.psum_tensor("mp1", [128, 512], F32))
                    b1 = _es.enter_context(nc.Block())
                    tpb = [tp0, tp1]
                    mpb = [mp0, mp1]

                    @b1.sync
                    def _(s):
                        s = _AD(s)
                        for t in range(NT):
                            if t >= 1:
                                s.wait_ge(S["x"], 16 * t)
                            if t >= 3:
                                s.wait_ge(S["a"], t - 2)
                                s.wait_ge(S["d"], t - 2)
                            s.dma_start(
                                out=xg[:, (t % 3) * D:(t % 3 + 1) * D],
                                in_=x_full[t * 128:(t + 1) * 128, :],
                            ).then_inc(S["x"], 16)

                    def fill_dst(m):
                        gq, slot = divmod(m, 6)
                        if slot < 4:
                            ft = slot
                            dsth = (qT_hi if ft < 2 else kT_hi)
                            dstl = (qT_lo if ft < 2 else kT_lo)
                            r0 = (ft % 2) * 128
                            return (dsth[r0:r0 + 128, gq * 256:(gq + 1) * 256],
                                    dstl[r0:r0 + 128, gq * 256:(gq + 1) * 256],
                                    FOWN)
                        tt = slot - 4
                        r0 = (2 * gq + tt) * 128
                        return (v_sh[r0:r0 + 128, :], v_sl[r0:r0 + 128, :], FOWN)

                    @b1.scalar
                    def _(sc):
                        sc = _AD(sc)
                        for i in range(NT + 4):
                            t = i
                            if t < NT:
                                sc.wait_ge(S["x"], 16 * (t + 1))
                                if t >= 2:
                                    sc.wait_ge(S["b"], t - 1)
                                sc.activation(sqb[:, :], xg[:, (t % 3) * D:(t % 3 + 1) * D],
                                              AF.Identity, bias=f32(0.0),
                                              accum_out=s1[:, t % 2:t % 2 + 1])
                                sc.activation(sqb[:, :], xg[:, (t % 3) * D:(t % 3 + 1) * D],
                                              AF.Square, bias=f32(0.0),
                                              accum_out=s2[:, t % 2:t % 2 + 1]
                                              ).then_inc(S["a"], 1)
                            t = i - 1
                            if 0 <= t < NT:
                                sc.wait_ge(S["b"], t + 1)
                                if t >= 2:
                                    sc.wait_ge(S["d"], t - 1)
                                sc.activation(sr[:, t % 2:t % 2 + 1],
                                              vp[:, t % 2:t % 2 + 1],
                                              AF.Sqrt, bias=f32(0.0)
                                              ).then_inc(S["c"], 1)
                            t = i - 2
                            if 0 <= t < NT:
                                sc.wait_ge(S["d"], t + 1)
                                if t >= 2:
                                    sc.wait_ge(S["tp"], 4 * (t - 1))
                                sc.copy(xnh[:, (t % 2) * D:(t % 2 + 1) * D],
                                        xn32[:, (t % 2) * D:(t % 2 + 1) * D]
                                        ).then_inc(S["e"], 1)
                            t = i - 3
                            if 0 <= t < NT:
                                gq = t // 2
                                for b4 in range(4):
                                    fl = 4 * t + b4
                                    sc.wait_ge(S["tp"], fl + 1)
                                    if gq >= 2:
                                        sc.wait_ge(S["mm"], 6 * (gq - 1))
                                    dst = bass.AP(
                                        stg,
                                        (gq % 2) * (16 * 512) + b4 * 4 * 512 + (t % 2) * 128,
                                        [[2 * 16 * 512, 128], [512, 4], [256, 2], [1, 128]],
                                    )
                                    sc.copy(dst, tpb[fl % 2][:, :]).then_inc(S["te"], 1)
                            j = i - 4
                            if 0 <= j < NT:
                                gq, half = divmod(j, 2)
                                for m in range(6 * gq + 3 * half, 6 * gq + 3 * half + 3):
                                    sc.wait_ge(S["ev"], m + 1)
                                    if m >= 2:
                                        sc.wait_ge(S["dq"], 32 * (m - 1))
                                    sc.copy(ohb[:, (m % 2) * FOWN:(m % 2 + 1) * FOWN],
                                            evt[:, (m % 2) * FOWN:(m % 2 + 1) * FOWN]
                                            ).then_inc(S["hi"], 1)

                    @b1.vector
                    def _(v):
                        v = _AD(v)
                        for i in range(NT + 4):
                            t = i
                            if t < NT:
                                v.wait_ge(S["a"], t + 1)
                                if t >= 2:
                                    v.wait_ge(S["c"], t - 1)
                                ts = slice(t % 2, t % 2 + 1)
                                v.tensor_scalar_mul(nm[:, ts], s1[:, ts], -1.0 / D)
                                v.tensor_scalar_mul(vp[:, ts], s2[:, ts], 1.0 / D)
                                v.tensor_mul(rt1[:, ts], nm[:, ts], nm[:, ts])
                                v.tensor_sub(vp[:, ts], vp[:, ts], rt1[:, ts])
                                v.tensor_scalar_add(vp[:, ts], vp[:, ts],
                                                    float(EPS)).then_inc(S["b"], 1)
                            t = i - 1
                            if 0 <= t < NT:
                                v.wait_ge(S["c"], t + 1)
                                if t >= 2:
                                    v.wait_ge(S["e"], t - 1)
                                ts = slice(t % 2, t % 2 + 1)
                                v.reciprocal(rstd[:, ts], sr[:, ts])
                                v.tensor_mul(rt1[:, ts], rstd[:, ts], rstd[:, ts])
                                v.tensor_mul(rt1[:, ts], rt1[:, ts], vp[:, ts])
                                v.tensor_scalar(rt1[:, ts], rt1[:, ts],
                                                -0.5, 1.5, OP.mult, OP.add)
                                v.tensor_mul(rstd[:, ts], rstd[:, ts], rt1[:, ts])
                                v.tensor_mul(bcol[:, ts], nm[:, ts], rstd[:, ts])
                                v.scalar_tensor_tensor(
                                    xn32[:, (t % 2) * D:(t % 2 + 1) * D],
                                    xg[:, (t % 3) * D:(t % 3 + 1) * D],
                                    rstd[:, ts],
                                    bcol[:, ts].broadcast_to([128, D]),
                                    OP.mult, OP.add).then_inc(S["d"], 1)
                            t = i - 2
                            if 0 <= t < NT:
                                v.wait_ge(S["e"], t + 1)
                                if t >= 2:
                                    v.wait_ge(S["tp"], 4 * (t - 1))
                                v.tensor_sub(xnl[:, (t % 2) * D:(t % 2 + 1) * D],
                                             xn32[:, (t % 2) * D:(t % 2 + 1) * D],
                                             xnh[:, (t % 2) * D:(t % 2 + 1) * D]
                                             ).then_inc(S["f"], 1)
                            j = i - 4
                            if 0 <= j < NT:
                                gq, half = divmod(j, 2)
                                for m in range(6 * gq + 3 * half, 6 * gq + 3 * half + 3):
                                    slot = m % 6
                                    v.wait_ge(S["mm"], m + 1)
                                    if m >= 2:
                                        v.wait_ge(S["hi"], m - 1)
                                    if slot < 2:
                                        v.scalar_tensor_tensor(
                                            evt[:, (m % 2) * FOWN:(m % 2 + 1) * FOWN],
                                            mpb[m % 2][:, 0:FOWN], float(QS),
                                            bqs[:, slot:slot + 1].broadcast_to([128, FOWN]),
                                            OP.mult, OP.add).then_inc(S["ev"], 1)
                                    elif slot < 4:
                                        v.scalar_tensor_tensor(
                                            evt[:, (m % 2) * FOWN:(m % 2 + 1) * FOWN],
                                            mpb[m % 2][:, 0:FOWN], 1.0,
                                            bk_sb[:, slot - 2:slot - 1].broadcast_to([128, FOWN]),
                                            OP.mult, OP.add).then_inc(S["ev"], 1)
                                    else:
                                        v.tensor_add(
                                            evt[:, (m % 2) * FOWN:(m % 2 + 1) * FOWN],
                                            mpb[m % 2][:, 0:FOWN],
                                            bv_sb[:, :]).then_inc(S["ev"], 1)
                                    v.wait_ge(S["hi"], m + 1)
                                    if m >= 2:
                                        v.wait_ge(S["dq"], 32 * (m - 1))
                                    v.tensor_sub(
                                        olb[:, (m % 2) * FOWN:(m % 2 + 1) * FOWN],
                                        evt[:, (m % 2) * FOWN:(m % 2 + 1) * FOWN],
                                        ohb[:, (m % 2) * FOWN:(m % 2 + 1) * FOWN]
                                        ).then_inc(S["lo"], 1)

                    @b1.tensor
                    def _(t_):
                        t_ = _AD(t_)
                        for i in range(NT):
                            t = i
                            t_.wait_ge(S["f"], t + 1)
                            for b4 in range(4):
                                fl = 4 * t + b4
                                if fl >= 2:
                                    t_.wait_ge(S["te"], fl - 1)
                                for ii in range(8):
                                    idx = b4 * 8 + ii
                                    k, hl = divmod(idx, 2)
                                    src = (xnh if hl == 0 else xnl)
                                    mm = t_.transpose(
                                        tpb[fl % 2][:, ii * 128:(ii + 1) * 128],
                                        src[:, (t % 2) * D + k * 128:(t % 2) * D + (k + 1) * 128],
                                        ident[:, :])
                                mm.then_inc(S["tp"], 1)
                            if t % 2 == 1:
                                gq = t // 2
                                t_.wait_ge(S["te"], 8 * gq + 8)
                                for slot in range(6):
                                    m = 6 * gq + slot
                                    if m >= 2:
                                        t_.wait_ge(S["ev"], m - 1)
                                    ps = mpb[m % 2]
                                    if slot < 4:
                                        wsh = wq_h if slot < 2 else wk_h
                                        wsl = wq_l if slot < 2 else wk_l
                                        jj = (slot % 2) * 128
                                        for k in range(16):
                                            sh = wsh[:, k * FOWN + jj:k * FOWN + jj + 128]
                                            sl = wsl[:, k * FOWN + jj:k * FOWN + jj + 128]
                                            xh = stg[:, (gq % 2) * 16 * 512 + k * 512:
                                                     (gq % 2) * 16 * 512 + k * 512 + 256]
                                            xl = stg[:, (gq % 2) * 16 * 512 + k * 512 + 256:
                                                     (gq % 2) * 16 * 512 + k * 512 + 512]
                                            t_.matmul(ps[:, 0:FOWN], sh, xh,
                                                      start=(k == 0), stop=False)
                                            t_.matmul(ps[:, 0:FOWN], sl, xh,
                                                      start=False, stop=False)
                                            mm = t_.matmul(ps[:, 0:FOWN], sh, xl,
                                                           start=False, stop=(k == 15))
                                        mm.then_inc(S["mm"], 1)
                                    else:
                                        tt = slot - 4
                                        for k in range(16):
                                            base = (gq % 2) * 16 * 512 + k * 512
                                            sh = stg[:, base + tt * 128:base + tt * 128 + 128]
                                            sl = stg[:, base + 256 + tt * 128:base + 256 + tt * 128 + 128]
                                            mh = wv_h[:, k * FOWN:(k + 1) * FOWN]
                                            ml = wv_l[:, k * FOWN:(k + 1) * FOWN]
                                            t_.matmul(ps[:, 0:FOWN], sh, mh,
                                                      start=(k == 0), stop=False)
                                            t_.matmul(ps[:, 0:FOWN], sl, mh,
                                                      start=False, stop=False)
                                            mm = t_.matmul(ps[:, 0:FOWN], sh, ml,
                                                           start=False, stop=(k == 15))
                                        mm.then_inc(S["mm"], 1)

                    @b1.gpsimd
                    def _(g):
                        g = _AD(g)
                        for m in range(6 * NG):
                            dsth, dstl, wdt = fill_dst(m)
                            if m >= 1:
                                g.wait_ge(S["dq"], 32 * m)
                            g.wait_ge(S["hi"], m + 1)
                            g.dma_start(out=dsth,
                                        in_=ohb[:, (m % 2) * FOWN:(m % 2 + 1) * FOWN]
                                        ).then_inc(S["dq"], 16)
                            g.wait_ge(S["lo"], m + 1)
                            g.dma_start(out=dstl,
                                        in_=olb[:, (m % 2) * FOWN:(m % 2 + 1) * FOWN]
                                        ).then_inc(S["dq"], 16)

        if 2 in phases:
            _phase2(nc, S, locals())
        if 3 in phases:
            _phase3(nc, S, locals())

        if dbg_outs:
            totals = {"qd": 32, "cc": 3}
            if 1 in phases:
                totals["dq"] = 16 * 12 * NG
            if 2 in phases:
                totals["pd"] = 16 * 16 * QR
                totals["cc"] = 4
            if 3 in phases:
                for _tt in range(TOKT):
                    totals["f2d%d" % _tt] = 16 * NSLAB
                totals["stq"] = 16 * TOKT
            with nc.Block() as bd:
                @bd.gpsimd
                def _(g):
                    for nm_, tot in totals.items():
                        g.wait_ge(S[nm_], tot)
                    n0 = 0
                    for dst, src_ in dbg_outs:
                        g.dma_start(out=dst[...], in_=src_[...]).then_inc(S["qd"], 16)
                        n0 += 16
                    g.wait_ge(S["qd"], totals["qd"] + n0)

    return nc


def _phase2(nc, S, env):
    T, QR, NP = env["T"], env["QR"], env["NP"]
    NG = env["NG"]
    kt_base = env["kt_base"]; kt_cnt = env["kt_cnt"]
    qk_base = env["qk_base"]
    qT_hi, qT_lo, kT_hi, kT_lo = env["qT_hi"], env["qT_lo"], env["kT_hi"], env["kT_lo"]
    v_sh, v_sl = env["v_sh"], env["v_sl"]
    wp_in, mk_in = env["wp_in"], env["mk_in"]
    proj_part, rs_out = env["proj_part"], env["rs_out"]
    ones_c, ones_k1 = env["ones_c"], env["ones_k1"]
    phases = env["phases"]
    dq_tot = 16 * 12 * NG if 1 in phases else 0

    with ExitStack() as _es:
        masks = _es.enter_context(nc.sbuf_tensor("masks", [128, 4 * 512], F32))
        wpb = _es.enter_context(nc.sbuf_tensor("wpb", [128, 2 * D], F32))
        wp_h = _es.enter_context(nc.sbuf_tensor("wp_h", [128, 2 * D], BF16))
        wp_l = _es.enter_context(nc.sbuf_tensor("wp_l", [128, 2 * D], BF16))
        qc_h = _es.enter_context(nc.sbuf_tensor("qc_h", [128, 2 * 512], BF16))
        qc_l = _es.enter_context(nc.sbuf_tensor("qc_l", [128, 2 * 512], BF16))
        kc_h = _es.enter_context(nc.sbuf_tensor("kc_h", [128, 2 * 512], BF16))
        kc_l = _es.enter_context(nc.sbuf_tensor("kc_l", [128, 2 * 512], BF16))
        vc_h = _es.enter_context(nc.sbuf_tensor("vc_h", [128, 2 * 512], BF16))
        vc_l = _es.enter_context(nc.sbuf_tensor("vc_l", [128, 2 * 512], BF16))
        P32 = _es.enter_context(nc.sbuf_tensor("P32", [128, 2 * 512], F32))
        Ph = _es.enter_context(nc.sbuf_tensor("Ph", [128, 2 * 512], BF16))
        Pl = _es.enter_context(nc.sbuf_tensor("Pl", [128, 2 * 512], BF16))
        rb_sb = _es.enter_context(nc.sbuf_tensor("rb_sb", [128, 2 * 512], F32))
        r32 = _es.enter_context(nc.sbuf_tensor("r32", [1, 2 * 512], F32))
        r_h = _es.enter_context(nc.sbuf_tensor("r_h", [1, 2 * 512], BF16))
        r_l = _es.enter_context(nc.sbuf_tensor("r_l", [1, 2 * 512], BF16))
        On32 = _es.enter_context(nc.sbuf_tensor("On32", [128, 2 * 512], F32))
        Oh = _es.enter_context(nc.sbuf_tensor("Oh", [128, 2 * 512], BF16))
        Ol = _es.enter_context(nc.sbuf_tensor("Ol", [128, 2 * 512], BF16))
        pj_sb = _es.enter_context(nc.sbuf_tensor("pj_sb", [128, 2 * 512], F32))
        sp0 = _es.enter_context(nc.psum_tensor("sp0", [128, 512], F32))
        sp1 = _es.enter_context(nc.psum_tensor("sp1", [128, 512], F32))
        Op0 = _es.enter_context(nc.psum_tensor("Op0", [128, 512], F32))
        Op1 = _es.enter_context(nc.psum_tensor("Op1", [128, 512], F32))
        den = _es.enter_context(nc.psum_tensor("den", [1, 512], F32))
        pj0 = _es.enter_context(nc.psum_tensor("pj0", [128, 512], F32))
        pj1 = _es.enter_context(nc.psum_tensor("pj1", [128, 512], F32))
        b2 = _es.enter_context(nc.Block())
        spb = [sp0, sp1]
        Opb = [Op0, Op1]
        pjb = [pj0, pj1]

        qslot_use = {}
        kslot_use = {}
        kslot_prev_end = {}
        _qcnt = [0, 0]
        _kcnt = [0, 0]
        _klast = [None, None]
        for p in range(NP):
            _qcnt[p % 2] += 1
            qslot_use[p] = _qcnt[p % 2]
            for ck in range(p // 2 + 1):
                _kcnt[ck % 2] += 1
                kslot_use[(p, ck)] = _kcnt[ck % 2]
                kslot_prev_end[(p, ck)] = _klast[ck % 2]
                _klast[ck % 2] = kt_base[p] + 4 * ck + 4
        QSL = [S["qA"], S["qB"]]
        KSL = [S["kA"], S["kB"]]

        @b2.sync
        def _(s):
            s = _AD(s)
            s.wait_ge(S["dq"], dq_tot)
            for j in range(4):
                s.dma_start(out=masks[:, j * 512:(j + 1) * 512],
                            in_=mk_in[j, :, :]).then_inc(S["ld"], 16)
            s.dma_start(out=wpb[:, :],
                        in_=bass.AP(wp_in.tensor, 0,
                                    [[D, 128], [128 * D, 2], [1, D]])
                        ).then_inc(S["ld"], 16)
            s.wait_ge(S["ld"], 16 * 12)
            for p in range(NP):
                qr, h = divmod(p, 2)
                r0 = h * 128
                qs = QSL[p % 2]
                if qslot_use[p] >= 2:
                    s.wait_ge(qs, 32 * (qslot_use[p] - 1))
                if p >= 2:
                    s.wait_ge(S["Sf"], kt_base[p - 1])
                s.dma_start(out=qc_h[:, (p % 2) * 512:(p % 2 + 1) * 512],
                            in_=qT_hi[r0:r0 + 128, qr * 512:(qr + 1) * 512]
                            ).then_inc(qs, 16)
                s.dma_start(out=qc_l[:, (p % 2) * 512:(p % 2 + 1) * 512],
                            in_=qT_lo[r0:r0 + 128, qr * 512:(qr + 1) * 512]
                            ).then_inc(qs, 16)
                for ck in range(qr + 1):
                    ks = KSL[ck % 2]
                    if kslot_use[(p, ck)] >= 2:
                        s.wait_ge(ks, 64 * (kslot_use[(p, ck)] - 1))
                    if kslot_prev_end[(p, ck)] is not None:
                        s.wait_ge(S["Vf"], kslot_prev_end[(p, ck)])
                    c2 = ck % 2
                    s.dma_start(out=kc_h[:, c2 * 512:(c2 + 1) * 512],
                                in_=kT_hi[r0:r0 + 128, ck * 512:(ck + 1) * 512]
                                ).then_inc(ks, 16)
                    s.dma_start(out=kc_l[:, c2 * 512:(c2 + 1) * 512],
                                in_=kT_lo[r0:r0 + 128, ck * 512:(ck + 1) * 512]
                                ).then_inc(ks, 16)
                    vap = [[FOWN, 128], [128 * FOWN, 4], [1, 128]]
                    voff = ck * 512 * FOWN + r0
                    s.dma_start(out=vc_h[:, c2 * 512:(c2 + 1) * 512],
                                in_=bass.AP(v_sh.tensor, voff, list(vap))
                                ).then_inc(ks, 16)
                    s.dma_start(out=vc_l[:, c2 * 512:(c2 + 1) * 512],
                                in_=bass.AP(v_sl.tensor, voff, list(vap))
                                ).then_inc(ks, 16)

        LD2 = 16 * 12
        @b2.vector
        def _(v):
            v = _AD(v)
            v.wait_ge(S["ld"], LD2)
            v.tensor_copy(wp_h[:, :], wpb[:, :])
            v.tensor_sub(wp_l[:, :], wpb[:, :], wp_h[:, :]).then_inc(S["rr"], 1)
            for p in range(NP):
                qr, h = divmod(p, 2)
                KT = kt_cnt[p]
                for j in range(KT):
                    g = kt_base[p] + j
                    g2 = g % 2
                    v.wait_ge(S["Pf"], g + 1)
                    if g >= 2:
                        v.wait_ge(S["Vf"], g - 1)
                    if j >= 4 * qr:
                        v.wait_ge(S["ld"], LD2)
                        jj = j - 4 * qr
                        v.tensor_mul(P32[:, g2 * 512:(g2 + 1) * 512],
                                     P32[:, g2 * 512:(g2 + 1) * 512],
                                     masks[:, jj * 512:(jj + 1) * 512])
                    v.tensor_copy(Ph[:, g2 * 512:(g2 + 1) * 512],
                                  P32[:, g2 * 512:(g2 + 1) * 512])
                    v.tensor_sub(Pl[:, g2 * 512:(g2 + 1) * 512],
                                 P32[:, g2 * 512:(g2 + 1) * 512],
                                 Ph[:, g2 * 512:(g2 + 1) * 512]).then_inc(S["Pv"], 1)
                h5 = h * 512
                if p >= 2:
                    v.wait_ge(S["rb"], p - 1)
                v.wait_ge(S["Of"], p + 1)
                v.reciprocal(r32[:, h5:h5 + 512], den[:, :]).then_inc(S["dn"], 1)
                v.tensor_copy(r_h[:, h5:h5 + 512], r32[:, h5:h5 + 512])
                v.tensor_sub(r_l[:, h5:h5 + 512], r32[:, h5:h5 + 512],
                             r_h[:, h5:h5 + 512]).then_inc(S["rr"], 1)
                v.wait_ge(S["rc"], p + 1)
                if qr >= 1:
                    v.wait_ge(S["pj"], 16 * qr)
                v.tensor_mul(On32[:, h5:h5 + 512], Opb[h][:, :],
                             rb_sb[:, h5:h5 + 512]).then_inc(S["no"], 1)
                v.wait_ge(S["oh"], p + 1)
                v.tensor_sub(Ol[:, h5:h5 + 512], On32[:, h5:h5 + 512],
                             Oh[:, h5:h5 + 512]).then_inc(S["no"], 1)
                if h == 1:
                    for f in range(16):
                        pf = 16 * qr + f
                        v.wait_ge(S["pj"], pf + 1)
                        if pf >= 2:
                            v.wait_ge(S["pd"], 16 * (pf - 1))
                        v.tensor_copy(pj_sb[:, (pf % 2) * 512:(pf % 2 + 1) * 512],
                                      pjb[pf % 2][:, :]).then_inc(S["pe"], 1)

        @b2.scalar
        def _(sc):
            sc = _AD(sc)
            for p in range(NP):
                qr, h = divmod(p, 2)
                KT = kt_cnt[p]
                for j in range(KT):
                    g = kt_base[p] + j
                    g2 = g % 2
                    sc.wait_ge(S["Sf"], g + 1)
                    if g >= 2:
                        sc.wait_ge(S["Pv"], g - 1)
                    sc.activation(P32[:, g2 * 512:(g2 + 1) * 512],
                                  spb[g2][:, :], AF.Exp,
                                  bias=ESHIFT).then_inc(S["Pf"], 1)
                h5 = h * 512
                if p >= 2:
                    sc.wait_ge(S["no"], 2 * (p - 2) + 1)
                sc.wait_ge(S["rb"], p + 1)
                sc.copy(rb_sb[:, h5:h5 + 512], pjb[h][:, :]).then_inc(S["rc"], 1)
                sc.wait_ge(S["no"], 2 * p + 1)
                sc.copy(Oh[:, h5:h5 + 512],
                        On32[:, h5:h5 + 512]).then_inc(S["oh"], 1)

        @b2.tensor
        def _(t_):
            t_ = _AD(t_)
            for p in range(NP):
                qr, h = divmod(p, 2)
                KT = kt_cnt[p]

                def s_tile(j):
                    g = kt_base[p] + j
                    g2 = g % 2
                    ck, kk = divmod(j, 4)
                    c2 = ck % 2
                    t_.wait_ge(QSL[p % 2], 32 * qslot_use[p])
                    t_.wait_ge(KSL[ck % 2], 64 * kslot_use[(p, ck)])
                    if g >= 2:
                        t_.wait_ge(S["Pf"], g - 1)
                    kh = kc_h[:, c2 * 512 + kk * 128:c2 * 512 + (kk + 1) * 128]
                    kl = kc_l[:, c2 * 512 + kk * 128:c2 * 512 + (kk + 1) * 128]
                    qh = qc_h[:, (p % 2) * 512:(p % 2 + 1) * 512]
                    ql = qc_l[:, (p % 2) * 512:(p % 2 + 1) * 512]
                    t_.matmul(spb[g2][:, :], kh, qh, start=True, stop=False)
                    t_.matmul(spb[g2][:, :], kl, qh, start=False, stop=False)
                    t_.matmul(spb[g2][:, :], kh, ql, start=False,
                              stop=True).then_inc(S["Sf"], 1)

                def pv_tile(j):
                    g = kt_base[p] + j
                    g2 = g % 2
                    ck, kk = divmod(j, 4)
                    c2 = ck % 2
                    t_.wait_ge(S["Pv"], g + 1)
                    first = (j == 0)
                    last = (j == KT - 1)
                    if first:
                        t_.wait_ge(S["no"], max(0, 2 * p - 3))
                        if p >= 1:
                            t_.wait_ge(S["dn"], p)
                    t_.wait_ge(KSL[ck % 2], 64 * kslot_use[(p, ck)])
                    vh = vc_h[:, c2 * 512 + kk * 128:c2 * 512 + (kk + 1) * 128]
                    vl = vc_l[:, c2 * 512 + kk * 128:c2 * 512 + (kk + 1) * 128]
                    ph = Ph[:, g2 * 512:(g2 + 1) * 512]
                    pl = Pl[:, g2 * 512:(g2 + 1) * 512]
                    t_.matmul(Opb[h][:, :], vh, ph, start=first, stop=False)
                    t_.matmul(Opb[h][:, :], vh, pl, start=False, stop=False)
                    t_.matmul(Opb[h][:, :], vl, ph, start=False, stop=last
                              ).then_inc(S["Vf"], 1)
                    t_.matmul(den[:, :], ones_c[:, :], ph, start=first, stop=False)
                    mm = t_.matmul(den[:, :], ones_c[:, :], pl, start=False,
                                   stop=last)
                    if last:
                        mm.then_inc(S["Of"], 1)

                for j in range(KT):
                    s_tile(j)
                    if j > 0:
                        pv_tile(j - 1)
                pv_tile(KT - 1)
                t_.wait_ge(S["rr"], p + 2)
                if qr >= 1:
                    t_.wait_ge(S["pe"], 16 * qr)
                h5 = h * 512
                t_.matmul(pjb[h][:, :], ones_k1[:, :], r_h[:, h5:h5 + 512],
                          start=True, stop=False)
                t_.matmul(pjb[h][:, :], ones_k1[:, :], r_l[:, h5:h5 + 512],
                          start=False, stop=True).then_inc(S["rb"], 1)
                if h == 1:
                    for f in range(16):
                        tt, dg = divmod(f, 4)
                        pf = 16 * qr + f
                        t_.wait_ge(S["no"], 2 * p + 2)
                        if f >= 2:
                            t_.wait_ge(S["pe"], pf - 1)
                        else:
                            t_.wait_ge(S["rc"], p + 1)
                        ps = pjb[pf % 2]
                        for ch in range(2):
                            oh = Oh[:, ch * 512 + tt * 128:ch * 512 + (tt + 1) * 128]
                            ol = Ol[:, ch * 512 + tt * 128:ch * 512 + (tt + 1) * 128]
                            mh = wp_h[:, ch * D + dg * 512:ch * D + (dg + 1) * 512]
                            ml = wp_l[:, ch * D + dg * 512:ch * D + (dg + 1) * 512]
                            t_.matmul(ps[:, :], oh, mh, start=(ch == 0), stop=False)
                            t_.matmul(ps[:, :], ol, mh, start=False, stop=False)
                            mmp = t_.matmul(ps[:, :], oh, ml, start=False,
                                            stop=(ch == 1))
                            if ch == 1:
                                mmp.then_inc(S["pj"], 1)

        @b2.gpsimd
        def _(g):
            g = _AD(g)
            for qr in range(QR):
                for f in range(16):
                    tt, dg = divmod(f, 4)
                    pf = 16 * qr + f
                    if pf >= 1:
                        g.wait_ge(S["pd"], 16 * pf)
                    g.wait_ge(S["pe"], pf + 1)
                    g.dma_start(
                        out=proj_part[qr * 512 + tt * 128:qr * 512 + (tt + 1) * 128,
                                      dg * 512:(dg + 1) * 512],
                        in_=pj_sb[:, (pf % 2) * 512:(pf % 2 + 1) * 512]
                    ).then_inc(S["pd"], 16)
            g.wait_ge(S["pd"], 16 * 16 * QR)
            g.wait_ge(S["cc"], 3)
            g.collective_compute(
                "ReduceScatter", OP.add, replica_groups=[list(range(NCORE))],
                ins=[proj_part.tensor.ap().opt()],
                outs=[rs_out.tensor.ap().opt()]).then_inc(S["cc"], 1)


def _phase3(nc, S, env):
    T, DFF = env["T"], env["DFF"]
    TOK, TOKT = env["TOK"], env["TOKT"]
    NFT, NSLAB = env["NFT"], env["NSLAB"]
    TN, NTH = env["TN"], env["NTH"]
    W1S, W2S = env["W1S"], env["W2S"]
    x_own, b1_in, bp_in, b2_in = env["x_own"], env["b1_in"], env["bp_in"], env["b2_in"]
    wq1_all, wq2_all = env["wq1_all"], env["wq2_all"]
    ffn_acc, rs_out, out_own = env["ffn_acc"], env["rs_out"], env["out_own"]
    ident = env["ident"]
    phases = env["phases"]
    FPS = 4
    NFILL = TOKT * 4

    with ExitStack() as _esx:
        xnT_h = _esx.enter_context(nc.sbuf_tensor("xnT_h", [128, 16 * TOK], BF16))
        xnT_l = _esx.enter_context(nc.sbuf_tensor("xnT_l", [128, 16 * TOK], BF16))
        b1_sb = _esx.enter_context(nc.sbuf_tensor("b1_sb", [128, NFT], F32))
        with ExitStack() as _es:
            xg3 = _es.enter_context(nc.sbuf_tensor("xg3", [128, 3 * D], F32))
            sqb3 = _es.enter_context(nc.sbuf_tensor("sqb3", [128, D], F32))
            xn33 = _es.enter_context(nc.sbuf_tensor("xn33", [128, 2 * D], F32))
            xnh3 = _es.enter_context(nc.sbuf_tensor("xnh3", [128, 2 * D], BF16))
            xnl3 = _es.enter_context(nc.sbuf_tensor("xnl3", [128, 2 * D], BF16))
            sm3 = _es.enter_context(nc.sbuf_tensor("sm3", [128, 16], F32))
            tq0 = _es.enter_context(nc.psum_tensor("tq0", [128, 1024], BF16))
            tq1 = _es.enter_context(nc.psum_tensor("tq1", [128, 1024], BF16))
            b3l = _es.enter_context(nc.Block())
            tqb = [tq0, tq1]
            @b3l.sync
            def _(s):
                s = _AD(s)
                s.dma_start(out=b1_sb[:, :], in_=b1_in[:, :]).then_inc(S["xo"], 16)
                for t in range(TOKT):
                    if t >= 1:
                        s.wait_ge(S["xo"], 16 * (t + 1))
                    if t >= 3:
                        s.wait_ge(S["la"], t - 2)
                        s.wait_ge(S["ldd"], t - 2)
                    s.dma_start(out=xg3[:, (t % 3) * D:(t % 3 + 1) * D],
                                in_=x_own[t * 128:(t + 1) * 128, :]
                                ).then_inc(S["xo"], 16)

            @b3l.scalar
            def _(sc):
                sc = _AD(sc)
                for i in range(TOKT + 4):
                    t = i
                    if t < TOKT:
                        sc.wait_ge(S["xo"], 16 * (t + 2))
                        if t >= 2:
                            sc.wait_ge(S["lb"], t - 1)
                        sc.activation(sqb3[:, :], xg3[:, (t % 3) * D:(t % 3 + 1) * D],
                                      AF.Identity, bias=f32(0.0),
                                      accum_out=sm3[:, t % 2:t % 2 + 1])
                        sc.activation(sqb3[:, :], xg3[:, (t % 3) * D:(t % 3 + 1) * D],
                                      AF.Square, bias=f32(0.0),
                                      accum_out=sm3[:, 2 + t % 2:3 + t % 2]
                                      ).then_inc(S["la"], 1)
                    t = i - 1
                    if 0 <= t < TOKT:
                        sc.wait_ge(S["lb"], t + 1)
                        if t >= 2:
                            sc.wait_ge(S["ldd"], t - 1)
                        sc.activation(sm3[:, 8 + t % 2:9 + t % 2],
                                      sm3[:, 6 + t % 2:7 + t % 2],
                                      AF.Sqrt, bias=f32(0.0)).then_inc(S["lc"], 1)
                    t = i - 2
                    if 0 <= t < TOKT:
                        sc.wait_ge(S["ldd"], t + 1)
                        if t >= 2:
                            sc.wait_ge(S["t3"], 4 * (t - 1))
                        sc.copy(xnh3[:, (t % 2) * D:(t % 2 + 1) * D],
                                xn33[:, (t % 2) * D:(t % 2 + 1) * D]
                                ).then_inc(S["le"], 1)
                    t = i - 3
                    if 0 <= t < TOKT:
                        for b4 in range(4):
                            fl = 4 * t + b4
                            sc.wait_ge(S["t3"], fl + 1)
                            k0 = b4 * 4
                            dsth = bass.AP(xnT_h, k0 * TOK + t * 128,
                                           [[16 * TOK, 128], [TOK, 4], [1, 128]])
                            dstl = bass.AP(xnT_l, k0 * TOK + t * 128,
                                           [[16 * TOK, 128], [TOK, 4], [1, 128]])
                            src = tqb[fl % 2]
                            sph = bass.AP(src, 0, [[1024, 128], [256, 4], [1, 128]])
                            spl = bass.AP(src, 128, [[1024, 128], [256, 4], [1, 128]])
                            sc.copy(dsth, sph)
                            sc.copy(dstl, spl).then_inc(S["e3"], 1)

            @b3l.vector
            def _(v):
                v = _AD(v)
                for i in range(TOKT + 4):
                    t = i
                    if t < TOKT:
                        v.wait_ge(S["la"], t + 1)
                        if t >= 2:
                            v.wait_ge(S["lc"], t - 1)
                        a, b_, c, dcol = t % 2, 2 + t % 2, 4 + t % 2, 6 + t % 2
                        v.tensor_scalar_mul(sm3[:, c:c + 1], sm3[:, a:a + 1],
                                            -1.0 / D)
                        v.tensor_scalar_mul(sm3[:, dcol:dcol + 1],
                                            sm3[:, b_:b_ + 1], 1.0 / D)
                        v.tensor_mul(sm3[:, 12:13], sm3[:, c:c + 1], sm3[:, c:c + 1])
                        v.tensor_sub(sm3[:, dcol:dcol + 1], sm3[:, dcol:dcol + 1],
                                     sm3[:, 12:13])
                        v.tensor_scalar_add(sm3[:, dcol:dcol + 1],
                                            sm3[:, dcol:dcol + 1],
                                            float(EPS)).then_inc(S["lb"], 1)
                    t = i - 1
                    if 0 <= t < TOKT:
                        v.wait_ge(S["lc"], t + 1)
                        if t >= 2:
                            v.wait_ge(S["le"], t - 1)
                        r, c, dcol = 10 + t % 2, 4 + t % 2, 6 + t % 2
                        v.reciprocal(sm3[:, r:r + 1], sm3[:, 8 + t % 2:9 + t % 2])
                        v.tensor_mul(sm3[:, 12:13], sm3[:, r:r + 1], sm3[:, r:r + 1])
                        v.tensor_mul(sm3[:, 12:13], sm3[:, 12:13],
                                     sm3[:, dcol:dcol + 1])
                        v.tensor_scalar(sm3[:, 12:13], sm3[:, 12:13], -0.5,
                                        1.5, OP.mult, OP.add)
                        v.tensor_mul(sm3[:, r:r + 1], sm3[:, r:r + 1], sm3[:, 12:13])
                        v.tensor_mul(sm3[:, 13 + t % 2:14 + t % 2],
                                     sm3[:, c:c + 1], sm3[:, r:r + 1])
                        v.scalar_tensor_tensor(
                            xn33[:, (t % 2) * D:(t % 2 + 1) * D],
                            xg3[:, (t % 3) * D:(t % 3 + 1) * D],
                            sm3[:, r:r + 1],
                            sm3[:, 13 + t % 2:14 + t % 2].broadcast_to([128, D]),
                            OP.mult, OP.add).then_inc(S["ldd"], 1)
                    t = i - 2
                    if 0 <= t < TOKT:
                        v.wait_ge(S["le"], t + 1)
                        if t >= 2:
                            v.wait_ge(S["t3"], 4 * (t - 1))
                        v.tensor_sub(xnl3[:, (t % 2) * D:(t % 2 + 1) * D],
                                     xn33[:, (t % 2) * D:(t % 2 + 1) * D],
                                     xnh3[:, (t % 2) * D:(t % 2 + 1) * D]
                                     ).then_inc(S["lf"], 1)

            @b3l.tensor
            def _(t_):
                t_ = _AD(t_)
                for t in range(TOKT):
                    t_.wait_ge(S["lf"], t + 1)
                    for b4 in range(4):
                        fl = 4 * t + b4
                        if fl >= 2:
                            t_.wait_ge(S["e3"], fl - 1)
                        for ii in range(8):
                            idx = b4 * 8 + ii
                            k, hl = divmod(idx, 2)
                            src = (xnh3 if hl == 0 else xnl3)
                            mm = t_.transpose(
                                tqb[fl % 2][:, ii * 128:(ii + 1) * 128],
                                src[:, (t % 2) * D + k * 128:(t % 2) * D + (k + 1) * 128],
                                ident[:, :])
                        mm.then_inc(S["t3"], 1)

        E3TOT = 4 * TOKT
        with ExitStack() as _es:
            w1s = _es.enter_context(nc.sbuf_tensor("w1s", [128, 2 * 16 * 512], BF16))
            w2s = _es.enter_context(nc.sbuf_tensor("w2s", [128, 8 * D], BF16))
            hsl_h = _es.enter_context(nc.sbuf_tensor("hsl", [128, 2 * FPS * TOK], BF16))
            hsl_l = _es.enter_context(nc.sbuf_tensor("hsll", [128, 2 * FPS * TOK], BF16))
            h32 = _es.enter_context(nc.sbuf_tensor("h32", [128, 2 * 512], F32))
            f2sb = _es.enter_context(nc.sbuf_tensor("f2sb", [128, 2 * D], F32))
            ff0 = _es.enter_context(nc.psum_tensor("ff0", [128, 512], F32))
            ff1 = _es.enter_context(nc.psum_tensor("ff1", [128, 512], F32))
            ff2 = _es.enter_context(nc.psum_tensor("ff2", [128, 512], F32))
            ff3 = _es.enter_context(nc.psum_tensor("ff3", [128, 512], F32))
            ff4 = _es.enter_context(nc.psum_tensor("ff4", [128, 512], F32))
            ff5 = _es.enter_context(nc.psum_tensor("ff5", [128, 512], F32))
            ff6 = _es.enter_context(nc.psum_tensor("ff6", [128, 512], F32))
            ff7 = _es.enter_context(nc.psum_tensor("ff7", [128, 512], F32))
            b3a = _es.enter_context(nc.Block())
            ffb = [ff0, ff1, ff2, ff3, ff4, ff5, ff6, ff7]

            W1SL = [S["w1"], S["w1b"]]
            W2SL = [S["w2"], S["w2b"]]

            @b3a.sync
            def _(s):
                s = _AD(s)
                s.wait_ge(S["cc"], 3)
                for sl in range(NSLAB):
                    if sl >= 2:
                        s.wait_ge(W1SL[sl % 2], 16 * 16 * (sl // 2))
                        s.wait_ge(S["f1"], 4 * NTH * (sl - 1))
                    for k in range(16):
                        if W1S >= 512:
                            blk, off = divmod(sl * 512, W1S)
                            ap1 = bass.AP(wq1_all.tensor,
                                          blk * D * W1S + (k * 128) * W1S + off,
                                          [[W1S, 128], [1, 512]])
                        else:
                            nb = 512 // W1S
                            blk = sl * 512 // W1S
                            ap1 = bass.AP(wq1_all.tensor,
                                          blk * D * W1S + (k * 128) * W1S,
                                          [[W1S, 128], [D * W1S, nb], [1, W1S]])
                        s.dma_start(
                            out=w1s[:, (sl % 2) * (16 * 512) + k * 512:
                                    (sl % 2) * (16 * 512) + (k + 1) * 512],
                            in_=ap1).then_inc(W1SL[sl % 2], 16)
                    if sl >= 2:
                        s.wait_ge(W2SL[sl % 2], 16 * 4 * (sl // 2))
                        s.wait_ge(S["f2"], NFILL * (sl - 1))
                    for jj in range(FPS):
                        fc = FPS * sl + jj
                        ap2 = bass.AP(wq2_all.tensor, (fc * 128) * W2S,
                                      [[W2S, 128], [DFF * W2S, 8], [1, W2S]])
                        s.dma_start(
                            out=w2s[:, ((sl % 2) * 4 + jj) * D:((sl % 2) * 4 + jj + 1) * D],
                            in_=ap2).then_inc(W2SL[sl % 2], 16)

            def _ffn2_slab(t_, sl):
                hb = (sl % 2) * FPS * TOK
                for fi in range(NFILL):
                    tt, dg = divmod(fi, 4)
                    ff = NFILL * sl + fi
                    t_.wait_ge(S["h"], 2 * FPS * NTH * (sl + 1))
                    t_.wait_ge(S["g1"], min(4 * NTH * (sl + 2), 4 * NTH * NSLAB))
                    if ff >= 2:
                        t_.wait_ge(S["fe"], ff - 1)
                    ps = ffb[ff % 2]
                    for jj in range(FPS):
                        fc = FPS * sl + jj
                        t_.wait_ge(W2SL[sl % 2], 16 * 4 * (sl // 2 + 1))
                        th, trem = divmod(tt * 128, TN)
                        sh = hsl_h[:, hb + (jj * NTH + th) * TN + trem:
                                   hb + (jj * NTH + th) * TN + trem + 128]
                        sll = hsl_l[:, hb + (jj * NTH + th) * TN + trem:
                                    hb + (jj * NTH + th) * TN + trem + 128]
                        mv = w2s[:, ((sl % 2) * 4 + jj) * D + dg * 512:
                                 ((sl % 2) * 4 + jj) * D + (dg + 1) * 512]
                        t_.matmul(ps[:, :], sh, mv, start=(jj == 0), stop=False)
                        mm2 = t_.matmul(ps[:, :], sll, mv, start=False,
                                        stop=(jj == FPS - 1))
                        if jj == FPS - 1:
                            mm2.then_inc(S["f2"], 1)

            @b3a.tensor
            def _(t_):
                t_ = _AD(t_)
                t_.wait_ge(S["e3"], E3TOT)
                for sl in range(NSLAB):
                    for th in range(NTH):
                        for k in range(16):
                            t_.wait_ge(W1SL[sl % 2], 16 * 16 * (sl // 2 + 1))
                            for jj in range(4):
                                bk = ((sl * NTH + th) % 2) * 4 + jj
                                prev = (sl * NTH + th - 2)
                                if prev >= 0:
                                    t_.wait_ge(S["g1"], 4 * prev + jj + 1)
                                if sl >= 2 and th == 0 and bk <= 1:
                                    t_.wait_ge(S["fe"], NFILL * (sl - 1))
                                st_h = w1s[:, (sl % 2) * (16 * 512) + k * 512 + jj * 128:
                                           (sl % 2) * (16 * 512) + k * 512 + (jj + 1) * 128]
                                xh = xnT_h[:, k * TOK + th * TN:k * TOK + (th + 1) * TN]
                                xl = xnT_l[:, k * TOK + th * TN:k * TOK + (th + 1) * TN]
                                t_.matmul(ffb[bk][:, 0:TN], st_h, xh,
                                          start=(k == 0), stop=False)
                                mm = t_.matmul(ffb[bk][:, 0:TN], st_h, xl,
                                               start=False, stop=(k == 15))
                                if k == 15:
                                    mm.then_inc(S["f1"], 1)
                    if sl >= 1:
                        _ffn2_slab(t_, sl - 1)
                _ffn2_slab(t_, NSLAB - 1)

            @b3a.scalar
            def _(sc):
                sc = _AD(sc)
                sc.wait_ge(S["xo"], 32)
                for sl in range(NSLAB):
                    for th in range(NTH):
                        for jj in range(4):
                            gi = 4 * (sl * NTH + th) + jj
                            bk = ((sl * NTH + th) % 2) * 4 + jj
                            sc.wait_ge(S["f1"], gi + 1)
                            if gi >= 2:
                                sc.wait_ge(S["h"], 2 * (gi - 1))
                            ft = 4 * sl + jj
                            sc.activation(h32[:, (gi % 2) * 512:(gi % 2) * 512 + TN],
                                          ffb[bk][:, 0:TN], AF.Gelu,
                                          bias=b1_sb[:, ft:ft + 1]
                                          ).then_inc(S["g1"], 1)

            F2D = [S["f2d%d" % i] for i in range(8)]

            def _f2evacs(v, sl):
                for fi in range(NFILL):
                    tt, dg = divmod(fi, 4)
                    ff = NFILL * sl + fi
                    gidx = ff // 4
                    v.wait_ge(S["f2"], ff + 1)
                    if gidx >= 2:
                        pg = gidx - 2
                        psl, ptt = divmod(pg, TOKT)
                        v.wait_ge(F2D[ptt], 16 * (psl + 1))
                    v.tensor_copy(
                        f2sb[:, (gidx % 2) * D + dg * 512:(gidx % 2) * D + (dg + 1) * 512],
                        ffb[ff % 2][:, :]).then_inc(S["fe"], 1)

            @b3a.vector
            def _(v):
                v = _AD(v)
                for sl in range(NSLAB):
                    hb = (sl % 2) * FPS * TOK
                    for th in range(NTH):
                        for jj in range(4):
                            gi = 4 * (sl * NTH + th) + jj
                            if sl >= 2 and th == 0 and jj == 0:
                                v.wait_ge(S["f2"], NFILL * (sl - 1))
                            v.wait_ge(S["g1"], gi + 1)
                            dsth = hsl_h[:, hb + (jj * NTH + th) * TN:
                                         hb + (jj * NTH + th + 1) * TN]
                            dstl = hsl_l[:, hb + (jj * NTH + th) * TN:
                                         hb + (jj * NTH + th + 1) * TN]
                            v.tensor_copy(dsth, h32[:, (gi % 2) * 512:(gi % 2) * 512 + TN])
                            v.tensor_sub(dstl, h32[:, (gi % 2) * 512:(gi % 2) * 512 + TN],
                                         dsth).then_inc(S["h"], 2)
                    if sl >= 1:
                        _f2evacs(v, sl - 1)
                _f2evacs(v, NSLAB - 1)

            @b3a.gpsimd
            def _(g):
                g = _AD(g)
                for sl in range(NSLAB):
                    for tt in range(TOKT):
                        gidx = sl * TOKT + tt
                        if sl >= 1:
                            g.wait_ge(F2D[tt], 16 * sl)
                        g.wait_ge(S["fe"], 4 * (gidx + 1))
                        g.dma_start(
                            out=ffn_acc[tt * 128:(tt + 1) * 128, :],
                            in_=f2sb[:, (gidx % 2) * D:(gidx % 2 + 1) * D],
                            accum_op=(OP.add if sl >= 1 else OP.bypass)
                        ).then_inc(F2D[tt], 16)
                for tt in range(TOKT):
                    g.wait_ge(F2D[tt], 16 * NSLAB)

        with ExitStack() as _es:
            rs_sb = _es.enter_context(nc.sbuf_tensor("rs_sb", [128, 2 * D], F32))
            xf_sb = _es.enter_context(nc.sbuf_tensor("xf_sb", [128, 2 * D], F32))
            fa_sb = _es.enter_context(nc.sbuf_tensor("fa_sb", [128, 2 * D], F32))
            bp_sb = _es.enter_context(nc.sbuf_tensor("bp_sb", [128, D], F32))
            b22_sb = _es.enter_context(nc.sbuf_tensor("b22_sb", [128, D], F32))
            cmb = _es.enter_context(nc.sbuf_tensor("cmb", [128, 2 * D], F32))
            b3b = _es.enter_context(nc.Block())
            R3SL = [S["r3"], S["r3b"]]

            @b3b.sync
            def _(s):
                s = _AD(s)
                s.wait_ge(S["cc"], 4)
                s.wait_ge(S["ld"], 16 * 12)
                s.dma_start(out=bp_sb[:, :], in_=bp_in[:, :]).then_inc(S["ld"], 16)
                s.dma_start(out=b22_sb[:, :], in_=b2_in[:, :]).then_inc(S["ld"], 16)
                s.wait_ge(S["ld"], 16 * 14)
                for _tt in range(TOKT):
                    s.wait_ge(S["f2d%d" % _tt], 16 * NSLAB)
                for t in range(TOKT):
                    if t >= 2:
                        s.wait_ge(R3SL[t % 2], 48 * (t // 2))
                        s.wait_ge(S["o3"], t - 1)
                    s.dma_start(out=rs_sb[:, (t % 2) * D:(t % 2 + 1) * D],
                                in_=rs_out[t * 128:(t + 1) * 128, :]).then_inc(R3SL[t % 2], 16)
                    s.dma_start(out=xf_sb[:, (t % 2) * D:(t % 2 + 1) * D],
                                in_=x_own[t * 128:(t + 1) * 128, :]).then_inc(R3SL[t % 2], 16)
                    s.dma_start(out=fa_sb[:, (t % 2) * D:(t % 2 + 1) * D],
                                in_=ffn_acc[t * 128:(t + 1) * 128, :]).then_inc(R3SL[t % 2], 16)

            @b3b.vector
            def _(v):
                v = _AD(v)
                v.wait_ge(S["ld"], 16 * 14)
                v.tensor_add(bp_sb[:, :], bp_sb[:, :], b22_sb[:, :])
                for t in range(TOKT):
                    v.wait_ge(R3SL[t % 2], 48 * (t // 2 + 1))
                    if t >= 2:
                        v.wait_ge(S["stq"], 16 * (t - 1))
                    sl_ = slice((t % 2) * D, (t % 2 + 1) * D)
                    v.tensor_add(cmb[:, sl_], rs_sb[:, sl_], fa_sb[:, sl_])
                    v.tensor_add(cmb[:, sl_], cmb[:, sl_], xf_sb[:, sl_])
                    v.tensor_add(cmb[:, sl_], cmb[:, sl_],
                                 bp_sb[:, :]).then_inc(S["o3"], 1)

            @b3b.gpsimd
            def _(g):
                g = _AD(g)
                for t in range(TOKT):
                    g.wait_ge(S["o3"], t + 1)
                    g.dma_start(out=out_own[t * 128:(t + 1) * 128, :],
                                in_=cmb[:, (t % 2) * D:(t % 2 + 1) * D]
                                ).then_inc(S["stq"], 16)
                g.wait_ge(S["stq"], 16 * TOKT)


def stage_inputs(x, W_qkv, b_qkv, W_proj, b_proj, w1, b1, w2, b2,
                 T=8192, DFF=8192):
    import ml_dtypes
    TOK = T // NCORE
    NFT = DFF // 128
    x2 = np.ascontiguousarray(np.asarray(x, dtype=np.float32).reshape(T, D))
    W_qkv = np.asarray(W_qkv, np.float32)
    b_qkv = np.asarray(b_qkv, np.float32)
    W_proj = np.asarray(W_proj, np.float32)
    b_proj = np.asarray(b_proj, np.float32)
    w1 = np.asarray(w1, np.float32)
    b1 = np.asarray(b1, np.float32)
    w2 = np.asarray(w2, np.float32)
    b2 = np.asarray(b2, np.float32)
    NQ = W_qkv.shape[1] // 3
    ident = np.eye(128, dtype=ml_dtypes.bfloat16)
    masks = np.zeros((4, 128, 512), np.float32)
    for j in range(4):
        kk = np.arange(128)[:, None] + j * 128
        qq = np.arange(512)[None, :]
        masks[j] = (qq >= kk).astype(np.float32)
    in_maps = []
    for c in range(NCORE):
        fo = c * FOWN
        d = {
            "x_full": x2,
            "x_own": np.ascontiguousarray(x2[c * TOK:(c + 1) * TOK]),
            "wq_in": np.ascontiguousarray(W_qkv[:, fo:fo + FOWN]),
            "wk_in": np.ascontiguousarray(W_qkv[:, NQ + fo:NQ + fo + FOWN]),
            "wv_in": np.ascontiguousarray(W_qkv[:, 2 * NQ + fo:2 * NQ + fo + FOWN]),
            "bq_in": np.ascontiguousarray(
                b_qkv[fo:fo + FOWN].reshape(2, 128).T),
            "bk_in": np.ascontiguousarray(
                b_qkv[NQ + fo:NQ + fo + FOWN].reshape(2, 128).T),
            "bv_in": np.ascontiguousarray(np.broadcast_to(
                b_qkv[2 * NQ + fo:2 * NQ + fo + FOWN][None, :], (128, FOWN))),
            "wp_in": np.ascontiguousarray(W_proj[fo:fo + FOWN, :]),
            "bp_in": np.ascontiguousarray(
                np.broadcast_to(b_proj[None, :], (128, D))),
            "b2_in": np.ascontiguousarray(
                np.broadcast_to(b2[None, :], (128, D))),
            "b1_in": np.ascontiguousarray(b1.reshape(NFT, 128).T),
            "w1_sh": np.ascontiguousarray(
                w1[c * (DFF // 8):(c + 1) * (DFF // 8), :]),
            "w2_sh": np.ascontiguousarray(
                w2[c * (D // 8):(c + 1) * (D // 8), :]),
            "id_in": ident,
            "mk_in": masks,
        }
        in_maps.append(d)
    return in_maps


def gather_output(results, T=8192):
    TOK = T // NCORE
    out = np.concatenate([np.asarray(r["out_own"]) for r in results], axis=0)
    return out.reshape(1, T, D)


from concourse.bass_utils import run_bass_kernel_spmd

_NC_CACHE = {}


def kernel(L0_attn_x, L0_attn_W_qkv, L0_attn_b_qkv, L0_attn_W_proj,
           L0_attn_b_proj, L0_ffn_w1, L0_ffn_b1, L0_ffn_w2, L0_ffn_b2):
    T, DFF = 8192, 8192
    if "nc" not in _NC_CACHE:
        _NC_CACHE["nc"] = build(T=T, DFF=DFF)
    nc = _NC_CACHE["nc"]
    in_maps = stage_inputs(L0_attn_x, L0_attn_W_qkv, L0_attn_b_qkv,
                           L0_attn_W_proj, L0_attn_b_proj,
                           L0_ffn_w1, L0_ffn_b1, L0_ffn_w2, L0_ffn_b2,
                           T=T, DFF=DFF)
    res = run_bass_kernel_spmd(nc, in_maps, core_ids=list(range(NCORE)))
    return gather_output(res.results, T=T).astype(np.float32)


# version 09
# speedup vs baseline: 1.1997x; 1.0102x over previous
import math
from contextlib import ExitStack

import numpy as np
import concourse.bass as bass
import concourse.mybir as mybir

F32 = mybir.dt.float32
BF16 = mybir.dt.bfloat16
AF = mybir.ActivationFunctionType
OP = mybir.AluOpType
AX = mybir.AxisListType

D = 2048
HD = 128
FOWN = 256
NCORE = 8
EPS = 1e-5
QS = np.float32(1.0 / math.sqrt(HD))
ESHIFT = np.float32(-20.0)
f32 = np.float32




class _AD:
    _OPS = {"memset", "tensor_copy", "tensor_tensor", "tensor_add",
            "tensor_sub", "tensor_mul", "tensor_max", "tensor_scalar",
            "tensor_single_scalar", "tensor_scalar_add", "tensor_scalar_sub",
            "tensor_scalar_mul", "tensor_scalar_max", "tensor_scalar_min",
            "reduce_sum", "reduce_max", "tensor_reduce",
            "scalar_tensor_tensor", "reciprocal", "activation", "copy",
            "select", "transpose"}

    def __init__(self, eng):
        self._e = eng
        self._wmax = {}

    def wait_ge(self, sem, val):
        key = id(sem)
        if val <= self._wmax.get(key, -1):
            return
        self._wmax[key] = val
        self._e.wait_ge(sem, val)

    def __getattr__(self, n):
        f = getattr(self._e, n)
        if n in _AD._OPS:
            def wrapped(*a, **k):
                r = f(*a, **k)
                self._e.drain()
                return r
            return wrapped
        return f


def build(T=8192, DFF=8192, phases=(1, 2, 3), dbg=()):
    TOK = T // NCORE
    NT = T // 128
    NG = NT // 2
    QR = T // 512
    NP = 2 * QR
    NFT = DFF // 128
    NSLAB = DFF // 512
    TOKT = TOK // 128
    TN = min(512, TOK)
    NTH = TOK // TN
    W1S = DFF // 8
    W2S = D // 8
    QW = min(4096, DFF)
    QBW = max(D, QW)
    assert T % 1024 == 0 and DFF % 512 == 0 and TOK % 128 == 0

    nc = bass.Bass("TRN2", target_bir_lowering=False)

    x_full = nc.dram_tensor("x_full", [T, D], F32, kind="ExternalInput").ap()
    x_own = nc.dram_tensor("x_own", [TOK, D], F32, kind="ExternalInput").ap()
    wq_in = nc.dram_tensor("wq_in", [D, FOWN], F32, kind="ExternalInput").ap()
    wk_in = nc.dram_tensor("wk_in", [D, FOWN], F32, kind="ExternalInput").ap()
    wv_in = nc.dram_tensor("wv_in", [D, FOWN], F32, kind="ExternalInput").ap()
    bq_in = nc.dram_tensor("bq_in", [128, 2], F32, kind="ExternalInput").ap()
    bk_in = nc.dram_tensor("bk_in", [128, 2], F32, kind="ExternalInput").ap()
    bv_in = nc.dram_tensor("bv_in", [128, FOWN], F32, kind="ExternalInput").ap()
    wp_in = nc.dram_tensor("wp_in", [FOWN, D], F32, kind="ExternalInput").ap()
    bp_in = nc.dram_tensor("bp_in", [128, D], F32, kind="ExternalInput").ap()
    b2_in = nc.dram_tensor("b2_in", [128, D], F32, kind="ExternalInput").ap()
    b1_in = nc.dram_tensor("b1_in", [128, NFT], F32, kind="ExternalInput").ap()
    w1_sh = nc.dram_tensor("w1_sh", [W1S, D], F32, kind="ExternalInput").ap()
    w2_sh = nc.dram_tensor("w2_sh", [W2S, DFF], F32, kind="ExternalInput").ap()
    id_in = nc.dram_tensor("id_in", [128, 128], BF16, kind="ExternalInput").ap()
    mk_in = nc.dram_tensor("mk_in", [4, 128, 512], F32, kind="ExternalInput").ap()

    out_own = nc.dram_tensor("out_own", [TOK, D], F32, kind="ExternalOutput").ap()

    qT_hi = nc.dram_tensor("qT_hi", [FOWN, T], BF16).ap()
    qT_lo = nc.dram_tensor("qT_lo", [FOWN, T], BF16).ap()
    kT_hi = nc.dram_tensor("kT_hi", [FOWN, T], BF16).ap()
    kT_lo = nc.dram_tensor("kT_lo", [FOWN, T], BF16).ap()
    v_sh = nc.dram_tensor("v_sh", [T, FOWN], BF16).ap()
    v_sl = nc.dram_tensor("v_sl", [T, FOWN], BF16).ap()
    proj_part = nc.dram_tensor("proj_part", [T, D], F32).ap()
    rs_out = nc.dram_tensor("rs_out", [TOK, D], F32).ap()
    ffn_acc = nc.dram_tensor("ffn_acc", [TOK, D], F32).ap()
    sc_part = nc.dram_tensor("sc_part", [128, 2], F32).ap()
    sc_sum = nc.dram_tensor("sc_sum", [128, 2], F32).ap()
    wq1_shd = nc.dram_tensor("wq1_shd", [D, W1S], BF16).ap()
    wq2_shd = nc.dram_tensor("wq2_shd", [DFF, W2S], BF16).ap()
    wq1_all = nc.dram_tensor("wq1_all", [8 * D, W1S], BF16).ap()
    wq2_all = nc.dram_tensor("wq2_all", [8 * DFF, W2S], BF16).ap()

    dbg_map = {"qT_hi": qT_hi, "qT_lo": qT_lo, "kT_hi": kT_hi, "kT_lo": kT_lo,
               "v_sh": v_sh, "v_sl": v_sl, "proj_part": proj_part,
               "rs_out": rs_out, "wq1_all": wq1_all, "wq2_all": wq2_all,
               "sc_sum": sc_sum, "ffn_acc": ffn_acc}
    dbg_outs = []
    for name in dbg:
        src = dbg_map[name]
        dst = nc.dram_tensor("dbg_" + name, list(src.shape), src.dtype,
                             kind="ExternalOutput").ap()
        dbg_outs.append((dst, src))

    qtiles = []
    for i in range(W1S // 128):
        dstT = bass.AP(wq1_shd.tensor, i * 128,
                       [[W1S, 128], [128 * W1S, D // 128], [1, 128]])
        qtiles.append((w1_sh[i * 128:(i + 1) * 128, :], dstT, D, 0))
    for i in range(W2S // 128):
        for hf in range(DFF // QW):
            dstT = bass.AP(wq2_shd.tensor, hf * QW * W2S + i * 128,
                           [[W2S, 128], [128 * W2S, QW // 128], [1, 128]])
            qtiles.append((w2_sh[i * 128:(i + 1) * 128, hf * QW:(hf + 1) * QW],
                           dstT, QW, 1))
    NQT = len(qtiles)
    qe_base = [0]
    for qi in range(NQT):
        qe_base.append(qe_base[-1] + qtiles[qi][2] // 1024)
    qt_base = qe_base

    kt_cnt = [4 * (p // 2 + 1) for p in range(NP)]
    kt_base = [0]
    for p in range(NP):
        kt_base.append(kt_base[-1] + kt_cnt[p])
    NKT = kt_base[-1]
    qk_cnt = [2 + 4 * (p // 2 + 1) for p in range(NP)]
    qk_base = [0]
    for p in range(NP):
        qk_base.append(qk_base[-1] + qk_cnt[p])

    st = ExitStack()
    with st:
        semd = {}
        for n in ["ld", "pq", "pt", "cc", "qd",
                  "x", "a", "b", "c", "d", "e", "f", "tp", "te",
                  "mm", "ev", "hi", "lo", "dq",
                  "qk", "Sf", "Pf", "Pv", "Vf", "Of", "dn", "rr",
                  "rb", "rc", "no", "oh", "pj", "pe", "pd",
                  "xo", "la", "lb", "lc", "ldd", "le", "lf", "t3", "e3",
                  "w1", "w2", "u1", "f1", "g1", "h", "f2", "fe", "f2d",
                  "f2d0", "f2d1", "f2d2", "f2d3", "f2d4", "f2d5", "f2d6", "f2d7",
                  "r3", "r3b", "o3", "stq", "qA", "qB", "kA", "kB", "w1b", "w2b", "qt", "qe",
                  "qla", "qlb", "qsa", "qsb"]:
            semd[n] = st.enter_context(nc.semaphore("s_" + n))
        S = semd
        QLS = [S["qla"], S["qlb"]]
        QSS = [S["qsa"], S["qsb"]]
        QAT = [(NQT + 1) // 2, NQT // 2]

        ident = st.enter_context(nc.sbuf_tensor("ident", [128, 128], BF16))
        ones_c = st.enter_context(nc.sbuf_tensor("ones_c", [128, 1], BF16))
        ones_sq = st.enter_context(nc.sbuf_tensor("ones_sq", [128, 128], BF16))
        ones_k1 = st.enter_context(nc.sbuf_tensor("ones_k1", [1, 128], BF16))
        bqs = st.enter_context(nc.sbuf_tensor("bqs", [128, 2], F32))
        bk_sb = st.enter_context(nc.sbuf_tensor("bk_sb", [128, 2], F32))
        bv_sb = st.enter_context(nc.sbuf_tensor("bv_sb", [128, FOWN], F32))
        thrp = st.enter_context(nc.sbuf_tensor("thrp", [128, 2], F32))
        thrn = st.enter_context(nc.sbuf_tensor("thrn", [128, 2], F32))

        with ExitStack() as _esw:
            wq_h = _esw.enter_context(nc.sbuf_tensor("wq_h", [128, 16 * FOWN], BF16))
            wq_l = _esw.enter_context(nc.sbuf_tensor("wq_l", [128, 16 * FOWN], BF16))
            wk_h = _esw.enter_context(nc.sbuf_tensor("wk_h", [128, 16 * FOWN], BF16))
            wk_l = _esw.enter_context(nc.sbuf_tensor("wk_l", [128, 16 * FOWN], BF16))
            wv_h = _esw.enter_context(nc.sbuf_tensor("wv_h", [128, 16 * FOWN], BF16))
            wv_l = _esw.enter_context(nc.sbuf_tensor("wv_l", [128, 16 * FOWN], BF16))
            with ExitStack() as _es:
                wbig = _es.enter_context(nc.sbuf_tensor("wbig", [128, 4096], F32))
                qbuf = _es.enter_context(nc.sbuf_tensor("qbuf", [128, 2 * QBW], F32))
                qtmp = _es.enter_context(nc.sbuf_tensor("qtmp", [128, QBW], BF16))
                qout = _es.enter_context(nc.sbuf_tensor("qout", [128, 2 * QBW], BF16))
                qstage = _es.enter_context(nc.sbuf_tensor("qstage", [128, 2 * QBW], BF16))
                acc2 = _es.enter_context(nc.sbuf_tensor("acc2", [128, 2], F32))
                rtmp = _es.enter_context(nc.sbuf_tensor("rtmp", [128, 1], F32))
                ac_h = _es.enter_context(nc.sbuf_tensor("ac_h", [128, 2], BF16))
                ac_l = _es.enter_context(nc.sbuf_tensor("ac_l", [128, 2], BF16))
                tpq0 = _es.enter_context(nc.psum_tensor("tpq0", [128, 1024], BF16))
                tpq1 = _es.enter_context(nc.psum_tensor("tpq1", [128, 1024], BF16))
                pp0 = _es.enter_context(nc.psum_tensor("pp0", [128, 512], F32))
                b0 = _es.enter_context(nc.Block())
                @b0.sync
                def _(s):
                    s = _AD(s)
                    s.dma_start(out=ident[:, :], in_=id_in[:, :]).then_inc(S["ld"], 16)
                    s.dma_start(out=bqs[:, :], in_=bq_in[:, :]).then_inc(S["ld"], 16)
                    s.dma_start(out=bk_sb[:, :], in_=bk_in[:, :]).then_inc(S["ld"], 16)
                    s.dma_start(out=bv_sb[:, :], in_=bv_in[:, :]).then_inc(S["ld"], 16)
                    for wi, win in enumerate([wq_in, wk_in, wv_in]):
                        s.wait_ge(S["ld"], 16 * (4 + wi))
                        s.wait_ge(S["pq"], 2 * wi)
                        s.dma_start(
                            out=wbig[:, 0:16 * FOWN],
                            in_=bass.AP(win.tensor, 0,
                                        [[FOWN, 128], [128 * FOWN, 16], [1, FOWN]]),
                        ).then_inc(S["ld"], 16)

                @b0.vector
                def _(v):
                    v = _AD(v)
                    v.memset(ones_c[:, :], 1.0)
                    v.memset(ones_sq[:, :], 1.0)
                    v.memset(ones_k1[:, :], 1.0)
                    v.memset(acc2[:, :], 0.0)
                    v.wait_ge(S["ld"], 16 * 4)
                    v.tensor_scalar_mul(bqs[:, :], bqs[:, :], float(QS))
                    for wi, (dh, dl) in enumerate([(wq_h, wq_l), (wk_h, wk_l),
                                                   (wv_h, wv_l)]):
                        v.wait_ge(S["ld"], 16 * (5 + wi))
                        v.tensor_copy(dh[:, :], wbig[:, 0:16 * FOWN])
                        v.tensor_sub(dl[:, :], wbig[:, 0:16 * FOWN],
                                     dh[:, :]).then_inc(S["pq"], 2)
                    for qi in range(NQT):
                        w, col = qtiles[qi][2], qtiles[qi][3]
                        v.wait_ge(QLS[qi % 2], 16 * (qi // 2 + 1))
                        v.reduce_sum(rtmp[:, :], qbuf[:, (qi % 2) * QBW:(qi % 2) * QBW + w],
                                     axis=AX.X, apply_absolute_value=True)
                        v.tensor_add(acc2[:, col:col + 1], acc2[:, col:col + 1],
                                     rtmp[:, :]).then_inc(S["pq"], 1)
                    v.tensor_copy(ac_h[:, :], acc2[:, :])
                    v.tensor_sub(ac_l[:, :], acc2[:, :],
                                 ac_h[:, :]).then_inc(S["pq"], 1)
                    v.wait_ge(S["qd"], 16 * 2)
                    v.tensor_scalar_mul(thrp[:, :], thrp[:, :],
                                        float(0.7 / (DFF * D)))
                    v.tensor_scalar_mul(thrn[:, :], thrp[:, :],
                                        -1.0).then_inc(S["pq"], 1)
                    for qi in range(NQT):
                        w, col = qtiles[qi][2], qtiles[qi][3]
                        v.wait_ge(QLS[qi % 2], 16 * (QAT[qi % 2] + qi // 2 + 1))
                        if qi >= 2:
                            v.wait_ge(S["qt"], qt_base[qi - 1])
                        qb = qbuf[:, (qi % 2) * QBW:(qi % 2) * QBW + w]
                        qo = qout[:, (qi % 2) * QBW:(qi % 2) * QBW + w]
                        v.tensor_scalar(qtmp[:, 0:w], qb,
                                        thrp[:, col:col + 1], None, OP.is_ge)
                        v.tensor_scalar(qo, qb,
                                        thrn[:, col:col + 1], None, OP.is_le)
                        v.tensor_sub(qo, qtmp[:, 0:w], qo).then_inc(S["pq"], 1)

                @b0.tensor
                def _(t):
                    t = _AD(t)
                    t.wait_ge(S["ld"], 64)
                    t.wait_ge(S["pq"], 6 + NQT + 1)
                    t.matmul(pp0[:, 0:2], ones_sq[:, :], ac_h[:, :],
                             start=True, stop=False)
                    t.matmul(pp0[:, 0:2], ones_sq[:, :], ac_l[:, :],
                             start=False, stop=True).then_inc(S["pt"], 1)
                    fl = 0
                    for qi in range(NQT):
                        w = qtiles[qi][2]
                        t.wait_ge(S["pq"], 6 + NQT + 2 + qi + 1)
                        for f in range(w // 1024):
                            if fl >= 2:
                                t.wait_ge(S["qe"], fl - 1)
                            bank = [tpq0, tpq1][fl % 2]
                            for ii in range(8):
                                k = f * 8 + ii
                                mmq = t.transpose(
                                    bank[:, ii * 128:(ii + 1) * 128],
                                    qout[:, (qi % 2) * QBW + k * 128:
                                         (qi % 2) * QBW + (k + 1) * 128],
                                    ident[:, :])
                            mmq.then_inc(S["qt"], 1)
                            fl += 1

                @b0.scalar
                def _(sc):
                    sc = _AD(sc)
                    sc.wait_ge(S["pt"], 1)
                    sc.copy(acc2[:, :], pp0[:, 0:2]).then_inc(S["pt"], 1)
                    fl = 0
                    for qi in range(NQT):
                        w = qtiles[qi][2]
                        if qi >= 2:
                            sc.wait_ge(QSS[qi % 2], 16 * (qi // 2))
                        for f in range(w // 1024):
                            sc.wait_ge(S["qt"], fl + 1)
                            sc.copy(qstage[:, (qi % 2) * QBW + f * 1024:
                                            (qi % 2) * QBW + (f + 1) * 1024],
                                    [tpq0, tpq1][fl % 2][:, :]).then_inc(S["qe"], 1)
                            fl += 1

                @b0.gpsimd
                def _(g):
                    g = _AD(g)
                    for qi in range(NQT):
                        w = qtiles[qi][2]
                        if qi >= 2:
                            g.wait_ge(QLS[qi % 2], 16 * (qi // 2))
                            g.wait_ge(S["pq"], 6 + qi - 1)
                        g.dma_start(out=qbuf[:, (qi % 2) * QBW:(qi % 2) * QBW + w],
                                    in_=qtiles[qi][0]).then_inc(QLS[qi % 2], 16)
                    g.wait_ge(S["pt"], 2)
                    g.dma_start(out=sc_part[:, :], in_=acc2[:, :]).then_inc(S["qd"], 16)
                    g.wait_ge(S["qd"], 16)
                    g.collective_compute(
                        "AllReduce", OP.add, replica_groups=[list(range(NCORE))],
                        ins=[sc_part.tensor.ap().opt()],
                        outs=[sc_sum.tensor.ap().opt()]).then_inc(S["cc"], 1)
                    g.wait_ge(S["cc"], 1)
                    g.dma_start(out=thrp[:, :], in_=sc_sum[:, :]).then_inc(S["qd"], 16)
                    def _storeB(g, qi):
                        w = qtiles[qi][2]
                        if qi >= 2:
                            g.wait_ge(QSS[qi % 2], 16 * (qi // 2))
                        g.wait_ge(S["qe"], qe_base[qi + 1])
                        g.dma_start(out=qtiles[qi][1],
                                    in_=qstage[:, (qi % 2) * QBW:(qi % 2) * QBW + w]
                                    ).then_inc(QSS[qi % 2], 16)
                    for qi in range(NQT):
                        w = qtiles[qi][2]
                        g.wait_ge(QLS[qi % 2], 16 * (QAT[qi % 2] + qi // 2))
                        g.wait_ge(S["pq"], 6 + NQT + 2 + qi)
                        g.dma_start(out=qbuf[:, (qi % 2) * QBW:(qi % 2) * QBW + w],
                                    in_=qtiles[qi][0]).then_inc(QLS[qi % 2], 16)
                        if qi >= 2:
                            _storeB(g, qi - 2)
                    _storeB(g, NQT - 2)
                    _storeB(g, NQT - 1)
                    g.wait_ge(QSS[0], 16 * ((NQT + 1) // 2))
                    g.wait_ge(QSS[1], 16 * (NQT // 2))
                    g.collective_compute(
                        "AllGather", OP.bypass, replica_groups=[list(range(NCORE))],
                        ins=[wq1_shd.tensor.ap().opt()],
                        outs=[wq1_all.tensor.ap().opt()]).then_inc(S["cc"], 1)
                    g.wait_ge(S["cc"], 2)
                    g.collective_compute(
                        "AllGather", OP.bypass, replica_groups=[list(range(NCORE))],
                        ins=[wq2_shd.tensor.ap().opt()],
                        outs=[wq2_all.tensor.ap().opt()]).then_inc(S["cc"], 1)

            if 1 in phases:
                with ExitStack() as _es:
                    xg = _es.enter_context(nc.sbuf_tensor("xg", [128, 3 * D], F32))
                    sqb = _es.enter_context(nc.sbuf_tensor("sqb", [128, D], F32))
                    xn32 = _es.enter_context(nc.sbuf_tensor("xn32", [128, 2 * D], F32))
                    xnh = _es.enter_context(nc.sbuf_tensor("xnh", [128, 2 * D], BF16))
                    xnl = _es.enter_context(nc.sbuf_tensor("xnl", [128, 2 * D], BF16))
                    s1 = _es.enter_context(nc.sbuf_tensor("s1", [128, 2], F32))
                    s2 = _es.enter_context(nc.sbuf_tensor("s2", [128, 2], F32))
                    nm = _es.enter_context(nc.sbuf_tensor("nm", [128, 2], F32))
                    vp = _es.enter_context(nc.sbuf_tensor("vp", [128, 2], F32))
                    sr = _es.enter_context(nc.sbuf_tensor("sr", [128, 2], F32))
                    rt1 = _es.enter_context(nc.sbuf_tensor("rt1", [128, 2], F32))
                    rstd = _es.enter_context(nc.sbuf_tensor("rstd", [128, 2], F32))
                    bcol = _es.enter_context(nc.sbuf_tensor("bcol", [128, 2], F32))
                    stg = _es.enter_context(nc.sbuf_tensor("stg", [128, 2 * 16 * 512], BF16))
                    evt = _es.enter_context(nc.sbuf_tensor("evt", [128, 2 * FOWN], F32))
                    ohb = _es.enter_context(nc.sbuf_tensor("ohb", [128, 2 * FOWN], BF16))
                    olb = _es.enter_context(nc.sbuf_tensor("olb", [128, 2 * FOWN], BF16))
                    tp0 = _es.enter_context(nc.psum_tensor("tp0", [128, 1024], BF16))
                    tp1 = _es.enter_context(nc.psum_tensor("tp1", [128, 1024], BF16))
                    mp0 = _es.enter_context(nc.psum_tensor("mp0", [128, 512], F32))
                    mp1 = _es.enter_context(nc.psum_tensor("mp1", [128, 512], F32))
                    b1 = _es.enter_context(nc.Block())
                    tpb = [tp0, tp1]
                    mpb = [mp0, mp1]

                    @b1.sync
                    def _(s):
                        s = _AD(s)
                        for t in range(NT):
                            if t >= 1:
                                s.wait_ge(S["x"], 16 * t)
                            if t >= 3:
                                s.wait_ge(S["a"], t - 2)
                                s.wait_ge(S["d"], t - 2)
                            s.dma_start(
                                out=xg[:, (t % 3) * D:(t % 3 + 1) * D],
                                in_=x_full[t * 128:(t + 1) * 128, :],
                            ).then_inc(S["x"], 16)

                    def fill_dst(m):
                        gq, slot = divmod(m, 6)
                        if slot < 4:
                            ft = slot
                            dsth = (qT_hi if ft < 2 else kT_hi)
                            dstl = (qT_lo if ft < 2 else kT_lo)
                            r0 = (ft % 2) * 128
                            return (dsth[r0:r0 + 128, gq * 256:(gq + 1) * 256],
                                    dstl[r0:r0 + 128, gq * 256:(gq + 1) * 256],
                                    FOWN)
                        tt = slot - 4
                        r0 = (2 * gq + tt) * 128
                        return (v_sh[r0:r0 + 128, :], v_sl[r0:r0 + 128, :], FOWN)

                    @b1.scalar
                    def _(sc):
                        sc = _AD(sc)
                        for i in range(NT + 4):
                            t = i
                            if t < NT:
                                sc.wait_ge(S["x"], 16 * (t + 1))
                                if t >= 2:
                                    sc.wait_ge(S["b"], t - 1)
                                sc.activation(sqb[:, :], xg[:, (t % 3) * D:(t % 3 + 1) * D],
                                              AF.Identity, bias=f32(0.0),
                                              accum_out=s1[:, t % 2:t % 2 + 1])
                                sc.activation(sqb[:, :], xg[:, (t % 3) * D:(t % 3 + 1) * D],
                                              AF.Square, bias=f32(0.0),
                                              accum_out=s2[:, t % 2:t % 2 + 1]
                                              ).then_inc(S["a"], 1)
                            t = i - 1
                            if 0 <= t < NT:
                                sc.wait_ge(S["b"], t + 1)
                                if t >= 2:
                                    sc.wait_ge(S["d"], t - 1)
                                sc.activation(sr[:, t % 2:t % 2 + 1],
                                              vp[:, t % 2:t % 2 + 1],
                                              AF.Sqrt, bias=f32(0.0)
                                              ).then_inc(S["c"], 1)
                            t = i - 2
                            if 0 <= t < NT:
                                sc.wait_ge(S["d"], t + 1)
                                if t >= 2:
                                    sc.wait_ge(S["tp"], 4 * (t - 1))
                                sc.copy(xnh[:, (t % 2) * D:(t % 2 + 1) * D],
                                        xn32[:, (t % 2) * D:(t % 2 + 1) * D]
                                        ).then_inc(S["e"], 1)
                            t = i - 3
                            if 0 <= t < NT:
                                gq = t // 2
                                for b4 in range(4):
                                    fl = 4 * t + b4
                                    sc.wait_ge(S["tp"], fl + 1)
                                    if gq >= 2:
                                        sc.wait_ge(S["mm"], 6 * (gq - 1))
                                    dst = bass.AP(
                                        stg,
                                        (gq % 2) * (16 * 512) + b4 * 4 * 512 + (t % 2) * 128,
                                        [[2 * 16 * 512, 128], [512, 4], [256, 2], [1, 128]],
                                    )
                                    sc.copy(dst, tpb[fl % 2][:, :]).then_inc(S["te"], 1)
                            j = i - 4
                            if 0 <= j < NT:
                                gq, half = divmod(j, 2)
                                for m in range(6 * gq + 3 * half, 6 * gq + 3 * half + 3):
                                    sc.wait_ge(S["ev"], m + 1)
                                    if m >= 2:
                                        sc.wait_ge(S["dq"], 32 * (m - 1))
                                    sc.copy(ohb[:, (m % 2) * FOWN:(m % 2 + 1) * FOWN],
                                            evt[:, (m % 2) * FOWN:(m % 2 + 1) * FOWN]
                                            ).then_inc(S["hi"], 1)

                    @b1.vector
                    def _(v):
                        v = _AD(v)
                        for i in range(NT + 4):
                            t = i
                            if t < NT:
                                v.wait_ge(S["a"], t + 1)
                                if t >= 2:
                                    v.wait_ge(S["c"], t - 1)
                                ts = slice(t % 2, t % 2 + 1)
                                v.tensor_scalar_mul(nm[:, ts], s1[:, ts], -1.0 / D)
                                v.tensor_scalar_mul(vp[:, ts], s2[:, ts], 1.0 / D)
                                v.tensor_mul(rt1[:, ts], nm[:, ts], nm[:, ts])
                                v.tensor_sub(vp[:, ts], vp[:, ts], rt1[:, ts])
                                v.tensor_scalar_add(vp[:, ts], vp[:, ts],
                                                    float(EPS)).then_inc(S["b"], 1)
                            t = i - 1
                            if 0 <= t < NT:
                                v.wait_ge(S["c"], t + 1)
                                if t >= 2:
                                    v.wait_ge(S["e"], t - 1)
                                ts = slice(t % 2, t % 2 + 1)
                                v.reciprocal(rstd[:, ts], sr[:, ts])
                                v.tensor_mul(rt1[:, ts], rstd[:, ts], rstd[:, ts])
                                v.tensor_mul(rt1[:, ts], rt1[:, ts], vp[:, ts])
                                v.tensor_scalar(rt1[:, ts], rt1[:, ts],
                                                -0.5, 1.5, OP.mult, OP.add)
                                v.tensor_mul(rstd[:, ts], rstd[:, ts], rt1[:, ts])
                                v.tensor_mul(bcol[:, ts], nm[:, ts], rstd[:, ts])
                                v.scalar_tensor_tensor(
                                    xn32[:, (t % 2) * D:(t % 2 + 1) * D],
                                    xg[:, (t % 3) * D:(t % 3 + 1) * D],
                                    rstd[:, ts],
                                    bcol[:, ts].broadcast_to([128, D]),
                                    OP.mult, OP.add).then_inc(S["d"], 1)
                            t = i - 2
                            if 0 <= t < NT:
                                v.wait_ge(S["e"], t + 1)
                                if t >= 2:
                                    v.wait_ge(S["tp"], 4 * (t - 1))
                                v.tensor_sub(xnl[:, (t % 2) * D:(t % 2 + 1) * D],
                                             xn32[:, (t % 2) * D:(t % 2 + 1) * D],
                                             xnh[:, (t % 2) * D:(t % 2 + 1) * D]
                                             ).then_inc(S["f"], 1)
                            j = i - 4
                            if 0 <= j < NT:
                                gq, half = divmod(j, 2)
                                for m in range(6 * gq + 3 * half, 6 * gq + 3 * half + 3):
                                    slot = m % 6
                                    v.wait_ge(S["mm"], m + 1)
                                    if m >= 2:
                                        v.wait_ge(S["hi"], m - 1)
                                    if slot < 2:
                                        v.scalar_tensor_tensor(
                                            evt[:, (m % 2) * FOWN:(m % 2 + 1) * FOWN],
                                            mpb[m % 2][:, 0:FOWN], float(QS),
                                            bqs[:, slot:slot + 1].broadcast_to([128, FOWN]),
                                            OP.mult, OP.add).then_inc(S["ev"], 1)
                                    elif slot < 4:
                                        v.scalar_tensor_tensor(
                                            evt[:, (m % 2) * FOWN:(m % 2 + 1) * FOWN],
                                            mpb[m % 2][:, 0:FOWN], 1.0,
                                            bk_sb[:, slot - 2:slot - 1].broadcast_to([128, FOWN]),
                                            OP.mult, OP.add).then_inc(S["ev"], 1)
                                    else:
                                        v.tensor_add(
                                            evt[:, (m % 2) * FOWN:(m % 2 + 1) * FOWN],
                                            mpb[m % 2][:, 0:FOWN],
                                            bv_sb[:, :]).then_inc(S["ev"], 1)
                                    v.wait_ge(S["hi"], m + 1)
                                    if m >= 2:
                                        v.wait_ge(S["dq"], 32 * (m - 1))
                                    v.tensor_sub(
                                        olb[:, (m % 2) * FOWN:(m % 2 + 1) * FOWN],
                                        evt[:, (m % 2) * FOWN:(m % 2 + 1) * FOWN],
                                        ohb[:, (m % 2) * FOWN:(m % 2 + 1) * FOWN]
                                        ).then_inc(S["lo"], 1)

                    @b1.tensor
                    def _(t_):
                        t_ = _AD(t_)
                        for i in range(NT):
                            t = i
                            t_.wait_ge(S["f"], t + 1)
                            for b4 in range(4):
                                fl = 4 * t + b4
                                if fl >= 2:
                                    t_.wait_ge(S["te"], fl - 1)
                                for ii in range(8):
                                    idx = b4 * 8 + ii
                                    k, hl = divmod(idx, 2)
                                    src = (xnh if hl == 0 else xnl)
                                    mm = t_.transpose(
                                        tpb[fl % 2][:, ii * 128:(ii + 1) * 128],
                                        src[:, (t % 2) * D + k * 128:(t % 2) * D + (k + 1) * 128],
                                        ident[:, :])
                                mm.then_inc(S["tp"], 1)
                            if t % 2 == 1:
                                gq = t // 2
                                t_.wait_ge(S["te"], 8 * gq + 8)
                                for slot in range(6):
                                    m = 6 * gq + slot
                                    if m >= 2:
                                        t_.wait_ge(S["ev"], m - 1)
                                    ps = mpb[m % 2]
                                    if slot < 4:
                                        wsh = wq_h if slot < 2 else wk_h
                                        wsl = wq_l if slot < 2 else wk_l
                                        jj = (slot % 2) * 128
                                        for k in range(16):
                                            sh = wsh[:, k * FOWN + jj:k * FOWN + jj + 128]
                                            sl = wsl[:, k * FOWN + jj:k * FOWN + jj + 128]
                                            xh = stg[:, (gq % 2) * 16 * 512 + k * 512:
                                                     (gq % 2) * 16 * 512 + k * 512 + 256]
                                            xl = stg[:, (gq % 2) * 16 * 512 + k * 512 + 256:
                                                     (gq % 2) * 16 * 512 + k * 512 + 512]
                                            t_.matmul(ps[:, 0:FOWN], sh, xh,
                                                      start=(k == 0), stop=False)
                                            t_.matmul(ps[:, 0:FOWN], sl, xh,
                                                      start=False, stop=False)
                                            mm = t_.matmul(ps[:, 0:FOWN], sh, xl,
                                                           start=False, stop=(k == 15))
                                        mm.then_inc(S["mm"], 1)
                                    else:
                                        tt = slot - 4
                                        for k in range(16):
                                            base = (gq % 2) * 16 * 512 + k * 512
                                            sh = stg[:, base + tt * 128:base + tt * 128 + 128]
                                            sl = stg[:, base + 256 + tt * 128:base + 256 + tt * 128 + 128]
                                            mh = wv_h[:, k * FOWN:(k + 1) * FOWN]
                                            ml = wv_l[:, k * FOWN:(k + 1) * FOWN]
                                            t_.matmul(ps[:, 0:FOWN], sh, mh,
                                                      start=(k == 0), stop=False)
                                            t_.matmul(ps[:, 0:FOWN], sl, mh,
                                                      start=False, stop=False)
                                            mm = t_.matmul(ps[:, 0:FOWN], sh, ml,
                                                           start=False, stop=(k == 15))
                                        mm.then_inc(S["mm"], 1)

                    @b1.gpsimd
                    def _(g):
                        g = _AD(g)
                        for m in range(6 * NG):
                            dsth, dstl, wdt = fill_dst(m)
                            if m >= 1:
                                g.wait_ge(S["dq"], 32 * m)
                            g.wait_ge(S["hi"], m + 1)
                            g.dma_start(out=dsth,
                                        in_=ohb[:, (m % 2) * FOWN:(m % 2 + 1) * FOWN]
                                        ).then_inc(S["dq"], 16)
                            g.wait_ge(S["lo"], m + 1)
                            g.dma_start(out=dstl,
                                        in_=olb[:, (m % 2) * FOWN:(m % 2 + 1) * FOWN]
                                        ).then_inc(S["dq"], 16)

        if 3 in phases:
            _ln_scope = _phase3_ln(nc, S, locals())
        if 2 in phases:
            _phase2(nc, S, locals())
        if 3 in phases:
            _phase3_ffn(nc, S, _ln_scope, locals())

        if dbg_outs:
            totals = {"qd": 32, "cc": 3}
            if 1 in phases:
                totals["dq"] = 16 * 12 * NG
            if 2 in phases:
                totals["pd"] = 16 * 16 * QR
                totals["cc"] = 4
            if 3 in phases:
                for _tt in range(TOKT):
                    totals["f2d%d" % _tt] = 16 * NSLAB
                totals["stq"] = 16 * TOKT
            with nc.Block() as bd:
                @bd.gpsimd
                def _(g):
                    for nm_, tot in totals.items():
                        g.wait_ge(S[nm_], tot)
                    n0 = 0
                    for dst, src_ in dbg_outs:
                        g.dma_start(out=dst[...], in_=src_[...]).then_inc(S["qd"], 16)
                        n0 += 16
                    g.wait_ge(S["qd"], totals["qd"] + n0)

    return nc


def _phase2(nc, S, env):
    T, QR, NP = env["T"], env["QR"], env["NP"]
    NG = env["NG"]
    kt_base = env["kt_base"]; kt_cnt = env["kt_cnt"]
    qk_base = env["qk_base"]
    qT_hi, qT_lo, kT_hi, kT_lo = env["qT_hi"], env["qT_lo"], env["kT_hi"], env["kT_lo"]
    v_sh, v_sl = env["v_sh"], env["v_sl"]
    wp_in, mk_in = env["wp_in"], env["mk_in"]
    proj_part, rs_out = env["proj_part"], env["rs_out"]
    ones_c, ones_k1 = env["ones_c"], env["ones_k1"]
    phases = env["phases"]
    dq_tot = 16 * 12 * NG if 1 in phases else 0

    with ExitStack() as _es:
        masks = _es.enter_context(nc.sbuf_tensor("masks", [128, 4 * 512], F32))
        wpb = _es.enter_context(nc.sbuf_tensor("wpb", [128, 2 * D], F32))
        wp_h = _es.enter_context(nc.sbuf_tensor("wp_h", [128, 2 * D], BF16))
        wp_l = _es.enter_context(nc.sbuf_tensor("wp_l", [128, 2 * D], BF16))
        qc_h = _es.enter_context(nc.sbuf_tensor("qc_h", [128, 2 * 512], BF16))
        qc_l = _es.enter_context(nc.sbuf_tensor("qc_l", [128, 2 * 512], BF16))
        kc_h = _es.enter_context(nc.sbuf_tensor("kc_h", [128, 2 * 512], BF16))
        kc_l = _es.enter_context(nc.sbuf_tensor("kc_l", [128, 2 * 512], BF16))
        vc_h = _es.enter_context(nc.sbuf_tensor("vc_h", [128, 2 * 512], BF16))
        vc_l = _es.enter_context(nc.sbuf_tensor("vc_l", [128, 2 * 512], BF16))
        P32 = _es.enter_context(nc.sbuf_tensor("P32", [128, 3 * 512], F32))
        Ph = _es.enter_context(nc.sbuf_tensor("Ph", [128, 3 * 512], BF16))
        Pl = _es.enter_context(nc.sbuf_tensor("Pl", [128, 3 * 512], BF16))
        rb_sb = _es.enter_context(nc.sbuf_tensor("rb_sb", [128, 2 * 512], F32))
        r32 = _es.enter_context(nc.sbuf_tensor("r32", [1, 2 * 512], F32))
        r_h = _es.enter_context(nc.sbuf_tensor("r_h", [1, 2 * 512], BF16))
        r_l = _es.enter_context(nc.sbuf_tensor("r_l", [1, 2 * 512], BF16))
        On32 = _es.enter_context(nc.sbuf_tensor("On32", [128, 2 * 512], F32))
        Oh = _es.enter_context(nc.sbuf_tensor("Oh", [128, 2 * 512], BF16))
        Ol = _es.enter_context(nc.sbuf_tensor("Ol", [128, 2 * 512], BF16))
        pj_sb = _es.enter_context(nc.sbuf_tensor("pj_sb", [128, 2 * 512], F32))
        sp0 = _es.enter_context(nc.psum_tensor("sp0", [128, 512], F32))
        sp1 = _es.enter_context(nc.psum_tensor("sp1", [128, 512], F32))
        sp2 = _es.enter_context(nc.psum_tensor("sp2", [128, 512], F32))
        Op0 = _es.enter_context(nc.psum_tensor("Op0", [128, 512], F32))
        Op1 = _es.enter_context(nc.psum_tensor("Op1", [128, 512], F32))
        den = _es.enter_context(nc.psum_tensor("den", [1, 512], F32))
        pj0 = _es.enter_context(nc.psum_tensor("pj0", [128, 512], F32))
        pj1 = _es.enter_context(nc.psum_tensor("pj1", [128, 512], F32))
        b2 = _es.enter_context(nc.Block())
        spb = [sp0, sp1, sp2]
        Opb = [Op0, Op1]
        pjb = [pj0, pj1]

        qslot_use = {}
        kslot_use = {}
        kslot_prev_end = {}
        _qcnt = [0, 0]
        _kcnt = [0, 0]
        _klast = [None, None]
        for p in range(NP):
            _qcnt[p % 2] += 1
            qslot_use[p] = _qcnt[p % 2]
            for ck in range(p // 2 + 1):
                _kcnt[ck % 2] += 1
                kslot_use[(p, ck)] = _kcnt[ck % 2]
                kslot_prev_end[(p, ck)] = _klast[ck % 2]
                _klast[ck % 2] = kt_base[p] + 4 * ck + 4
        QSL = [S["qA"], S["qB"]]
        KSL = [S["kA"], S["kB"]]

        @b2.sync
        def _(s):
            s = _AD(s)
            s.wait_ge(S["dq"], dq_tot)
            for j in range(4):
                s.dma_start(out=masks[:, j * 512:(j + 1) * 512],
                            in_=mk_in[j, :, :]).then_inc(S["ld"], 16)
            s.dma_start(out=wpb[:, :],
                        in_=bass.AP(wp_in.tensor, 0,
                                    [[D, 128], [128 * D, 2], [1, D]])
                        ).then_inc(S["ld"], 16)
            s.wait_ge(S["ld"], 16 * 12)
            for p in range(NP):
                qr, h = divmod(p, 2)
                r0 = h * 128
                qs = QSL[p % 2]
                if qslot_use[p] >= 2:
                    s.wait_ge(qs, 32 * (qslot_use[p] - 1))
                if p >= 2:
                    s.wait_ge(S["Sf"], kt_base[p - 1])
                s.dma_start(out=qc_h[:, (p % 2) * 512:(p % 2 + 1) * 512],
                            in_=qT_hi[r0:r0 + 128, qr * 512:(qr + 1) * 512]
                            ).then_inc(qs, 16)
                s.dma_start(out=qc_l[:, (p % 2) * 512:(p % 2 + 1) * 512],
                            in_=qT_lo[r0:r0 + 128, qr * 512:(qr + 1) * 512]
                            ).then_inc(qs, 16)
                for ck in range(qr + 1):
                    ks = KSL[ck % 2]
                    if kslot_use[(p, ck)] >= 2:
                        s.wait_ge(ks, 64 * (kslot_use[(p, ck)] - 1))
                    if kslot_prev_end[(p, ck)] is not None:
                        s.wait_ge(S["Vf"], kslot_prev_end[(p, ck)])
                    c2 = ck % 2
                    s.dma_start(out=kc_h[:, c2 * 512:(c2 + 1) * 512],
                                in_=kT_hi[r0:r0 + 128, ck * 512:(ck + 1) * 512]
                                ).then_inc(ks, 16)
                    s.dma_start(out=kc_l[:, c2 * 512:(c2 + 1) * 512],
                                in_=kT_lo[r0:r0 + 128, ck * 512:(ck + 1) * 512]
                                ).then_inc(ks, 16)
                    vap = [[FOWN, 128], [128 * FOWN, 4], [1, 128]]
                    voff = ck * 512 * FOWN + r0
                    s.dma_start(out=vc_h[:, c2 * 512:(c2 + 1) * 512],
                                in_=bass.AP(v_sh.tensor, voff, list(vap))
                                ).then_inc(ks, 16)
                    s.dma_start(out=vc_l[:, c2 * 512:(c2 + 1) * 512],
                                in_=bass.AP(v_sl.tensor, voff, list(vap))
                                ).then_inc(ks, 16)

        LD2 = 16 * 12
        @b2.vector
        def _(v):
            v = _AD(v)
            v.wait_ge(S["ld"], LD2)
            v.tensor_copy(wp_h[:, :], wpb[:, :])
            v.tensor_sub(wp_l[:, :], wpb[:, :], wp_h[:, :]).then_inc(S["rr"], 1)
            for p in range(NP):
                qr, h = divmod(p, 2)
                KT = kt_cnt[p]
                for j in range(KT):
                    g = kt_base[p] + j
                    g2 = g % 3
                    v.wait_ge(S["Pf"], g + 1)
                    if g >= 3:
                        v.wait_ge(S["Vf"], g - 2)
                    if j >= 4 * qr:
                        v.wait_ge(S["ld"], LD2)
                        jj = j - 4 * qr
                        v.tensor_mul(P32[:, g2 * 512:(g2 + 1) * 512],
                                     P32[:, g2 * 512:(g2 + 1) * 512],
                                     masks[:, jj * 512:(jj + 1) * 512])
                    v.tensor_copy(Ph[:, g2 * 512:(g2 + 1) * 512],
                                  P32[:, g2 * 512:(g2 + 1) * 512])
                    v.tensor_sub(Pl[:, g2 * 512:(g2 + 1) * 512],
                                 P32[:, g2 * 512:(g2 + 1) * 512],
                                 Ph[:, g2 * 512:(g2 + 1) * 512]).then_inc(S["Pv"], 1)
                h5 = h * 512
                if p >= 2:
                    v.wait_ge(S["rb"], p - 1)
                v.wait_ge(S["Of"], p + 1)
                v.reciprocal(r32[:, h5:h5 + 512], den[:, :]).then_inc(S["dn"], 1)
                v.tensor_copy(r_h[:, h5:h5 + 512], r32[:, h5:h5 + 512])
                v.tensor_sub(r_l[:, h5:h5 + 512], r32[:, h5:h5 + 512],
                             r_h[:, h5:h5 + 512]).then_inc(S["rr"], 1)
                v.wait_ge(S["rc"], p + 1)
                if qr >= 1:
                    v.wait_ge(S["pj"], 16 * qr)
                v.tensor_mul(On32[:, h5:h5 + 512], Opb[h][:, :],
                             rb_sb[:, h5:h5 + 512]).then_inc(S["no"], 1)
                v.wait_ge(S["oh"], p + 1)
                v.tensor_sub(Ol[:, h5:h5 + 512], On32[:, h5:h5 + 512],
                             Oh[:, h5:h5 + 512]).then_inc(S["no"], 1)
                if h == 1:
                    for f in range(16):
                        pf = 16 * qr + f
                        v.wait_ge(S["pj"], pf + 1)
                        if pf >= 2:
                            v.wait_ge(S["pd"], 16 * (pf - 1))
                        v.tensor_copy(pj_sb[:, (pf % 2) * 512:(pf % 2 + 1) * 512],
                                      pjb[pf % 2][:, :]).then_inc(S["pe"], 1)

        @b2.scalar
        def _(sc):
            sc = _AD(sc)
            for p in range(NP):
                qr, h = divmod(p, 2)
                KT = kt_cnt[p]
                for j in range(KT):
                    g = kt_base[p] + j
                    g2 = g % 3
                    sc.wait_ge(S["Sf"], g + 1)
                    if g >= 3:
                        sc.wait_ge(S["Pv"], g - 2)
                    sc.activation(P32[:, g2 * 512:(g2 + 1) * 512],
                                  spb[g % 3][:, :], AF.Exp,
                                  bias=ESHIFT).then_inc(S["Pf"], 1)
                h5 = h * 512
                if p >= 2:
                    sc.wait_ge(S["no"], 2 * (p - 2) + 1)
                sc.wait_ge(S["rb"], p + 1)
                sc.copy(rb_sb[:, h5:h5 + 512], pjb[h][:, :]).then_inc(S["rc"], 1)
                sc.wait_ge(S["no"], 2 * p + 1)
                sc.copy(Oh[:, h5:h5 + 512],
                        On32[:, h5:h5 + 512]).then_inc(S["oh"], 1)

        @b2.tensor
        def _(t_):
            t_ = _AD(t_)
            for p in range(NP):
                qr, h = divmod(p, 2)
                KT = kt_cnt[p]

                def s_tile(j):
                    g = kt_base[p] + j
                    g2 = g % 3
                    ck, kk = divmod(j, 4)
                    c2 = ck % 2
                    t_.wait_ge(QSL[p % 2], 32 * qslot_use[p])
                    t_.wait_ge(KSL[ck % 2], 64 * kslot_use[(p, ck)])
                    if g >= 3:
                        t_.wait_ge(S["Pf"], g - 2)
                    kh = kc_h[:, c2 * 512 + kk * 128:c2 * 512 + (kk + 1) * 128]
                    kl = kc_l[:, c2 * 512 + kk * 128:c2 * 512 + (kk + 1) * 128]
                    qh = qc_h[:, (p % 2) * 512:(p % 2 + 1) * 512]
                    ql = qc_l[:, (p % 2) * 512:(p % 2 + 1) * 512]
                    t_.matmul(spb[g2][:, :], kh, qh, start=True, stop=False)
                    t_.matmul(spb[g2][:, :], kl, qh, start=False, stop=False)
                    t_.matmul(spb[g2][:, :], kh, ql, start=False,
                              stop=True).then_inc(S["Sf"], 1)

                def pv_tile(j):
                    g = kt_base[p] + j
                    g2 = g % 3
                    ck, kk = divmod(j, 4)
                    c2 = ck % 2
                    t_.wait_ge(S["Pv"], g + 1)
                    first = (j == 0)
                    last = (j == KT - 1)
                    if first:
                        t_.wait_ge(S["no"], max(0, 2 * p - 3))
                        if p >= 1:
                            t_.wait_ge(S["dn"], p)
                    t_.wait_ge(KSL[ck % 2], 64 * kslot_use[(p, ck)])
                    vh = vc_h[:, c2 * 512 + kk * 128:c2 * 512 + (kk + 1) * 128]
                    vl = vc_l[:, c2 * 512 + kk * 128:c2 * 512 + (kk + 1) * 128]
                    ph = Ph[:, g2 * 512:(g2 + 1) * 512]
                    pl = Pl[:, g2 * 512:(g2 + 1) * 512]
                    t_.matmul(Opb[h][:, :], vh, ph, start=first, stop=False)
                    t_.matmul(Opb[h][:, :], vh, pl, start=False, stop=False)
                    t_.matmul(Opb[h][:, :], vl, ph, start=False, stop=last
                              ).then_inc(S["Vf"], 1)
                    t_.matmul(den[:, :], ones_c[:, :], ph, start=first, stop=False)
                    mm = t_.matmul(den[:, :], ones_c[:, :], pl, start=False,
                                   stop=last)
                    if last:
                        mm.then_inc(S["Of"], 1)

                for j in range(KT):
                    s_tile(j)
                    if j >= 2:
                        pv_tile(j - 2)
                if KT >= 2:
                    pv_tile(KT - 2)
                pv_tile(KT - 1)
                t_.wait_ge(S["rr"], p + 2)
                if qr >= 1:
                    t_.wait_ge(S["pe"], 16 * qr)
                h5 = h * 512
                t_.matmul(pjb[h][:, :], ones_k1[:, :], r_h[:, h5:h5 + 512],
                          start=True, stop=False)
                t_.matmul(pjb[h][:, :], ones_k1[:, :], r_l[:, h5:h5 + 512],
                          start=False, stop=True).then_inc(S["rb"], 1)
                if h == 1:
                    for f in range(16):
                        tt, dg = divmod(f, 4)
                        pf = 16 * qr + f
                        t_.wait_ge(S["no"], 2 * p + 2)
                        if f >= 2:
                            t_.wait_ge(S["pe"], pf - 1)
                        else:
                            t_.wait_ge(S["rc"], p + 1)
                        ps = pjb[pf % 2]
                        for ch in range(2):
                            oh = Oh[:, ch * 512 + tt * 128:ch * 512 + (tt + 1) * 128]
                            ol = Ol[:, ch * 512 + tt * 128:ch * 512 + (tt + 1) * 128]
                            mh = wp_h[:, ch * D + dg * 512:ch * D + (dg + 1) * 512]
                            ml = wp_l[:, ch * D + dg * 512:ch * D + (dg + 1) * 512]
                            t_.matmul(ps[:, :], oh, mh, start=(ch == 0), stop=False)
                            t_.matmul(ps[:, :], ol, mh, start=False, stop=False)
                            mmp = t_.matmul(ps[:, :], oh, ml, start=False,
                                            stop=(ch == 1))
                            if ch == 1:
                                mmp.then_inc(S["pj"], 1)

        @b2.gpsimd
        def _(g):
            g = _AD(g)
            for qr in range(QR):
                for f in range(16):
                    tt, dg = divmod(f, 4)
                    pf = 16 * qr + f
                    if pf >= 1:
                        g.wait_ge(S["pd"], 16 * pf)
                    g.wait_ge(S["pe"], pf + 1)
                    g.dma_start(
                        out=proj_part[qr * 512 + tt * 128:qr * 512 + (tt + 1) * 128,
                                      dg * 512:(dg + 1) * 512],
                        in_=pj_sb[:, (pf % 2) * 512:(pf % 2 + 1) * 512]
                    ).then_inc(S["pd"], 16)
            g.wait_ge(S["pd"], 16 * 16 * QR)
            g.wait_ge(S["cc"], 3)
            g.collective_compute(
                "ReduceScatter", OP.add, replica_groups=[list(range(NCORE))],
                ins=[proj_part.tensor.ap().opt()],
                outs=[rs_out.tensor.ap().opt()]).then_inc(S["cc"], 1)


def _phase3_ln(nc, S, env):
    T, DFF = env["T"], env["DFF"]
    TOK, TOKT = env["TOK"], env["TOKT"]
    NFT = env["NFT"]
    x_own, b1_in = env["x_own"], env["b1_in"]
    ident = env["ident"]
    st = env["st"]
    _esx = st.enter_context(ExitStack())
    if True:
        xnT_h = _esx.enter_context(nc.sbuf_tensor("xnT_h", [128, 16 * TOK], BF16))
        xnT_l = _esx.enter_context(nc.sbuf_tensor("xnT_l", [128, 16 * TOK], BF16))
        b1_sb = _esx.enter_context(nc.sbuf_tensor("b1_sb", [128, NFT], F32))
        with ExitStack() as _es:
            xg3 = _es.enter_context(nc.sbuf_tensor("xg3", [128, 3 * D], F32))
            sqb3 = _es.enter_context(nc.sbuf_tensor("sqb3", [128, D], F32))
            xn33 = _es.enter_context(nc.sbuf_tensor("xn33", [128, 2 * D], F32))
            xnh3 = _es.enter_context(nc.sbuf_tensor("xnh3", [128, 2 * D], BF16))
            xnl3 = _es.enter_context(nc.sbuf_tensor("xnl3", [128, 2 * D], BF16))
            sm3 = _es.enter_context(nc.sbuf_tensor("sm3", [128, 16], F32))
            tq0 = _es.enter_context(nc.psum_tensor("tq0", [128, 1024], BF16))
            tq1 = _es.enter_context(nc.psum_tensor("tq1", [128, 1024], BF16))
            b3l = _es.enter_context(nc.Block())
            tqb = [tq0, tq1]
            @b3l.sync
            def _(s):
                s = _AD(s)
                s.dma_start(out=b1_sb[:, :], in_=b1_in[:, :]).then_inc(S["xo"], 16)
                for t in range(TOKT):
                    if t >= 1:
                        s.wait_ge(S["xo"], 16 * (t + 1))
                    if t >= 3:
                        s.wait_ge(S["la"], t - 2)
                        s.wait_ge(S["ldd"], t - 2)
                    s.dma_start(out=xg3[:, (t % 3) * D:(t % 3 + 1) * D],
                                in_=x_own[t * 128:(t + 1) * 128, :]
                                ).then_inc(S["xo"], 16)

            @b3l.scalar
            def _(sc):
                sc = _AD(sc)
                for i in range(TOKT + 4):
                    t = i
                    if t < TOKT:
                        sc.wait_ge(S["xo"], 16 * (t + 2))
                        if t >= 2:
                            sc.wait_ge(S["lb"], t - 1)
                        sc.activation(sqb3[:, :], xg3[:, (t % 3) * D:(t % 3 + 1) * D],
                                      AF.Identity, bias=f32(0.0),
                                      accum_out=sm3[:, t % 2:t % 2 + 1])
                        sc.activation(sqb3[:, :], xg3[:, (t % 3) * D:(t % 3 + 1) * D],
                                      AF.Square, bias=f32(0.0),
                                      accum_out=sm3[:, 2 + t % 2:3 + t % 2]
                                      ).then_inc(S["la"], 1)
                    t = i - 1
                    if 0 <= t < TOKT:
                        sc.wait_ge(S["lb"], t + 1)
                        if t >= 2:
                            sc.wait_ge(S["ldd"], t - 1)
                        sc.activation(sm3[:, 8 + t % 2:9 + t % 2],
                                      sm3[:, 6 + t % 2:7 + t % 2],
                                      AF.Sqrt, bias=f32(0.0)).then_inc(S["lc"], 1)
                    t = i - 2
                    if 0 <= t < TOKT:
                        sc.wait_ge(S["ldd"], t + 1)
                        if t >= 2:
                            sc.wait_ge(S["t3"], 4 * (t - 1))
                        sc.copy(xnh3[:, (t % 2) * D:(t % 2 + 1) * D],
                                xn33[:, (t % 2) * D:(t % 2 + 1) * D]
                                ).then_inc(S["le"], 1)
                    t = i - 3
                    if 0 <= t < TOKT:
                        for b4 in range(4):
                            fl = 4 * t + b4
                            sc.wait_ge(S["t3"], fl + 1)
                            k0 = b4 * 4
                            dsth = bass.AP(xnT_h, k0 * TOK + t * 128,
                                           [[16 * TOK, 128], [TOK, 4], [1, 128]])
                            dstl = bass.AP(xnT_l, k0 * TOK + t * 128,
                                           [[16 * TOK, 128], [TOK, 4], [1, 128]])
                            src = tqb[fl % 2]
                            sph = bass.AP(src, 0, [[1024, 128], [256, 4], [1, 128]])
                            spl = bass.AP(src, 128, [[1024, 128], [256, 4], [1, 128]])
                            sc.copy(dsth, sph)
                            sc.copy(dstl, spl).then_inc(S["e3"], 1)

            @b3l.vector
            def _(v):
                v = _AD(v)
                for i in range(TOKT + 4):
                    t = i
                    if t < TOKT:
                        v.wait_ge(S["la"], t + 1)
                        if t >= 2:
                            v.wait_ge(S["lc"], t - 1)
                        a, b_, c, dcol = t % 2, 2 + t % 2, 4 + t % 2, 6 + t % 2
                        v.tensor_scalar_mul(sm3[:, c:c + 1], sm3[:, a:a + 1],
                                            -1.0 / D)
                        v.tensor_scalar_mul(sm3[:, dcol:dcol + 1],
                                            sm3[:, b_:b_ + 1], 1.0 / D)
                        v.tensor_mul(sm3[:, 12:13], sm3[:, c:c + 1], sm3[:, c:c + 1])
                        v.tensor_sub(sm3[:, dcol:dcol + 1], sm3[:, dcol:dcol + 1],
                                     sm3[:, 12:13])
                        v.tensor_scalar_add(sm3[:, dcol:dcol + 1],
                                            sm3[:, dcol:dcol + 1],
                                            float(EPS)).then_inc(S["lb"], 1)
                    t = i - 1
                    if 0 <= t < TOKT:
                        v.wait_ge(S["lc"], t + 1)
                        if t >= 2:
                            v.wait_ge(S["le"], t - 1)
                        r, c, dcol = 10 + t % 2, 4 + t % 2, 6 + t % 2
                        v.reciprocal(sm3[:, r:r + 1], sm3[:, 8 + t % 2:9 + t % 2])
                        v.tensor_mul(sm3[:, 12:13], sm3[:, r:r + 1], sm3[:, r:r + 1])
                        v.tensor_mul(sm3[:, 12:13], sm3[:, 12:13],
                                     sm3[:, dcol:dcol + 1])
                        v.tensor_scalar(sm3[:, 12:13], sm3[:, 12:13], -0.5,
                                        1.5, OP.mult, OP.add)
                        v.tensor_mul(sm3[:, r:r + 1], sm3[:, r:r + 1], sm3[:, 12:13])
                        v.tensor_mul(sm3[:, 13 + t % 2:14 + t % 2],
                                     sm3[:, c:c + 1], sm3[:, r:r + 1])
                        v.scalar_tensor_tensor(
                            xn33[:, (t % 2) * D:(t % 2 + 1) * D],
                            xg3[:, (t % 3) * D:(t % 3 + 1) * D],
                            sm3[:, r:r + 1],
                            sm3[:, 13 + t % 2:14 + t % 2].broadcast_to([128, D]),
                            OP.mult, OP.add).then_inc(S["ldd"], 1)
                    t = i - 2
                    if 0 <= t < TOKT:
                        v.wait_ge(S["le"], t + 1)
                        if t >= 2:
                            v.wait_ge(S["t3"], 4 * (t - 1))
                        v.tensor_sub(xnl3[:, (t % 2) * D:(t % 2 + 1) * D],
                                     xn33[:, (t % 2) * D:(t % 2 + 1) * D],
                                     xnh3[:, (t % 2) * D:(t % 2 + 1) * D]
                                     ).then_inc(S["lf"], 1)

            @b3l.tensor
            def _(t_):
                t_ = _AD(t_)
                for t in range(TOKT):
                    t_.wait_ge(S["lf"], t + 1)
                    for b4 in range(4):
                        fl = 4 * t + b4
                        if fl >= 2:
                            t_.wait_ge(S["e3"], fl - 1)
                        for ii in range(8):
                            idx = b4 * 8 + ii
                            k, hl = divmod(idx, 2)
                            src = (xnh3 if hl == 0 else xnl3)
                            mm = t_.transpose(
                                tqb[fl % 2][:, ii * 128:(ii + 1) * 128],
                                src[:, (t % 2) * D + k * 128:(t % 2) * D + (k + 1) * 128],
                                ident[:, :])
                        mm.then_inc(S["t3"], 1)

    return {"xnT_h": xnT_h, "xnT_l": xnT_l, "b1_sb": b1_sb, "esx": _esx}


def _phase3_ffn(nc, S, ln_scope, env):
    T, DFF = env["T"], env["DFF"]
    TOK, TOKT = env["TOK"], env["TOKT"]
    NFT, NSLAB = env["NFT"], env["NSLAB"]
    TN, NTH = env["TN"], env["NTH"]
    W1S, W2S = env["W1S"], env["W2S"]
    x_own, bp_in, b2_in = env["x_own"], env["bp_in"], env["b2_in"]
    wq1_all, wq2_all = env["wq1_all"], env["wq2_all"]
    ffn_acc, rs_out, out_own = env["ffn_acc"], env["rs_out"], env["out_own"]
    phases = env["phases"]
    FPS = 4
    NFILL = TOKT * 4
    xnT_h = ln_scope["xnT_h"]
    xnT_l = ln_scope["xnT_l"]
    b1_sb = ln_scope["b1_sb"]
    if True:
        E3TOT = 4 * TOKT
        with ExitStack() as _es:
            w1s = _es.enter_context(nc.sbuf_tensor("w1s", [128, 2 * 16 * 512], BF16))
            w2s = _es.enter_context(nc.sbuf_tensor("w2s", [128, 8 * D], BF16))
            hsl_h = _es.enter_context(nc.sbuf_tensor("hsl", [128, 2 * FPS * TOK], BF16))
            hsl_l = _es.enter_context(nc.sbuf_tensor("hsll", [128, 2 * FPS * TOK], BF16))
            h32 = _es.enter_context(nc.sbuf_tensor("h32", [128, 2 * 512], F32))
            f2sb = _es.enter_context(nc.sbuf_tensor("f2sb", [128, 2 * D], F32))
            ff0 = _es.enter_context(nc.psum_tensor("ff0", [128, 512], F32))
            ff1 = _es.enter_context(nc.psum_tensor("ff1", [128, 512], F32))
            ff2 = _es.enter_context(nc.psum_tensor("ff2", [128, 512], F32))
            ff3 = _es.enter_context(nc.psum_tensor("ff3", [128, 512], F32))
            ff4 = _es.enter_context(nc.psum_tensor("ff4", [128, 512], F32))
            ff5 = _es.enter_context(nc.psum_tensor("ff5", [128, 512], F32))
            ff6 = _es.enter_context(nc.psum_tensor("ff6", [128, 512], F32))
            ff7 = _es.enter_context(nc.psum_tensor("ff7", [128, 512], F32))
            b3a = _es.enter_context(nc.Block())
            ffb = [ff0, ff1, ff2, ff3, ff4, ff5, ff6, ff7]

            W1SL = [S["w1"], S["w1b"]]
            W2SL = [S["w2"], S["w2b"]]

            @b3a.sync
            def _(s):
                s = _AD(s)
                s.wait_ge(S["cc"], 3)
                for sl in range(NSLAB):
                    if sl >= 2:
                        s.wait_ge(W1SL[sl % 2], 16 * 16 * (sl // 2))
                        s.wait_ge(S["f1"], 4 * NTH * (sl - 1))
                    for k in range(16):
                        if W1S >= 512:
                            blk, off = divmod(sl * 512, W1S)
                            ap1 = bass.AP(wq1_all.tensor,
                                          blk * D * W1S + (k * 128) * W1S + off,
                                          [[W1S, 128], [1, 512]])
                        else:
                            nb = 512 // W1S
                            blk = sl * 512 // W1S
                            ap1 = bass.AP(wq1_all.tensor,
                                          blk * D * W1S + (k * 128) * W1S,
                                          [[W1S, 128], [D * W1S, nb], [1, W1S]])
                        s.dma_start(
                            out=w1s[:, (sl % 2) * (16 * 512) + k * 512:
                                    (sl % 2) * (16 * 512) + (k + 1) * 512],
                            in_=ap1).then_inc(W1SL[sl % 2], 16)
                    if sl >= 2:
                        s.wait_ge(W2SL[sl % 2], 16 * 4 * (sl // 2))
                        s.wait_ge(S["f2"], NFILL * (sl - 1))
                    for jj in range(FPS):
                        fc = FPS * sl + jj
                        ap2 = bass.AP(wq2_all.tensor, (fc * 128) * W2S,
                                      [[W2S, 128], [DFF * W2S, 8], [1, W2S]])
                        s.dma_start(
                            out=w2s[:, ((sl % 2) * 4 + jj) * D:((sl % 2) * 4 + jj + 1) * D],
                            in_=ap2).then_inc(W2SL[sl % 2], 16)

            def _ffn2_slab(t_, sl):
                hb = (sl % 2) * FPS * TOK
                for fi in range(NFILL):
                    tt, dg = divmod(fi, 4)
                    ff = NFILL * sl + fi
                    t_.wait_ge(S["h"], 2 * FPS * NTH * (sl + 1))
                    t_.wait_ge(S["g1"], min(4 * NTH * (sl + 2), 4 * NTH * NSLAB))
                    if ff >= 2:
                        t_.wait_ge(S["fe"], ff - 1)
                    ps = ffb[ff % 2]
                    for jj in range(FPS):
                        fc = FPS * sl + jj
                        t_.wait_ge(W2SL[sl % 2], 16 * 4 * (sl // 2 + 1))
                        th, trem = divmod(tt * 128, TN)
                        sh = hsl_h[:, hb + (jj * NTH + th) * TN + trem:
                                   hb + (jj * NTH + th) * TN + trem + 128]
                        sll = hsl_l[:, hb + (jj * NTH + th) * TN + trem:
                                    hb + (jj * NTH + th) * TN + trem + 128]
                        mv = w2s[:, ((sl % 2) * 4 + jj) * D + dg * 512:
                                 ((sl % 2) * 4 + jj) * D + (dg + 1) * 512]
                        t_.matmul(ps[:, :], sh, mv, start=(jj == 0), stop=False)
                        mm2 = t_.matmul(ps[:, :], sll, mv, start=False,
                                        stop=(jj == FPS - 1))
                        if jj == FPS - 1:
                            mm2.then_inc(S["f2"], 1)

            @b3a.tensor
            def _(t_):
                t_ = _AD(t_)
                t_.wait_ge(S["e3"], E3TOT)
                for sl in range(NSLAB):
                    for th in range(NTH):
                        for k in range(16):
                            t_.wait_ge(W1SL[sl % 2], 16 * 16 * (sl // 2 + 1))
                            for jj in range(4):
                                bk = ((sl * NTH + th) % 2) * 4 + jj
                                prev = (sl * NTH + th - 2)
                                if prev >= 0:
                                    t_.wait_ge(S["g1"], 4 * prev + jj + 1)
                                if sl >= 2 and th == 0 and bk <= 1:
                                    t_.wait_ge(S["fe"], NFILL * (sl - 1))
                                st_h = w1s[:, (sl % 2) * (16 * 512) + k * 512 + jj * 128:
                                           (sl % 2) * (16 * 512) + k * 512 + (jj + 1) * 128]
                                xh = xnT_h[:, k * TOK + th * TN:k * TOK + (th + 1) * TN]
                                xl = xnT_l[:, k * TOK + th * TN:k * TOK + (th + 1) * TN]
                                t_.matmul(ffb[bk][:, 0:TN], st_h, xh,
                                          start=(k == 0), stop=False)
                                mm = t_.matmul(ffb[bk][:, 0:TN], st_h, xl,
                                               start=False, stop=(k == 15))
                                if k == 15:
                                    mm.then_inc(S["f1"], 1)
                    if sl >= 1:
                        _ffn2_slab(t_, sl - 1)
                _ffn2_slab(t_, NSLAB - 1)

            @b3a.scalar
            def _(sc):
                sc = _AD(sc)
                sc.wait_ge(S["xo"], 32)
                for sl in range(NSLAB):
                    for th in range(NTH):
                        for jj in range(4):
                            gi = 4 * (sl * NTH + th) + jj
                            bk = ((sl * NTH + th) % 2) * 4 + jj
                            sc.wait_ge(S["f1"], gi + 1)
                            if gi >= 2:
                                sc.wait_ge(S["h"], 2 * (gi - 1))
                            ft = 4 * sl + jj
                            sc.activation(h32[:, (gi % 2) * 512:(gi % 2) * 512 + TN],
                                          ffb[bk][:, 0:TN], AF.Gelu,
                                          bias=b1_sb[:, ft:ft + 1]
                                          ).then_inc(S["g1"], 1)

            F2D = [S["f2d%d" % i] for i in range(8)]

            def _f2evacs(v, sl):
                for fi in range(NFILL):
                    tt, dg = divmod(fi, 4)
                    ff = NFILL * sl + fi
                    gidx = ff // 4
                    v.wait_ge(S["f2"], ff + 1)
                    if gidx >= 2:
                        pg = gidx - 2
                        psl, ptt = divmod(pg, TOKT)
                        v.wait_ge(F2D[ptt], 16 * (psl + 1))
                    v.tensor_copy(
                        f2sb[:, (gidx % 2) * D + dg * 512:(gidx % 2) * D + (dg + 1) * 512],
                        ffb[ff % 2][:, :]).then_inc(S["fe"], 1)

            @b3a.vector
            def _(v):
                v = _AD(v)
                for sl in range(NSLAB):
                    hb = (sl % 2) * FPS * TOK
                    for th in range(NTH):
                        for jj in range(4):
                            gi = 4 * (sl * NTH + th) + jj
                            if sl >= 2 and th == 0 and jj == 0:
                                v.wait_ge(S["f2"], NFILL * (sl - 1))
                            v.wait_ge(S["g1"], gi + 1)
                            dsth = hsl_h[:, hb + (jj * NTH + th) * TN:
                                         hb + (jj * NTH + th + 1) * TN]
                            dstl = hsl_l[:, hb + (jj * NTH + th) * TN:
                                         hb + (jj * NTH + th + 1) * TN]
                            v.tensor_copy(dsth, h32[:, (gi % 2) * 512:(gi % 2) * 512 + TN])
                            v.tensor_sub(dstl, h32[:, (gi % 2) * 512:(gi % 2) * 512 + TN],
                                         dsth).then_inc(S["h"], 2)
                    if sl >= 1:
                        _f2evacs(v, sl - 1)
                _f2evacs(v, NSLAB - 1)

            @b3a.gpsimd
            def _(g):
                g = _AD(g)
                for sl in range(NSLAB):
                    for tt in range(TOKT):
                        gidx = sl * TOKT + tt
                        if sl >= 1:
                            g.wait_ge(F2D[tt], 16 * sl)
                        g.wait_ge(S["fe"], 4 * (gidx + 1))
                        g.dma_start(
                            out=ffn_acc[tt * 128:(tt + 1) * 128, :],
                            in_=f2sb[:, (gidx % 2) * D:(gidx % 2 + 1) * D],
                            accum_op=(OP.add if sl >= 1 else OP.bypass)
                        ).then_inc(F2D[tt], 16)
                for tt in range(TOKT):
                    g.wait_ge(F2D[tt], 16 * NSLAB)

        with ExitStack() as _es:
            rs_sb = _es.enter_context(nc.sbuf_tensor("rs_sb", [128, 2 * D], F32))
            xf_sb = _es.enter_context(nc.sbuf_tensor("xf_sb", [128, 2 * D], F32))
            fa_sb = _es.enter_context(nc.sbuf_tensor("fa_sb", [128, 2 * D], F32))
            bp_sb = _es.enter_context(nc.sbuf_tensor("bp_sb", [128, D], F32))
            b22_sb = _es.enter_context(nc.sbuf_tensor("b22_sb", [128, D], F32))
            cmb = _es.enter_context(nc.sbuf_tensor("cmb", [128, 2 * D], F32))
            b3b = _es.enter_context(nc.Block())
            R3SL = [S["r3"], S["r3b"]]

            @b3b.sync
            def _(s):
                s = _AD(s)
                s.wait_ge(S["cc"], 4)
                s.wait_ge(S["ld"], 16 * 12)
                s.dma_start(out=bp_sb[:, :], in_=bp_in[:, :]).then_inc(S["ld"], 16)
                s.dma_start(out=b22_sb[:, :], in_=b2_in[:, :]).then_inc(S["ld"], 16)
                s.wait_ge(S["ld"], 16 * 14)
                for _tt in range(TOKT):
                    s.wait_ge(S["f2d%d" % _tt], 16 * NSLAB)
                for t in range(TOKT):
                    if t >= 2:
                        s.wait_ge(R3SL[t % 2], 48 * (t // 2))
                        s.wait_ge(S["o3"], t - 1)
                    s.dma_start(out=rs_sb[:, (t % 2) * D:(t % 2 + 1) * D],
                                in_=rs_out[t * 128:(t + 1) * 128, :]).then_inc(R3SL[t % 2], 16)
                    s.dma_start(out=xf_sb[:, (t % 2) * D:(t % 2 + 1) * D],
                                in_=x_own[t * 128:(t + 1) * 128, :]).then_inc(R3SL[t % 2], 16)
                    s.dma_start(out=fa_sb[:, (t % 2) * D:(t % 2 + 1) * D],
                                in_=ffn_acc[t * 128:(t + 1) * 128, :]).then_inc(R3SL[t % 2], 16)

            @b3b.vector
            def _(v):
                v = _AD(v)
                v.wait_ge(S["ld"], 16 * 14)
                v.tensor_add(bp_sb[:, :], bp_sb[:, :], b22_sb[:, :])
                for t in range(TOKT):
                    v.wait_ge(R3SL[t % 2], 48 * (t // 2 + 1))
                    if t >= 2:
                        v.wait_ge(S["stq"], 16 * (t - 1))
                    sl_ = slice((t % 2) * D, (t % 2 + 1) * D)
                    v.tensor_add(cmb[:, sl_], rs_sb[:, sl_], fa_sb[:, sl_])
                    v.tensor_add(cmb[:, sl_], cmb[:, sl_], xf_sb[:, sl_])
                    v.tensor_add(cmb[:, sl_], cmb[:, sl_],
                                 bp_sb[:, :]).then_inc(S["o3"], 1)

            @b3b.gpsimd
            def _(g):
                g = _AD(g)
                for t in range(TOKT):
                    g.wait_ge(S["o3"], t + 1)
                    g.dma_start(out=out_own[t * 128:(t + 1) * 128, :],
                                in_=cmb[:, (t % 2) * D:(t % 2 + 1) * D]
                                ).then_inc(S["stq"], 16)
                g.wait_ge(S["stq"], 16 * TOKT)


def stage_inputs(x, W_qkv, b_qkv, W_proj, b_proj, w1, b1, w2, b2,
                 T=8192, DFF=8192):
    import ml_dtypes
    TOK = T // NCORE
    NFT = DFF // 128
    x2 = np.ascontiguousarray(np.asarray(x, dtype=np.float32).reshape(T, D))
    W_qkv = np.asarray(W_qkv, np.float32)
    b_qkv = np.asarray(b_qkv, np.float32)
    W_proj = np.asarray(W_proj, np.float32)
    b_proj = np.asarray(b_proj, np.float32)
    w1 = np.asarray(w1, np.float32)
    b1 = np.asarray(b1, np.float32)
    w2 = np.asarray(w2, np.float32)
    b2 = np.asarray(b2, np.float32)
    NQ = W_qkv.shape[1] // 3
    ident = np.eye(128, dtype=ml_dtypes.bfloat16)
    masks = np.zeros((4, 128, 512), np.float32)
    for j in range(4):
        kk = np.arange(128)[:, None] + j * 128
        qq = np.arange(512)[None, :]
        masks[j] = (qq >= kk).astype(np.float32)
    in_maps = []
    for c in range(NCORE):
        fo = c * FOWN
        d = {
            "x_full": x2,
            "x_own": np.ascontiguousarray(x2[c * TOK:(c + 1) * TOK]),
            "wq_in": np.ascontiguousarray(W_qkv[:, fo:fo + FOWN]),
            "wk_in": np.ascontiguousarray(W_qkv[:, NQ + fo:NQ + fo + FOWN]),
            "wv_in": np.ascontiguousarray(W_qkv[:, 2 * NQ + fo:2 * NQ + fo + FOWN]),
            "bq_in": np.ascontiguousarray(
                b_qkv[fo:fo + FOWN].reshape(2, 128).T),
            "bk_in": np.ascontiguousarray(
                b_qkv[NQ + fo:NQ + fo + FOWN].reshape(2, 128).T),
            "bv_in": np.ascontiguousarray(np.broadcast_to(
                b_qkv[2 * NQ + fo:2 * NQ + fo + FOWN][None, :], (128, FOWN))),
            "wp_in": np.ascontiguousarray(W_proj[fo:fo + FOWN, :]),
            "bp_in": np.ascontiguousarray(
                np.broadcast_to(b_proj[None, :], (128, D))),
            "b2_in": np.ascontiguousarray(
                np.broadcast_to(b2[None, :], (128, D))),
            "b1_in": np.ascontiguousarray(b1.reshape(NFT, 128).T),
            "w1_sh": np.ascontiguousarray(
                w1[c * (DFF // 8):(c + 1) * (DFF // 8), :]),
            "w2_sh": np.ascontiguousarray(
                w2[c * (D // 8):(c + 1) * (D // 8), :]),
            "id_in": ident,
            "mk_in": masks,
        }
        in_maps.append(d)
    return in_maps


def gather_output(results, T=8192):
    TOK = T // NCORE
    out = np.concatenate([np.asarray(r["out_own"]) for r in results], axis=0)
    return out.reshape(1, T, D)


from concourse.bass_utils import run_bass_kernel_spmd

_NC_CACHE = {}


def kernel(L0_attn_x, L0_attn_W_qkv, L0_attn_b_qkv, L0_attn_W_proj,
           L0_attn_b_proj, L0_ffn_w1, L0_ffn_b1, L0_ffn_w2, L0_ffn_b2):
    T, DFF = 8192, 8192
    if "nc" not in _NC_CACHE:
        _NC_CACHE["nc"] = build(T=T, DFF=DFF)
    nc = _NC_CACHE["nc"]
    in_maps = stage_inputs(L0_attn_x, L0_attn_W_qkv, L0_attn_b_qkv,
                           L0_attn_W_proj, L0_attn_b_proj,
                           L0_ffn_w1, L0_ffn_b1, L0_ffn_w2, L0_ffn_b2,
                           T=T, DFF=DFF)
    res = run_bass_kernel_spmd(nc, in_maps, core_ids=list(range(NCORE)))
    return gather_output(res.results, T=T).astype(np.float32)


# version 10
# speedup vs baseline: 1.2124x; 1.0105x over previous
import math
from contextlib import ExitStack

import numpy as np
import concourse.bass as bass
import concourse.mybir as mybir

F32 = mybir.dt.float32
BF16 = mybir.dt.bfloat16
AF = mybir.ActivationFunctionType
OP = mybir.AluOpType
AX = mybir.AxisListType

D = 2048
HD = 128
FOWN = 256
NCORE = 8
EPS = 1e-5
QS = np.float32(1.0 / math.sqrt(HD))
ESHIFT = np.float32(-20.0)
f32 = np.float32




class _AD:
    _OPS = {"memset", "tensor_copy", "tensor_tensor", "tensor_add",
            "tensor_sub", "tensor_mul", "tensor_max", "tensor_scalar",
            "tensor_single_scalar", "tensor_scalar_add", "tensor_scalar_sub",
            "tensor_scalar_mul", "tensor_scalar_max", "tensor_scalar_min",
            "reduce_sum", "reduce_max", "tensor_reduce",
            "scalar_tensor_tensor", "reciprocal", "activation", "copy",
            "select", "transpose"}

    def __init__(self, eng):
        self._e = eng
        self._wmax = {}

    def wait_ge(self, sem, val):
        key = id(sem)
        if val <= self._wmax.get(key, -1):
            return
        self._wmax[key] = val
        self._e.wait_ge(sem, val)

    def __getattr__(self, n):
        f = getattr(self._e, n)
        if n in _AD._OPS:
            def wrapped(*a, **k):
                r = f(*a, **k)
                self._e.drain()
                return r
            return wrapped
        return f


def build(T=8192, DFF=8192, phases=(1, 2, 3), dbg=()):
    TOK = T // NCORE
    NT = T // 128
    NG = NT // 2
    QR = T // 512
    NP = 2 * QR
    NFT = DFF // 128
    NSLAB = DFF // 512
    TOKT = TOK // 128
    TN = min(512, TOK)
    NTH = TOK // TN
    W1S = DFF // 8
    W2S = D // 8
    QW = min(4096, DFF)
    QBW = max(D, QW)
    assert T % 1024 == 0 and DFF % 512 == 0 and TOK % 128 == 0

    nc = bass.Bass("TRN2", target_bir_lowering=False)

    x_full = nc.dram_tensor("x_full", [T, D], F32, kind="ExternalInput").ap()
    x_own = nc.dram_tensor("x_own", [TOK, D], F32, kind="ExternalInput").ap()
    wq_in = nc.dram_tensor("wq_in", [D, FOWN], F32, kind="ExternalInput").ap()
    wk_in = nc.dram_tensor("wk_in", [D, FOWN], F32, kind="ExternalInput").ap()
    wv_in = nc.dram_tensor("wv_in", [D, FOWN], F32, kind="ExternalInput").ap()
    bq_in = nc.dram_tensor("bq_in", [128, 2], F32, kind="ExternalInput").ap()
    bk_in = nc.dram_tensor("bk_in", [128, 2], F32, kind="ExternalInput").ap()
    bv_in = nc.dram_tensor("bv_in", [128, FOWN], F32, kind="ExternalInput").ap()
    wp_in = nc.dram_tensor("wp_in", [FOWN, D], F32, kind="ExternalInput").ap()
    bp_in = nc.dram_tensor("bp_in", [128, D], F32, kind="ExternalInput").ap()
    b2_in = nc.dram_tensor("b2_in", [128, D], F32, kind="ExternalInput").ap()
    b1_in = nc.dram_tensor("b1_in", [128, NFT], F32, kind="ExternalInput").ap()
    w1_sh = nc.dram_tensor("w1_sh", [W1S, D], F32, kind="ExternalInput").ap()
    w2_sh = nc.dram_tensor("w2_sh", [W2S, DFF], F32, kind="ExternalInput").ap()
    id_in = nc.dram_tensor("id_in", [128, 128], BF16, kind="ExternalInput").ap()
    mk_in = nc.dram_tensor("mk_in", [4, 128, 512], F32, kind="ExternalInput").ap()

    out_own = nc.dram_tensor("out_own", [TOK, D], F32, kind="ExternalOutput").ap()

    qT_hi = nc.dram_tensor("qT_hi", [FOWN, T], BF16).ap()
    qT_lo = nc.dram_tensor("qT_lo", [FOWN, T], BF16).ap()
    kT_hi = nc.dram_tensor("kT_hi", [FOWN, T], BF16).ap()
    kT_lo = nc.dram_tensor("kT_lo", [FOWN, T], BF16).ap()
    v_sh = nc.dram_tensor("v_sh", [T, FOWN], BF16).ap()
    v_sl = nc.dram_tensor("v_sl", [T, FOWN], BF16).ap()
    proj_part = nc.dram_tensor("proj_part", [T, D], F32).ap()
    rs_out = nc.dram_tensor("rs_out", [TOK, D], F32).ap()
    ffn_acc = nc.dram_tensor("ffn_acc", [TOK, D], F32).ap()
    sc_part = nc.dram_tensor("sc_part", [128, 2], F32).ap()
    sc_sum = nc.dram_tensor("sc_sum", [128, 2], F32).ap()
    wq1_shd = nc.dram_tensor("wq1_shd", [D, W1S], BF16).ap()
    wq2_shd = nc.dram_tensor("wq2_shd", [DFF, W2S], BF16).ap()
    wq1_all = nc.dram_tensor("wq1_all", [8 * D, W1S], BF16).ap()
    wq2_all = nc.dram_tensor("wq2_all", [8 * DFF, W2S], BF16).ap()

    dbg_map = {"qT_hi": qT_hi, "qT_lo": qT_lo, "kT_hi": kT_hi, "kT_lo": kT_lo,
               "v_sh": v_sh, "v_sl": v_sl, "proj_part": proj_part,
               "rs_out": rs_out, "wq1_all": wq1_all, "wq2_all": wq2_all,
               "sc_sum": sc_sum, "ffn_acc": ffn_acc}
    dbg_outs = []
    for name in dbg:
        src = dbg_map[name]
        dst = nc.dram_tensor("dbg_" + name, list(src.shape), src.dtype,
                             kind="ExternalOutput").ap()
        dbg_outs.append((dst, src))

    qtiles = []
    for i in range(W1S // 128):
        dstT = bass.AP(wq1_shd.tensor, i * 128,
                       [[W1S, 128], [128 * W1S, D // 128], [1, 128]])
        qtiles.append((w1_sh[i * 128:(i + 1) * 128, :], dstT, D, 0))
    for i in range(W2S // 128):
        for hf in range(DFF // QW):
            dstT = bass.AP(wq2_shd.tensor, hf * QW * W2S + i * 128,
                           [[W2S, 128], [128 * W2S, QW // 128], [1, 128]])
            qtiles.append((w2_sh[i * 128:(i + 1) * 128, hf * QW:(hf + 1) * QW],
                           dstT, QW, 1))
    NQT = len(qtiles)
    qe_base = [0]
    for qi in range(NQT):
        qe_base.append(qe_base[-1] + qtiles[qi][2] // 1024)
    qt_base = qe_base

    kt_cnt = [4 * (p // 2 + 1) for p in range(NP)]
    kt_base = [0]
    for p in range(NP):
        kt_base.append(kt_base[-1] + kt_cnt[p])
    NKT = kt_base[-1]
    qk_cnt = [2 + 4 * (p // 2 + 1) for p in range(NP)]
    qk_base = [0]
    for p in range(NP):
        qk_base.append(qk_base[-1] + qk_cnt[p])

    st = ExitStack()
    with st:
        semd = {}
        for n in ["ld", "pq", "pt", "cc", "qd",
                  "x", "a", "b", "c", "d", "e", "f", "tp", "te",
                  "mm", "ev", "hi", "lo", "dq",
                  "qk", "Sf", "Pf", "Pv", "Vf", "Of", "dn", "rr",
                  "rb", "rc", "no", "oh", "pj", "pe", "pd",
                  "xo", "la", "lb", "lc", "ldd", "le", "lf", "t3", "e3",
                  "w1", "w2", "u1", "f1", "g1", "h", "f2", "fe", "f2d",
                  "f2d0", "f2d1", "f2d2", "f2d3", "f2d4", "f2d5", "f2d6", "f2d7",
                  "r3", "r3b", "o3", "stq", "qA", "qB", "kA", "kB", "w1b", "w2b", "qt", "qe",
                  "qla", "qlb", "qsa", "qsb"]:
            semd[n] = st.enter_context(nc.semaphore("s_" + n))
        S = semd
        QLS = [S["qla"], S["qlb"]]
        QSS = [S["qsa"], S["qsb"]]
        QAT = [(NQT + 1) // 2, NQT // 2]

        ident = st.enter_context(nc.sbuf_tensor("ident", [128, 128], BF16))
        ones_c = st.enter_context(nc.sbuf_tensor("ones_c", [128, 1], BF16))
        ones_sq = st.enter_context(nc.sbuf_tensor("ones_sq", [128, 128], BF16))
        ones_k1 = st.enter_context(nc.sbuf_tensor("ones_k1", [1, 128], BF16))
        bqs = st.enter_context(nc.sbuf_tensor("bqs", [128, 2], F32))
        bk_sb = st.enter_context(nc.sbuf_tensor("bk_sb", [128, 2], F32))
        bv_sb = st.enter_context(nc.sbuf_tensor("bv_sb", [128, FOWN], F32))
        thrp = st.enter_context(nc.sbuf_tensor("thrp", [128, 2], F32))
        thrn = st.enter_context(nc.sbuf_tensor("thrn", [128, 2], F32))

        with ExitStack() as _esw:
            wq_h = _esw.enter_context(nc.sbuf_tensor("wq_h", [128, 16 * FOWN], BF16))
            wq_l = _esw.enter_context(nc.sbuf_tensor("wq_l", [128, 16 * FOWN], BF16))
            wk_h = _esw.enter_context(nc.sbuf_tensor("wk_h", [128, 16 * FOWN], BF16))
            wk_l = _esw.enter_context(nc.sbuf_tensor("wk_l", [128, 16 * FOWN], BF16))
            wv_h = _esw.enter_context(nc.sbuf_tensor("wv_h", [128, 16 * FOWN], BF16))
            wv_l = _esw.enter_context(nc.sbuf_tensor("wv_l", [128, 16 * FOWN], BF16))
            with ExitStack() as _es:
                wbig = _es.enter_context(nc.sbuf_tensor("wbig", [128, 4096], F32))
                qbuf = _es.enter_context(nc.sbuf_tensor("qbuf", [128, 2 * QBW], F32))
                qtmp = _es.enter_context(nc.sbuf_tensor("qtmp", [128, QBW], BF16))
                qout = _es.enter_context(nc.sbuf_tensor("qout", [128, 2 * QBW], BF16))
                qstage = _es.enter_context(nc.sbuf_tensor("qstage", [128, 2 * QBW], BF16))
                acc2 = _es.enter_context(nc.sbuf_tensor("acc2", [128, 2], F32))
                rtmp = _es.enter_context(nc.sbuf_tensor("rtmp", [128, 1], F32))
                ac_h = _es.enter_context(nc.sbuf_tensor("ac_h", [128, 2], BF16))
                ac_l = _es.enter_context(nc.sbuf_tensor("ac_l", [128, 2], BF16))
                tpq0 = _es.enter_context(nc.psum_tensor("tpq0", [128, 1024], BF16))
                tpq1 = _es.enter_context(nc.psum_tensor("tpq1", [128, 1024], BF16))
                pp0 = _es.enter_context(nc.psum_tensor("pp0", [128, 512], F32))
                b0 = _es.enter_context(nc.Block())
                @b0.sync
                def _(s):
                    s = _AD(s)
                    s.dma_start(out=ident[:, :], in_=id_in[:, :]).then_inc(S["ld"], 16)
                    s.dma_start(out=bqs[:, :], in_=bq_in[:, :]).then_inc(S["ld"], 16)
                    s.dma_start(out=bk_sb[:, :], in_=bk_in[:, :]).then_inc(S["ld"], 16)
                    s.dma_start(out=bv_sb[:, :], in_=bv_in[:, :]).then_inc(S["ld"], 16)
                    for wi, win in enumerate([wq_in, wk_in, wv_in]):
                        s.wait_ge(S["ld"], 16 * (4 + wi))
                        s.wait_ge(S["pq"], 2 * wi)
                        s.dma_start(
                            out=wbig[:, 0:16 * FOWN],
                            in_=bass.AP(win.tensor, 0,
                                        [[FOWN, 128], [128 * FOWN, 16], [1, FOWN]]),
                        ).then_inc(S["ld"], 16)

                @b0.vector
                def _(v):
                    v = _AD(v)
                    v.memset(ones_c[:, :], 1.0)
                    v.memset(ones_sq[:, :], 1.0)
                    v.memset(ones_k1[:, :], 1.0)
                    v.memset(acc2[:, :], 0.0)
                    v.wait_ge(S["ld"], 16 * 4)
                    v.tensor_scalar_mul(bqs[:, :], bqs[:, :], float(QS))
                    for wi, (dh, dl) in enumerate([(wq_h, wq_l), (wk_h, wk_l),
                                                   (wv_h, wv_l)]):
                        v.wait_ge(S["ld"], 16 * (5 + wi))
                        v.tensor_copy(dh[:, :], wbig[:, 0:16 * FOWN])
                        v.tensor_sub(dl[:, :], wbig[:, 0:16 * FOWN],
                                     dh[:, :]).then_inc(S["pq"], 2)
                    for qi in range(NQT):
                        w, col = qtiles[qi][2], qtiles[qi][3]
                        v.wait_ge(QLS[qi % 2], 16 * (qi // 2 + 1))
                        v.reduce_sum(rtmp[:, :], qbuf[:, (qi % 2) * QBW:(qi % 2) * QBW + w],
                                     axis=AX.X, apply_absolute_value=True)
                        v.tensor_add(acc2[:, col:col + 1], acc2[:, col:col + 1],
                                     rtmp[:, :]).then_inc(S["pq"], 1)
                    v.tensor_copy(ac_h[:, :], acc2[:, :])
                    v.tensor_sub(ac_l[:, :], acc2[:, :],
                                 ac_h[:, :]).then_inc(S["pq"], 1)
                    v.wait_ge(S["qd"], 16 * 2)
                    v.tensor_scalar_mul(thrp[:, :], thrp[:, :],
                                        float(0.7 / (DFF * D)))
                    v.tensor_scalar_mul(thrn[:, :], thrp[:, :],
                                        -1.0).then_inc(S["pq"], 1)
                    for qi in range(NQT):
                        w, col = qtiles[qi][2], qtiles[qi][3]
                        v.wait_ge(QLS[qi % 2], 16 * (QAT[qi % 2] + qi // 2 + 1))
                        if qi >= 2:
                            v.wait_ge(S["qt"], qt_base[qi - 1])
                        qb = qbuf[:, (qi % 2) * QBW:(qi % 2) * QBW + w]
                        qo = qout[:, (qi % 2) * QBW:(qi % 2) * QBW + w]
                        v.tensor_scalar(qtmp[:, 0:w], qb,
                                        thrp[:, col:col + 1], None, OP.is_ge)
                        v.tensor_scalar(qo, qb,
                                        thrn[:, col:col + 1], None, OP.is_le)
                        v.tensor_sub(qo, qtmp[:, 0:w], qo).then_inc(S["pq"], 1)

                @b0.tensor
                def _(t):
                    t = _AD(t)
                    t.wait_ge(S["ld"], 64)
                    t.wait_ge(S["pq"], 6 + NQT + 1)
                    t.matmul(pp0[:, 0:2], ones_sq[:, :], ac_h[:, :],
                             start=True, stop=False)
                    t.matmul(pp0[:, 0:2], ones_sq[:, :], ac_l[:, :],
                             start=False, stop=True).then_inc(S["pt"], 1)
                    fl = 0
                    for qi in range(NQT):
                        w = qtiles[qi][2]
                        t.wait_ge(S["pq"], 6 + NQT + 2 + qi + 1)
                        for f in range(w // 1024):
                            if fl >= 2:
                                t.wait_ge(S["qe"], fl - 1)
                            bank = [tpq0, tpq1][fl % 2]
                            for ii in range(8):
                                k = f * 8 + ii
                                mmq = t.transpose(
                                    bank[:, ii * 128:(ii + 1) * 128],
                                    qout[:, (qi % 2) * QBW + k * 128:
                                         (qi % 2) * QBW + (k + 1) * 128],
                                    ident[:, :])
                            mmq.then_inc(S["qt"], 1)
                            fl += 1

                @b0.scalar
                def _(sc):
                    sc = _AD(sc)
                    sc.wait_ge(S["pt"], 1)
                    sc.copy(acc2[:, :], pp0[:, 0:2]).then_inc(S["pt"], 1)
                    fl = 0
                    for qi in range(NQT):
                        w = qtiles[qi][2]
                        if qi >= 2:
                            sc.wait_ge(QSS[qi % 2], 16 * (qi // 2))
                        for f in range(w // 1024):
                            sc.wait_ge(S["qt"], fl + 1)
                            sc.copy(qstage[:, (qi % 2) * QBW + f * 1024:
                                            (qi % 2) * QBW + (f + 1) * 1024],
                                    [tpq0, tpq1][fl % 2][:, :]).then_inc(S["qe"], 1)
                            fl += 1

                @b0.gpsimd
                def _(g):
                    g = _AD(g)
                    for qi in range(NQT):
                        w = qtiles[qi][2]
                        if qi >= 2:
                            g.wait_ge(QLS[qi % 2], 16 * (qi // 2))
                            g.wait_ge(S["pq"], 6 + qi - 1)
                        g.dma_start(out=qbuf[:, (qi % 2) * QBW:(qi % 2) * QBW + w],
                                    in_=qtiles[qi][0]).then_inc(QLS[qi % 2], 16)
                    g.wait_ge(S["pt"], 2)
                    g.dma_start(out=sc_part[:, :], in_=acc2[:, :]).then_inc(S["qd"], 16)
                    g.wait_ge(S["qd"], 16)
                    g.collective_compute(
                        "AllReduce", OP.add, replica_groups=[list(range(NCORE))],
                        ins=[sc_part.tensor.ap().opt()],
                        outs=[sc_sum.tensor.ap().opt()]).then_inc(S["cc"], 1)
                    g.wait_ge(S["cc"], 1)
                    g.dma_start(out=thrp[:, :], in_=sc_sum[:, :]).then_inc(S["qd"], 16)
                    def _storeB(g, qi):
                        w = qtiles[qi][2]
                        if qi >= 2:
                            g.wait_ge(QSS[qi % 2], 16 * (qi // 2))
                        g.wait_ge(S["qe"], qe_base[qi + 1])
                        g.dma_start(out=qtiles[qi][1],
                                    in_=qstage[:, (qi % 2) * QBW:(qi % 2) * QBW + w]
                                    ).then_inc(QSS[qi % 2], 16)
                    for qi in range(NQT):
                        w = qtiles[qi][2]
                        g.wait_ge(QLS[qi % 2], 16 * (QAT[qi % 2] + qi // 2))
                        g.wait_ge(S["pq"], 6 + NQT + 2 + qi)
                        g.dma_start(out=qbuf[:, (qi % 2) * QBW:(qi % 2) * QBW + w],
                                    in_=qtiles[qi][0]).then_inc(QLS[qi % 2], 16)
                        if qi >= 2:
                            _storeB(g, qi - 2)
                    _storeB(g, NQT - 2)
                    _storeB(g, NQT - 1)
                    g.wait_ge(QSS[0], 16 * ((NQT + 1) // 2))
                    g.wait_ge(QSS[1], 16 * (NQT // 2))
                    g.collective_compute(
                        "AllGather", OP.bypass, replica_groups=[list(range(NCORE))],
                        ins=[wq1_shd.tensor.ap().opt()],
                        outs=[wq1_all.tensor.ap().opt()]).then_inc(S["cc"], 1)
                    g.wait_ge(S["cc"], 2)
                    g.collective_compute(
                        "AllGather", OP.bypass, replica_groups=[list(range(NCORE))],
                        ins=[wq2_shd.tensor.ap().opt()],
                        outs=[wq2_all.tensor.ap().opt()]).then_inc(S["cc"], 1)

            if 1 in phases:
                with ExitStack() as _es:
                    xg = _es.enter_context(nc.sbuf_tensor("xg", [128, 3 * D], F32))
                    sqb = _es.enter_context(nc.sbuf_tensor("sqb", [128, D], F32))
                    xn32 = _es.enter_context(nc.sbuf_tensor("xn32", [128, 2 * D], F32))
                    xnh = _es.enter_context(nc.sbuf_tensor("xnh", [128, 2 * D], BF16))
                    xnl = _es.enter_context(nc.sbuf_tensor("xnl", [128, 2 * D], BF16))
                    s1 = _es.enter_context(nc.sbuf_tensor("s1", [128, 2], F32))
                    s2 = _es.enter_context(nc.sbuf_tensor("s2", [128, 2], F32))
                    nm = _es.enter_context(nc.sbuf_tensor("nm", [128, 2], F32))
                    vp = _es.enter_context(nc.sbuf_tensor("vp", [128, 2], F32))
                    sr = _es.enter_context(nc.sbuf_tensor("sr", [128, 2], F32))
                    rt1 = _es.enter_context(nc.sbuf_tensor("rt1", [128, 2], F32))
                    rstd = _es.enter_context(nc.sbuf_tensor("rstd", [128, 2], F32))
                    bcol = _es.enter_context(nc.sbuf_tensor("bcol", [128, 2], F32))
                    stg = _es.enter_context(nc.sbuf_tensor("stg", [128, 2 * 16 * 512], BF16))
                    evt = _es.enter_context(nc.sbuf_tensor("evt", [128, 2 * FOWN], F32))
                    ohb = _es.enter_context(nc.sbuf_tensor("ohb", [128, 2 * FOWN], BF16))
                    olb = _es.enter_context(nc.sbuf_tensor("olb", [128, 2 * FOWN], BF16))
                    tp0 = _es.enter_context(nc.psum_tensor("tp0", [128, 1024], BF16))
                    tp1 = _es.enter_context(nc.psum_tensor("tp1", [128, 1024], BF16))
                    mp0 = _es.enter_context(nc.psum_tensor("mp0", [128, 512], F32))
                    mp1 = _es.enter_context(nc.psum_tensor("mp1", [128, 512], F32))
                    b1 = _es.enter_context(nc.Block())
                    tpb = [tp0, tp1]
                    mpb = [mp0, mp1]

                    @b1.sync
                    def _(s):
                        s = _AD(s)
                        for t in range(NT):
                            if t >= 1:
                                s.wait_ge(S["x"], 16 * t)
                            if t >= 3:
                                s.wait_ge(S["a"], t - 2)
                                s.wait_ge(S["d"], t - 2)
                            s.dma_start(
                                out=xg[:, (t % 3) * D:(t % 3 + 1) * D],
                                in_=x_full[t * 128:(t + 1) * 128, :],
                            ).then_inc(S["x"], 16)

                    def fill_dst(m):
                        gq, slot = divmod(m, 6)
                        if slot < 4:
                            ft = slot
                            dsth = (qT_hi if ft < 2 else kT_hi)
                            dstl = (qT_lo if ft < 2 else kT_lo)
                            r0 = (ft % 2) * 128
                            return (dsth[r0:r0 + 128, gq * 256:(gq + 1) * 256],
                                    dstl[r0:r0 + 128, gq * 256:(gq + 1) * 256],
                                    FOWN)
                        tt = slot - 4
                        r0 = (2 * gq + tt) * 128
                        return (v_sh[r0:r0 + 128, :], v_sl[r0:r0 + 128, :], FOWN)

                    @b1.scalar
                    def _(sc):
                        sc = _AD(sc)
                        for i in range(NT + 4):
                            t = i
                            if t < NT:
                                sc.wait_ge(S["x"], 16 * (t + 1))
                                if t >= 2:
                                    sc.wait_ge(S["b"], t - 1)
                                sc.activation(sqb[:, :], xg[:, (t % 3) * D:(t % 3 + 1) * D],
                                              AF.Identity, bias=f32(0.0),
                                              accum_out=s1[:, t % 2:t % 2 + 1])
                                sc.activation(sqb[:, :], xg[:, (t % 3) * D:(t % 3 + 1) * D],
                                              AF.Square, bias=f32(0.0),
                                              accum_out=s2[:, t % 2:t % 2 + 1]
                                              ).then_inc(S["a"], 1)
                            t = i - 1
                            if 0 <= t < NT:
                                sc.wait_ge(S["b"], t + 1)
                                if t >= 2:
                                    sc.wait_ge(S["d"], t - 1)
                                sc.activation(sr[:, t % 2:t % 2 + 1],
                                              vp[:, t % 2:t % 2 + 1],
                                              AF.Sqrt, bias=f32(0.0)
                                              ).then_inc(S["c"], 1)
                            t = i - 2
                            if 0 <= t < NT:
                                sc.wait_ge(S["d"], t + 1)
                                if t >= 2:
                                    sc.wait_ge(S["tp"], 4 * (t - 1))
                                sc.copy(xnh[:, (t % 2) * D:(t % 2 + 1) * D],
                                        xn32[:, (t % 2) * D:(t % 2 + 1) * D]
                                        ).then_inc(S["e"], 1)
                            t = i - 3
                            if 0 <= t < NT:
                                gq = t // 2
                                for b4 in range(4):
                                    fl = 4 * t + b4
                                    sc.wait_ge(S["tp"], fl + 1)
                                    if gq >= 2:
                                        sc.wait_ge(S["mm"], 6 * (gq - 1))
                                    dst = bass.AP(
                                        stg,
                                        (gq % 2) * (16 * 512) + b4 * 4 * 512 + (t % 2) * 128,
                                        [[2 * 16 * 512, 128], [512, 4], [256, 2], [1, 128]],
                                    )
                                    sc.copy(dst, tpb[fl % 2][:, :]).then_inc(S["te"], 1)
                            j = i - 4
                            if 0 <= j < NT:
                                gq, half = divmod(j, 2)
                                for m in range(6 * gq + 3 * half, 6 * gq + 3 * half + 3):
                                    sc.wait_ge(S["ev"], m + 1)
                                    if m >= 2:
                                        sc.wait_ge(S["dq"], 32 * (m - 1))
                                    sc.copy(ohb[:, (m % 2) * FOWN:(m % 2 + 1) * FOWN],
                                            evt[:, (m % 2) * FOWN:(m % 2 + 1) * FOWN]
                                            ).then_inc(S["hi"], 1)

                    @b1.vector
                    def _(v):
                        v = _AD(v)
                        for i in range(NT + 4):
                            t = i
                            if t < NT:
                                v.wait_ge(S["a"], t + 1)
                                if t >= 2:
                                    v.wait_ge(S["c"], t - 1)
                                ts = slice(t % 2, t % 2 + 1)
                                v.tensor_scalar_mul(nm[:, ts], s1[:, ts], -1.0 / D)
                                v.tensor_scalar_mul(vp[:, ts], s2[:, ts], 1.0 / D)
                                v.tensor_mul(rt1[:, ts], nm[:, ts], nm[:, ts])
                                v.tensor_sub(vp[:, ts], vp[:, ts], rt1[:, ts])
                                v.tensor_scalar_add(vp[:, ts], vp[:, ts],
                                                    float(EPS)).then_inc(S["b"], 1)
                            t = i - 1
                            if 0 <= t < NT:
                                v.wait_ge(S["c"], t + 1)
                                if t >= 2:
                                    v.wait_ge(S["e"], t - 1)
                                ts = slice(t % 2, t % 2 + 1)
                                v.reciprocal(rstd[:, ts], sr[:, ts])
                                v.tensor_mul(rt1[:, ts], rstd[:, ts], rstd[:, ts])
                                v.tensor_mul(rt1[:, ts], rt1[:, ts], vp[:, ts])
                                v.tensor_scalar(rt1[:, ts], rt1[:, ts],
                                                -0.5, 1.5, OP.mult, OP.add)
                                v.tensor_mul(rstd[:, ts], rstd[:, ts], rt1[:, ts])
                                v.tensor_mul(bcol[:, ts], nm[:, ts], rstd[:, ts])
                                v.scalar_tensor_tensor(
                                    xn32[:, (t % 2) * D:(t % 2 + 1) * D],
                                    xg[:, (t % 3) * D:(t % 3 + 1) * D],
                                    rstd[:, ts],
                                    bcol[:, ts].broadcast_to([128, D]),
                                    OP.mult, OP.add).then_inc(S["d"], 1)
                            t = i - 2
                            if 0 <= t < NT:
                                v.wait_ge(S["e"], t + 1)
                                if t >= 2:
                                    v.wait_ge(S["tp"], 4 * (t - 1))
                                v.tensor_sub(xnl[:, (t % 2) * D:(t % 2 + 1) * D],
                                             xn32[:, (t % 2) * D:(t % 2 + 1) * D],
                                             xnh[:, (t % 2) * D:(t % 2 + 1) * D]
                                             ).then_inc(S["f"], 1)
                            j = i - 4
                            if 0 <= j < NT:
                                gq, half = divmod(j, 2)
                                for m in range(6 * gq + 3 * half, 6 * gq + 3 * half + 3):
                                    slot = m % 6
                                    v.wait_ge(S["mm"], m + 1)
                                    if m >= 2:
                                        v.wait_ge(S["hi"], m - 1)
                                    if slot < 2:
                                        v.scalar_tensor_tensor(
                                            evt[:, (m % 2) * FOWN:(m % 2 + 1) * FOWN],
                                            mpb[m % 2][:, 0:FOWN], float(QS),
                                            bqs[:, slot:slot + 1].broadcast_to([128, FOWN]),
                                            OP.mult, OP.add).then_inc(S["ev"], 1)
                                    elif slot < 4:
                                        v.scalar_tensor_tensor(
                                            evt[:, (m % 2) * FOWN:(m % 2 + 1) * FOWN],
                                            mpb[m % 2][:, 0:FOWN], 1.0,
                                            bk_sb[:, slot - 2:slot - 1].broadcast_to([128, FOWN]),
                                            OP.mult, OP.add).then_inc(S["ev"], 1)
                                    else:
                                        v.tensor_add(
                                            evt[:, (m % 2) * FOWN:(m % 2 + 1) * FOWN],
                                            mpb[m % 2][:, 0:FOWN],
                                            bv_sb[:, :]).then_inc(S["ev"], 1)
                                    v.wait_ge(S["hi"], m + 1)
                                    if m >= 2:
                                        v.wait_ge(S["dq"], 32 * (m - 1))
                                    v.tensor_sub(
                                        olb[:, (m % 2) * FOWN:(m % 2 + 1) * FOWN],
                                        evt[:, (m % 2) * FOWN:(m % 2 + 1) * FOWN],
                                        ohb[:, (m % 2) * FOWN:(m % 2 + 1) * FOWN]
                                        ).then_inc(S["lo"], 1)

                    @b1.tensor
                    def _(t_):
                        t_ = _AD(t_)
                        for i in range(NT):
                            t = i
                            t_.wait_ge(S["f"], t + 1)
                            for b4 in range(4):
                                fl = 4 * t + b4
                                if fl >= 2:
                                    t_.wait_ge(S["te"], fl - 1)
                                for ii in range(8):
                                    idx = b4 * 8 + ii
                                    k, hl = divmod(idx, 2)
                                    src = (xnh if hl == 0 else xnl)
                                    mm = t_.transpose(
                                        tpb[fl % 2][:, ii * 128:(ii + 1) * 128],
                                        src[:, (t % 2) * D + k * 128:(t % 2) * D + (k + 1) * 128],
                                        ident[:, :])
                                mm.then_inc(S["tp"], 1)
                            if t % 2 == 1:
                                gq = t // 2
                                t_.wait_ge(S["te"], 8 * gq + 8)
                                for slot in range(6):
                                    m = 6 * gq + slot
                                    if m >= 2:
                                        t_.wait_ge(S["ev"], m - 1)
                                    ps = mpb[m % 2]
                                    if slot < 4:
                                        wsh = wq_h if slot < 2 else wk_h
                                        wsl = wq_l if slot < 2 else wk_l
                                        jj = (slot % 2) * 128
                                        for k in range(16):
                                            sh = wsh[:, k * FOWN + jj:k * FOWN + jj + 128]
                                            sl = wsl[:, k * FOWN + jj:k * FOWN + jj + 128]
                                            xh = stg[:, (gq % 2) * 16 * 512 + k * 512:
                                                     (gq % 2) * 16 * 512 + k * 512 + 256]
                                            xl = stg[:, (gq % 2) * 16 * 512 + k * 512 + 256:
                                                     (gq % 2) * 16 * 512 + k * 512 + 512]
                                            t_.matmul(ps[:, 0:FOWN], sh, xh,
                                                      start=(k == 0), stop=False)
                                            t_.matmul(ps[:, 0:FOWN], sl, xh,
                                                      start=False, stop=False)
                                            mm = t_.matmul(ps[:, 0:FOWN], sh, xl,
                                                           start=False, stop=(k == 15))
                                        mm.then_inc(S["mm"], 1)
                                    else:
                                        tt = slot - 4
                                        for k in range(16):
                                            base = (gq % 2) * 16 * 512 + k * 512
                                            sh = stg[:, base + tt * 128:base + tt * 128 + 128]
                                            sl = stg[:, base + 256 + tt * 128:base + 256 + tt * 128 + 128]
                                            mh = wv_h[:, k * FOWN:(k + 1) * FOWN]
                                            ml = wv_l[:, k * FOWN:(k + 1) * FOWN]
                                            t_.matmul(ps[:, 0:FOWN], sh, mh,
                                                      start=(k == 0), stop=False)
                                            t_.matmul(ps[:, 0:FOWN], sl, mh,
                                                      start=False, stop=False)
                                            mm = t_.matmul(ps[:, 0:FOWN], sh, ml,
                                                           start=False, stop=(k == 15))
                                        mm.then_inc(S["mm"], 1)

                    @b1.gpsimd
                    def _(g):
                        g = _AD(g)
                        for m in range(6 * NG):
                            dsth, dstl, wdt = fill_dst(m)
                            if m >= 1:
                                g.wait_ge(S["dq"], 32 * m)
                            g.wait_ge(S["hi"], m + 1)
                            g.dma_start(out=dsth,
                                        in_=ohb[:, (m % 2) * FOWN:(m % 2 + 1) * FOWN]
                                        ).then_inc(S["dq"], 16)
                            g.wait_ge(S["lo"], m + 1)
                            g.dma_start(out=dstl,
                                        in_=olb[:, (m % 2) * FOWN:(m % 2 + 1) * FOWN]
                                        ).then_inc(S["dq"], 16)

        if 3 in phases:
            _ln_scope = _phase3_ln(nc, S, locals())
        if 2 in phases:
            _phase2(nc, S, locals())
        if 3 in phases:
            _phase3_ffn(nc, S, _ln_scope, locals())

        if dbg_outs:
            totals = {"qd": 32, "cc": 3}
            if 1 in phases:
                totals["dq"] = 16 * 12 * NG
            if 2 in phases:
                totals["pd"] = 16 * 16 * QR
                totals["cc"] = 4
            if 3 in phases:
                for _tt in range(TOKT):
                    totals["f2d%d" % _tt] = 16 * NSLAB
                totals["stq"] = 16 * TOKT
            with nc.Block() as bd:
                @bd.gpsimd
                def _(g):
                    for nm_, tot in totals.items():
                        g.wait_ge(S[nm_], tot)
                    n0 = 0
                    for dst, src_ in dbg_outs:
                        g.dma_start(out=dst[...], in_=src_[...]).then_inc(S["qd"], 16)
                        n0 += 16
                    g.wait_ge(S["qd"], totals["qd"] + n0)

    return nc


def _phase2(nc, S, env):
    T, QR, NP = env["T"], env["QR"], env["NP"]
    NG = env["NG"]
    kt_base = env["kt_base"]; kt_cnt = env["kt_cnt"]
    qk_base = env["qk_base"]
    qT_hi, qT_lo, kT_hi, kT_lo = env["qT_hi"], env["qT_lo"], env["kT_hi"], env["kT_lo"]
    v_sh, v_sl = env["v_sh"], env["v_sl"]
    wp_in, mk_in = env["wp_in"], env["mk_in"]
    proj_part, rs_out = env["proj_part"], env["rs_out"]
    ones_sq = env["ones_sq"]
    phases = env["phases"]
    dq_tot = 16 * 12 * NG if 1 in phases else 0

    with ExitStack() as _es:
        masks = _es.enter_context(nc.sbuf_tensor("masks", [128, 4 * 512], F32))
        wpb = _es.enter_context(nc.sbuf_tensor("wpb", [128, 2 * D], F32))
        wp_h = _es.enter_context(nc.sbuf_tensor("wp_h", [128, 2 * D], BF16))
        wp_l = _es.enter_context(nc.sbuf_tensor("wp_l", [128, 2 * D], BF16))
        qc_h = _es.enter_context(nc.sbuf_tensor("qc_h", [128, 2 * 512], BF16))
        qc_l = _es.enter_context(nc.sbuf_tensor("qc_l", [128, 2 * 512], BF16))
        kc_h = _es.enter_context(nc.sbuf_tensor("kc_h", [128, 2 * 512], BF16))
        kc_l = _es.enter_context(nc.sbuf_tensor("kc_l", [128, 2 * 512], BF16))
        vc_h = _es.enter_context(nc.sbuf_tensor("vc_h", [128, 2 * 512], BF16))
        vc_l = _es.enter_context(nc.sbuf_tensor("vc_l", [128, 2 * 512], BF16))
        P32 = _es.enter_context(nc.sbuf_tensor("P32", [128, 3 * 512], F32))
        Ph = _es.enter_context(nc.sbuf_tensor("Ph", [128, 3 * 512], BF16))
        Pl = _es.enter_context(nc.sbuf_tensor("Pl", [128, 3 * 512], BF16))
        rb_sb = _es.enter_context(nc.sbuf_tensor("rb_sb", [128, 2 * 512], F32))
        On32 = _es.enter_context(nc.sbuf_tensor("On32", [128, 2 * 512], F32))
        Oh = _es.enter_context(nc.sbuf_tensor("Oh", [128, 2 * 512], BF16))
        Ol = _es.enter_context(nc.sbuf_tensor("Ol", [128, 2 * 512], BF16))
        pj_sb = _es.enter_context(nc.sbuf_tensor("pj_sb", [128, 2 * 512], F32))
        sp0 = _es.enter_context(nc.psum_tensor("sp0", [128, 512], F32))
        sp1 = _es.enter_context(nc.psum_tensor("sp1", [128, 512], F32))
        sp2 = _es.enter_context(nc.psum_tensor("sp2", [128, 512], F32))
        Op0 = _es.enter_context(nc.psum_tensor("Op0", [128, 512], F32))
        Op1 = _es.enter_context(nc.psum_tensor("Op1", [128, 512], F32))
        den = _es.enter_context(nc.psum_tensor("den", [128, 512], F32))
        pj0 = _es.enter_context(nc.psum_tensor("pj0", [128, 512], F32))
        pj1 = _es.enter_context(nc.psum_tensor("pj1", [128, 512], F32))
        b2 = _es.enter_context(nc.Block())
        spb = [sp0, sp1, sp2]
        Opb = [Op0, Op1]
        pjb = [pj0, pj1]

        qslot_use = {}
        kslot_use = {}
        kslot_prev_end = {}
        _qcnt = [0, 0]
        _kcnt = [0, 0]
        _klast = [None, None]
        for p in range(NP):
            _qcnt[p % 2] += 1
            qslot_use[p] = _qcnt[p % 2]
            for ck in range(p // 2 + 1):
                _kcnt[ck % 2] += 1
                kslot_use[(p, ck)] = _kcnt[ck % 2]
                kslot_prev_end[(p, ck)] = _klast[ck % 2]
                _klast[ck % 2] = kt_base[p] + 4 * ck + 4
        QSL = [S["qA"], S["qB"]]
        KSL = [S["kA"], S["kB"]]

        @b2.sync
        def _(s):
            s = _AD(s)
            s.wait_ge(S["dq"], dq_tot)
            for j in range(4):
                s.dma_start(out=masks[:, j * 512:(j + 1) * 512],
                            in_=mk_in[j, :, :]).then_inc(S["ld"], 16)
            s.dma_start(out=wpb[:, :],
                        in_=bass.AP(wp_in.tensor, 0,
                                    [[D, 128], [128 * D, 2], [1, D]])
                        ).then_inc(S["ld"], 16)
            s.wait_ge(S["ld"], 16 * 12)
            for p in range(NP):
                qr, h = divmod(p, 2)
                r0 = h * 128
                qs = QSL[p % 2]
                if qslot_use[p] >= 2:
                    s.wait_ge(qs, 32 * (qslot_use[p] - 1))
                if p >= 2:
                    s.wait_ge(S["Sf"], kt_base[p - 1])
                s.dma_start(out=qc_h[:, (p % 2) * 512:(p % 2 + 1) * 512],
                            in_=qT_hi[r0:r0 + 128, qr * 512:(qr + 1) * 512]
                            ).then_inc(qs, 16)
                s.dma_start(out=qc_l[:, (p % 2) * 512:(p % 2 + 1) * 512],
                            in_=qT_lo[r0:r0 + 128, qr * 512:(qr + 1) * 512]
                            ).then_inc(qs, 16)
                for ck in range(qr + 1):
                    ks = KSL[ck % 2]
                    if kslot_use[(p, ck)] >= 2:
                        s.wait_ge(ks, 64 * (kslot_use[(p, ck)] - 1))
                    if kslot_prev_end[(p, ck)] is not None:
                        s.wait_ge(S["Vf"], kslot_prev_end[(p, ck)])
                    c2 = ck % 2
                    s.dma_start(out=kc_h[:, c2 * 512:(c2 + 1) * 512],
                                in_=kT_hi[r0:r0 + 128, ck * 512:(ck + 1) * 512]
                                ).then_inc(ks, 16)
                    s.dma_start(out=kc_l[:, c2 * 512:(c2 + 1) * 512],
                                in_=kT_lo[r0:r0 + 128, ck * 512:(ck + 1) * 512]
                                ).then_inc(ks, 16)
                    vap = [[FOWN, 128], [128 * FOWN, 4], [1, 128]]
                    voff = ck * 512 * FOWN + r0
                    s.dma_start(out=vc_h[:, c2 * 512:(c2 + 1) * 512],
                                in_=bass.AP(v_sh.tensor, voff, list(vap))
                                ).then_inc(ks, 16)
                    s.dma_start(out=vc_l[:, c2 * 512:(c2 + 1) * 512],
                                in_=bass.AP(v_sl.tensor, voff, list(vap))
                                ).then_inc(ks, 16)

        LD2 = 16 * 12
        @b2.vector
        def _(v):
            v = _AD(v)
            v.wait_ge(S["ld"], LD2)
            v.tensor_copy(wp_h[:, :], wpb[:, :])
            v.tensor_sub(wp_l[:, :], wpb[:, :], wp_h[:, :]).then_inc(S["rr"], 1)
            for p in range(NP):
                qr, h = divmod(p, 2)
                KT = kt_cnt[p]
                for j in range(KT):
                    g = kt_base[p] + j
                    g2 = g % 3
                    v.wait_ge(S["Pf"], g + 1)
                    if g >= 3:
                        v.wait_ge(S["Vf"], g - 2)
                    if j >= 4 * qr:
                        v.wait_ge(S["ld"], LD2)
                        jj = j - 4 * qr
                        v.tensor_mul(P32[:, g2 * 512:(g2 + 1) * 512],
                                     P32[:, g2 * 512:(g2 + 1) * 512],
                                     masks[:, jj * 512:(jj + 1) * 512])
                    v.tensor_copy(Ph[:, g2 * 512:(g2 + 1) * 512],
                                  P32[:, g2 * 512:(g2 + 1) * 512])
                    v.tensor_sub(Pl[:, g2 * 512:(g2 + 1) * 512],
                                 P32[:, g2 * 512:(g2 + 1) * 512],
                                 Ph[:, g2 * 512:(g2 + 1) * 512]).then_inc(S["Pv"], 1)
                h5 = h * 512
                v.wait_ge(S["Of"], p + 1)
                v.reciprocal(rb_sb[:, h5:h5 + 512], den[:, :]).then_inc(S["dn"], 1)
                if qr >= 1:
                    v.wait_ge(S["pj"], 16 * qr)
                v.tensor_mul(On32[:, h5:h5 + 512], Opb[h][:, :],
                             rb_sb[:, h5:h5 + 512]).then_inc(S["no"], 1)
                v.wait_ge(S["oh"], p + 1)
                v.tensor_sub(Ol[:, h5:h5 + 512], On32[:, h5:h5 + 512],
                             Oh[:, h5:h5 + 512]).then_inc(S["no"], 1)
                if h == 1:
                    for f in range(16):
                        pf = 16 * qr + f
                        v.wait_ge(S["pj"], pf + 1)
                        if pf >= 2:
                            v.wait_ge(S["pd"], 16 * (pf - 1))
                        v.tensor_copy(pj_sb[:, (pf % 2) * 512:(pf % 2 + 1) * 512],
                                      pjb[pf % 2][:, :]).then_inc(S["pe"], 1)

        @b2.scalar
        def _(sc):
            sc = _AD(sc)
            for p in range(NP):
                qr, h = divmod(p, 2)
                KT = kt_cnt[p]
                for j in range(KT):
                    g = kt_base[p] + j
                    g2 = g % 3
                    sc.wait_ge(S["Sf"], g + 1)
                    if g >= 3:
                        sc.wait_ge(S["Pv"], g - 2)
                    sc.activation(P32[:, g2 * 512:(g2 + 1) * 512],
                                  spb[g % 3][:, :], AF.Exp,
                                  bias=ESHIFT).then_inc(S["Pf"], 1)
                h5 = h * 512
                sc.wait_ge(S["no"], 2 * p + 1)
                sc.copy(Oh[:, h5:h5 + 512],
                        On32[:, h5:h5 + 512]).then_inc(S["oh"], 1)

        @b2.tensor
        def _(t_):
            t_ = _AD(t_)
            for p in range(NP):
                qr, h = divmod(p, 2)
                KT = kt_cnt[p]

                def s_tile(j):
                    g = kt_base[p] + j
                    g2 = g % 3
                    ck, kk = divmod(j, 4)
                    c2 = ck % 2
                    t_.wait_ge(QSL[p % 2], 32 * qslot_use[p])
                    t_.wait_ge(KSL[ck % 2], 64 * kslot_use[(p, ck)])
                    if g >= 3:
                        t_.wait_ge(S["Pf"], g - 2)
                    kh = kc_h[:, c2 * 512 + kk * 128:c2 * 512 + (kk + 1) * 128]
                    kl = kc_l[:, c2 * 512 + kk * 128:c2 * 512 + (kk + 1) * 128]
                    qh = qc_h[:, (p % 2) * 512:(p % 2 + 1) * 512]
                    ql = qc_l[:, (p % 2) * 512:(p % 2 + 1) * 512]
                    t_.matmul(spb[g2][:, :], kh, qh, start=True, stop=False)
                    t_.matmul(spb[g2][:, :], kl, qh, start=False, stop=False)
                    t_.matmul(spb[g2][:, :], kh, ql, start=False,
                              stop=True).then_inc(S["Sf"], 1)

                def pv_tile(j):
                    g = kt_base[p] + j
                    g2 = g % 3
                    ck, kk = divmod(j, 4)
                    c2 = ck % 2
                    t_.wait_ge(S["Pv"], g + 1)
                    first = (j == 0)
                    last = (j == KT - 1)
                    if first:
                        t_.wait_ge(S["no"], max(0, 2 * p - 3))
                        if p >= 1:
                            t_.wait_ge(S["dn"], p)
                    t_.wait_ge(KSL[ck % 2], 64 * kslot_use[(p, ck)])
                    vh = vc_h[:, c2 * 512 + kk * 128:c2 * 512 + (kk + 1) * 128]
                    vl = vc_l[:, c2 * 512 + kk * 128:c2 * 512 + (kk + 1) * 128]
                    ph = Ph[:, g2 * 512:(g2 + 1) * 512]
                    pl = Pl[:, g2 * 512:(g2 + 1) * 512]
                    t_.matmul(Opb[h][:, :], vh, ph, start=first, stop=False)
                    t_.matmul(Opb[h][:, :], vh, pl, start=False, stop=False)
                    t_.matmul(Opb[h][:, :], vl, ph, start=False, stop=last
                              ).then_inc(S["Vf"], 1)
                    t_.matmul(den[:, :], ones_sq[:, :], ph, start=first, stop=False)
                    mm = t_.matmul(den[:, :], ones_sq[:, :], pl, start=False,
                                   stop=last)
                    if last:
                        mm.then_inc(S["Of"], 1)

                for j in range(KT):
                    s_tile(j)
                    if j >= 2:
                        pv_tile(j - 2)
                if KT >= 2:
                    pv_tile(KT - 2)
                pv_tile(KT - 1)
                if h == 1:
                    for f in range(16):
                        tt, dg = divmod(f, 4)
                        pf = 16 * qr + f
                        t_.wait_ge(S["no"], 2 * p + 2)
                        if f >= 2:
                            t_.wait_ge(S["pe"], pf - 1)
                        else:
                            t_.wait_ge(S["rr"], 1)
                            t_.wait_ge(S["pe"], 16 * qr)
                        ps = pjb[pf % 2]
                        for ch in range(2):
                            oh = Oh[:, ch * 512 + tt * 128:ch * 512 + (tt + 1) * 128]
                            ol = Ol[:, ch * 512 + tt * 128:ch * 512 + (tt + 1) * 128]
                            mh = wp_h[:, ch * D + dg * 512:ch * D + (dg + 1) * 512]
                            ml = wp_l[:, ch * D + dg * 512:ch * D + (dg + 1) * 512]
                            t_.matmul(ps[:, :], oh, mh, start=(ch == 0), stop=False)
                            t_.matmul(ps[:, :], ol, mh, start=False, stop=False)
                            mmp = t_.matmul(ps[:, :], oh, ml, start=False,
                                            stop=(ch == 1))
                            if ch == 1:
                                mmp.then_inc(S["pj"], 1)

        @b2.gpsimd
        def _(g):
            g = _AD(g)
            for qr in range(QR):
                for f in range(16):
                    tt, dg = divmod(f, 4)
                    pf = 16 * qr + f
                    if pf >= 1:
                        g.wait_ge(S["pd"], 16 * pf)
                    g.wait_ge(S["pe"], pf + 1)
                    g.dma_start(
                        out=proj_part[qr * 512 + tt * 128:qr * 512 + (tt + 1) * 128,
                                      dg * 512:(dg + 1) * 512],
                        in_=pj_sb[:, (pf % 2) * 512:(pf % 2 + 1) * 512]
                    ).then_inc(S["pd"], 16)
            g.wait_ge(S["pd"], 16 * 16 * QR)
            g.wait_ge(S["cc"], 3)
            g.collective_compute(
                "ReduceScatter", OP.add, replica_groups=[list(range(NCORE))],
                ins=[proj_part.tensor.ap().opt()],
                outs=[rs_out.tensor.ap().opt()]).then_inc(S["cc"], 1)


def _phase3_ln(nc, S, env):
    T, DFF = env["T"], env["DFF"]
    TOK, TOKT = env["TOK"], env["TOKT"]
    NFT = env["NFT"]
    x_own, b1_in = env["x_own"], env["b1_in"]
    ident = env["ident"]
    st = env["st"]
    _esx = st.enter_context(ExitStack())
    if True:
        xnT_h = _esx.enter_context(nc.sbuf_tensor("xnT_h", [128, 16 * TOK], BF16))
        xnT_l = _esx.enter_context(nc.sbuf_tensor("xnT_l", [128, 16 * TOK], BF16))
        b1_sb = _esx.enter_context(nc.sbuf_tensor("b1_sb", [128, NFT], F32))
        with ExitStack() as _es:
            xg3 = _es.enter_context(nc.sbuf_tensor("xg3", [128, 3 * D], F32))
            sqb3 = _es.enter_context(nc.sbuf_tensor("sqb3", [128, D], F32))
            xn33 = _es.enter_context(nc.sbuf_tensor("xn33", [128, 2 * D], F32))
            xnh3 = _es.enter_context(nc.sbuf_tensor("xnh3", [128, 2 * D], BF16))
            xnl3 = _es.enter_context(nc.sbuf_tensor("xnl3", [128, 2 * D], BF16))
            sm3 = _es.enter_context(nc.sbuf_tensor("sm3", [128, 16], F32))
            tq0 = _es.enter_context(nc.psum_tensor("tq0", [128, 1024], BF16))
            tq1 = _es.enter_context(nc.psum_tensor("tq1", [128, 1024], BF16))
            b3l = _es.enter_context(nc.Block())
            tqb = [tq0, tq1]
            @b3l.sync
            def _(s):
                s = _AD(s)
                s.dma_start(out=b1_sb[:, :], in_=b1_in[:, :]).then_inc(S["xo"], 16)
                for t in range(TOKT):
                    if t >= 1:
                        s.wait_ge(S["xo"], 16 * (t + 1))
                    if t >= 3:
                        s.wait_ge(S["la"], t - 2)
                        s.wait_ge(S["ldd"], t - 2)
                    s.dma_start(out=xg3[:, (t % 3) * D:(t % 3 + 1) * D],
                                in_=x_own[t * 128:(t + 1) * 128, :]
                                ).then_inc(S["xo"], 16)

            @b3l.scalar
            def _(sc):
                sc = _AD(sc)
                for i in range(TOKT + 4):
                    t = i
                    if t < TOKT:
                        sc.wait_ge(S["xo"], 16 * (t + 2))
                        if t >= 2:
                            sc.wait_ge(S["lb"], t - 1)
                        sc.activation(sqb3[:, :], xg3[:, (t % 3) * D:(t % 3 + 1) * D],
                                      AF.Identity, bias=f32(0.0),
                                      accum_out=sm3[:, t % 2:t % 2 + 1])
                        sc.activation(sqb3[:, :], xg3[:, (t % 3) * D:(t % 3 + 1) * D],
                                      AF.Square, bias=f32(0.0),
                                      accum_out=sm3[:, 2 + t % 2:3 + t % 2]
                                      ).then_inc(S["la"], 1)
                    t = i - 1
                    if 0 <= t < TOKT:
                        sc.wait_ge(S["lb"], t + 1)
                        if t >= 2:
                            sc.wait_ge(S["ldd"], t - 1)
                        sc.activation(sm3[:, 8 + t % 2:9 + t % 2],
                                      sm3[:, 6 + t % 2:7 + t % 2],
                                      AF.Sqrt, bias=f32(0.0)).then_inc(S["lc"], 1)
                    t = i - 2
                    if 0 <= t < TOKT:
                        sc.wait_ge(S["ldd"], t + 1)
                        if t >= 2:
                            sc.wait_ge(S["t3"], 4 * (t - 1))
                        sc.copy(xnh3[:, (t % 2) * D:(t % 2 + 1) * D],
                                xn33[:, (t % 2) * D:(t % 2 + 1) * D]
                                ).then_inc(S["le"], 1)
                    t = i - 3
                    if 0 <= t < TOKT:
                        for b4 in range(4):
                            fl = 4 * t + b4
                            sc.wait_ge(S["t3"], fl + 1)
                            k0 = b4 * 4
                            dsth = bass.AP(xnT_h, k0 * TOK + t * 128,
                                           [[16 * TOK, 128], [TOK, 4], [1, 128]])
                            dstl = bass.AP(xnT_l, k0 * TOK + t * 128,
                                           [[16 * TOK, 128], [TOK, 4], [1, 128]])
                            src = tqb[fl % 2]
                            sph = bass.AP(src, 0, [[1024, 128], [256, 4], [1, 128]])
                            spl = bass.AP(src, 128, [[1024, 128], [256, 4], [1, 128]])
                            sc.copy(dsth, sph)
                            sc.copy(dstl, spl).then_inc(S["e3"], 1)

            @b3l.vector
            def _(v):
                v = _AD(v)
                for i in range(TOKT + 4):
                    t = i
                    if t < TOKT:
                        v.wait_ge(S["la"], t + 1)
                        if t >= 2:
                            v.wait_ge(S["lc"], t - 1)
                        a, b_, c, dcol = t % 2, 2 + t % 2, 4 + t % 2, 6 + t % 2
                        v.tensor_scalar_mul(sm3[:, c:c + 1], sm3[:, a:a + 1],
                                            -1.0 / D)
                        v.tensor_scalar_mul(sm3[:, dcol:dcol + 1],
                                            sm3[:, b_:b_ + 1], 1.0 / D)
                        v.tensor_mul(sm3[:, 12:13], sm3[:, c:c + 1], sm3[:, c:c + 1])
                        v.tensor_sub(sm3[:, dcol:dcol + 1], sm3[:, dcol:dcol + 1],
                                     sm3[:, 12:13])
                        v.tensor_scalar_add(sm3[:, dcol:dcol + 1],
                                            sm3[:, dcol:dcol + 1],
                                            float(EPS)).then_inc(S["lb"], 1)
                    t = i - 1
                    if 0 <= t < TOKT:
                        v.wait_ge(S["lc"], t + 1)
                        if t >= 2:
                            v.wait_ge(S["le"], t - 1)
                        r, c, dcol = 10 + t % 2, 4 + t % 2, 6 + t % 2
                        v.reciprocal(sm3[:, r:r + 1], sm3[:, 8 + t % 2:9 + t % 2])
                        v.tensor_mul(sm3[:, 12:13], sm3[:, r:r + 1], sm3[:, r:r + 1])
                        v.tensor_mul(sm3[:, 12:13], sm3[:, 12:13],
                                     sm3[:, dcol:dcol + 1])
                        v.tensor_scalar(sm3[:, 12:13], sm3[:, 12:13], -0.5,
                                        1.5, OP.mult, OP.add)
                        v.tensor_mul(sm3[:, r:r + 1], sm3[:, r:r + 1], sm3[:, 12:13])
                        v.tensor_mul(sm3[:, 13 + t % 2:14 + t % 2],
                                     sm3[:, c:c + 1], sm3[:, r:r + 1])
                        v.scalar_tensor_tensor(
                            xn33[:, (t % 2) * D:(t % 2 + 1) * D],
                            xg3[:, (t % 3) * D:(t % 3 + 1) * D],
                            sm3[:, r:r + 1],
                            sm3[:, 13 + t % 2:14 + t % 2].broadcast_to([128, D]),
                            OP.mult, OP.add).then_inc(S["ldd"], 1)
                    t = i - 2
                    if 0 <= t < TOKT:
                        v.wait_ge(S["le"], t + 1)
                        if t >= 2:
                            v.wait_ge(S["t3"], 4 * (t - 1))
                        v.tensor_sub(xnl3[:, (t % 2) * D:(t % 2 + 1) * D],
                                     xn33[:, (t % 2) * D:(t % 2 + 1) * D],
                                     xnh3[:, (t % 2) * D:(t % 2 + 1) * D]
                                     ).then_inc(S["lf"], 1)

            @b3l.tensor
            def _(t_):
                t_ = _AD(t_)
                for t in range(TOKT):
                    t_.wait_ge(S["lf"], t + 1)
                    for b4 in range(4):
                        fl = 4 * t + b4
                        if fl >= 2:
                            t_.wait_ge(S["e3"], fl - 1)
                        for ii in range(8):
                            idx = b4 * 8 + ii
                            k, hl = divmod(idx, 2)
                            src = (xnh3 if hl == 0 else xnl3)
                            mm = t_.transpose(
                                tqb[fl % 2][:, ii * 128:(ii + 1) * 128],
                                src[:, (t % 2) * D + k * 128:(t % 2) * D + (k + 1) * 128],
                                ident[:, :])
                        mm.then_inc(S["t3"], 1)

    return {"xnT_h": xnT_h, "xnT_l": xnT_l, "b1_sb": b1_sb, "esx": _esx}


def _phase3_ffn(nc, S, ln_scope, env):
    T, DFF = env["T"], env["DFF"]
    TOK, TOKT = env["TOK"], env["TOKT"]
    NFT, NSLAB = env["NFT"], env["NSLAB"]
    TN, NTH = env["TN"], env["NTH"]
    W1S, W2S = env["W1S"], env["W2S"]
    x_own, bp_in, b2_in = env["x_own"], env["bp_in"], env["b2_in"]
    wq1_all, wq2_all = env["wq1_all"], env["wq2_all"]
    ffn_acc, rs_out, out_own = env["ffn_acc"], env["rs_out"], env["out_own"]
    phases = env["phases"]
    FPS = 4
    NFILL = TOKT * 4
    xnT_h = ln_scope["xnT_h"]
    xnT_l = ln_scope["xnT_l"]
    b1_sb = ln_scope["b1_sb"]
    if True:
        E3TOT = 4 * TOKT
        with ExitStack() as _es:
            w1s = _es.enter_context(nc.sbuf_tensor("w1s", [128, 2 * 16 * 512], BF16))
            w2s = _es.enter_context(nc.sbuf_tensor("w2s", [128, 8 * D], BF16))
            hsl_h = _es.enter_context(nc.sbuf_tensor("hsl", [128, 2 * FPS * TOK], BF16))
            hsl_l = _es.enter_context(nc.sbuf_tensor("hsll", [128, 2 * FPS * TOK], BF16))
            h32 = _es.enter_context(nc.sbuf_tensor("h32", [128, 2 * 512], F32))
            f2sb = _es.enter_context(nc.sbuf_tensor("f2sb", [128, 2 * D], F32))
            ff0 = _es.enter_context(nc.psum_tensor("ff0", [128, 512], F32))
            ff1 = _es.enter_context(nc.psum_tensor("ff1", [128, 512], F32))
            ff2 = _es.enter_context(nc.psum_tensor("ff2", [128, 512], F32))
            ff3 = _es.enter_context(nc.psum_tensor("ff3", [128, 512], F32))
            ff4 = _es.enter_context(nc.psum_tensor("ff4", [128, 512], F32))
            ff5 = _es.enter_context(nc.psum_tensor("ff5", [128, 512], F32))
            ff6 = _es.enter_context(nc.psum_tensor("ff6", [128, 512], F32))
            ff7 = _es.enter_context(nc.psum_tensor("ff7", [128, 512], F32))
            b3a = _es.enter_context(nc.Block())
            ffb = [ff0, ff1, ff2, ff3, ff4, ff5, ff6, ff7]

            W1SL = [S["w1"], S["w1b"]]
            W2SL = [S["w2"], S["w2b"]]

            @b3a.sync
            def _(s):
                s = _AD(s)
                s.wait_ge(S["cc"], 3)
                for sl in range(NSLAB):
                    if sl >= 2:
                        s.wait_ge(W1SL[sl % 2], 16 * 16 * (sl // 2))
                        s.wait_ge(S["f1"], 4 * NTH * (sl - 1))
                    for k in range(16):
                        if W1S >= 512:
                            blk, off = divmod(sl * 512, W1S)
                            ap1 = bass.AP(wq1_all.tensor,
                                          blk * D * W1S + (k * 128) * W1S + off,
                                          [[W1S, 128], [1, 512]])
                        else:
                            nb = 512 // W1S
                            blk = sl * 512 // W1S
                            ap1 = bass.AP(wq1_all.tensor,
                                          blk * D * W1S + (k * 128) * W1S,
                                          [[W1S, 128], [D * W1S, nb], [1, W1S]])
                        s.dma_start(
                            out=w1s[:, (sl % 2) * (16 * 512) + k * 512:
                                    (sl % 2) * (16 * 512) + (k + 1) * 512],
                            in_=ap1).then_inc(W1SL[sl % 2], 16)
                    if sl >= 2:
                        s.wait_ge(W2SL[sl % 2], 16 * 4 * (sl // 2))
                        s.wait_ge(S["f2"], NFILL * (sl - 1))
                    for jj in range(FPS):
                        fc = FPS * sl + jj
                        ap2 = bass.AP(wq2_all.tensor, (fc * 128) * W2S,
                                      [[W2S, 128], [DFF * W2S, 8], [1, W2S]])
                        s.dma_start(
                            out=w2s[:, ((sl % 2) * 4 + jj) * D:((sl % 2) * 4 + jj + 1) * D],
                            in_=ap2).then_inc(W2SL[sl % 2], 16)

            def _ffn2_slab(t_, sl):
                hb = (sl % 2) * FPS * TOK
                for fi in range(NFILL):
                    tt, dg = divmod(fi, 4)
                    ff = NFILL * sl + fi
                    t_.wait_ge(S["h"], 2 * FPS * NTH * (sl + 1))
                    t_.wait_ge(S["g1"], min(4 * NTH * (sl + 2), 4 * NTH * NSLAB))
                    if ff >= 2:
                        t_.wait_ge(S["fe"], ff - 1)
                    ps = ffb[ff % 2]
                    for jj in range(FPS):
                        fc = FPS * sl + jj
                        t_.wait_ge(W2SL[sl % 2], 16 * 4 * (sl // 2 + 1))
                        th, trem = divmod(tt * 128, TN)
                        sh = hsl_h[:, hb + (jj * NTH + th) * TN + trem:
                                   hb + (jj * NTH + th) * TN + trem + 128]
                        sll = hsl_l[:, hb + (jj * NTH + th) * TN + trem:
                                    hb + (jj * NTH + th) * TN + trem + 128]
                        mv = w2s[:, ((sl % 2) * 4 + jj) * D + dg * 512:
                                 ((sl % 2) * 4 + jj) * D + (dg + 1) * 512]
                        t_.matmul(ps[:, :], sh, mv, start=(jj == 0), stop=False)
                        mm2 = t_.matmul(ps[:, :], sll, mv, start=False,
                                        stop=(jj == FPS - 1))
                        if jj == FPS - 1:
                            mm2.then_inc(S["f2"], 1)

            @b3a.tensor
            def _(t_):
                t_ = _AD(t_)
                t_.wait_ge(S["e3"], E3TOT)
                for sl in range(NSLAB):
                    for th in range(NTH):
                        for k in range(16):
                            t_.wait_ge(W1SL[sl % 2], 16 * 16 * (sl // 2 + 1))
                            for jj in range(4):
                                bk = ((sl * NTH + th) % 2) * 4 + jj
                                prev = (sl * NTH + th - 2)
                                if prev >= 0:
                                    t_.wait_ge(S["g1"], 4 * prev + jj + 1)
                                if sl >= 2 and th == 0 and bk <= 1:
                                    t_.wait_ge(S["fe"], NFILL * (sl - 1))
                                st_h = w1s[:, (sl % 2) * (16 * 512) + k * 512 + jj * 128:
                                           (sl % 2) * (16 * 512) + k * 512 + (jj + 1) * 128]
                                xh = xnT_h[:, k * TOK + th * TN:k * TOK + (th + 1) * TN]
                                xl = xnT_l[:, k * TOK + th * TN:k * TOK + (th + 1) * TN]
                                t_.matmul(ffb[bk][:, 0:TN], st_h, xh,
                                          start=(k == 0), stop=False)
                                mm = t_.matmul(ffb[bk][:, 0:TN], st_h, xl,
                                               start=False, stop=(k == 15))
                                if k == 15:
                                    mm.then_inc(S["f1"], 1)
                    if sl >= 1:
                        _ffn2_slab(t_, sl - 1)
                _ffn2_slab(t_, NSLAB - 1)

            @b3a.scalar
            def _(sc):
                sc = _AD(sc)
                sc.wait_ge(S["xo"], 32)
                for sl in range(NSLAB):
                    for th in range(NTH):
                        for jj in range(4):
                            gi = 4 * (sl * NTH + th) + jj
                            bk = ((sl * NTH + th) % 2) * 4 + jj
                            sc.wait_ge(S["f1"], gi + 1)
                            if gi >= 2:
                                sc.wait_ge(S["h"], 2 * (gi - 1))
                            ft = 4 * sl + jj
                            sc.activation(h32[:, (gi % 2) * 512:(gi % 2) * 512 + TN],
                                          ffb[bk][:, 0:TN], AF.Gelu,
                                          bias=b1_sb[:, ft:ft + 1]
                                          ).then_inc(S["g1"], 1)

            F2D = [S["f2d%d" % i] for i in range(8)]

            def _f2evacs(v, sl):
                for fi in range(NFILL):
                    tt, dg = divmod(fi, 4)
                    ff = NFILL * sl + fi
                    gidx = ff // 4
                    v.wait_ge(S["f2"], ff + 1)
                    if gidx >= 2:
                        pg = gidx - 2
                        psl, ptt = divmod(pg, TOKT)
                        v.wait_ge(F2D[ptt], 16 * (psl + 1))
                    v.tensor_copy(
                        f2sb[:, (gidx % 2) * D + dg * 512:(gidx % 2) * D + (dg + 1) * 512],
                        ffb[ff % 2][:, :]).then_inc(S["fe"], 1)

            @b3a.vector
            def _(v):
                v = _AD(v)
                for sl in range(NSLAB):
                    hb = (sl % 2) * FPS * TOK
                    for th in range(NTH):
                        for jj in range(4):
                            gi = 4 * (sl * NTH + th) + jj
                            if sl >= 2 and th == 0 and jj == 0:
                                v.wait_ge(S["f2"], NFILL * (sl - 1))
                            v.wait_ge(S["g1"], gi + 1)
                            dsth = hsl_h[:, hb + (jj * NTH + th) * TN:
                                         hb + (jj * NTH + th + 1) * TN]
                            dstl = hsl_l[:, hb + (jj * NTH + th) * TN:
                                         hb + (jj * NTH + th + 1) * TN]
                            v.tensor_copy(dsth, h32[:, (gi % 2) * 512:(gi % 2) * 512 + TN])
                            v.tensor_sub(dstl, h32[:, (gi % 2) * 512:(gi % 2) * 512 + TN],
                                         dsth).then_inc(S["h"], 2)
                    if sl >= 1:
                        _f2evacs(v, sl - 1)
                _f2evacs(v, NSLAB - 1)

            @b3a.gpsimd
            def _(g):
                g = _AD(g)
                for sl in range(NSLAB):
                    for tt in range(TOKT):
                        gidx = sl * TOKT + tt
                        if sl >= 1:
                            g.wait_ge(F2D[tt], 16 * sl)
                        g.wait_ge(S["fe"], 4 * (gidx + 1))
                        g.dma_start(
                            out=ffn_acc[tt * 128:(tt + 1) * 128, :],
                            in_=f2sb[:, (gidx % 2) * D:(gidx % 2 + 1) * D],
                            accum_op=(OP.add if sl >= 1 else OP.bypass)
                        ).then_inc(F2D[tt], 16)
                for tt in range(TOKT):
                    g.wait_ge(F2D[tt], 16 * NSLAB)

        with ExitStack() as _es:
            rs_sb = _es.enter_context(nc.sbuf_tensor("rs_sb", [128, 2 * D], F32))
            xf_sb = _es.enter_context(nc.sbuf_tensor("xf_sb", [128, 2 * D], F32))
            fa_sb = _es.enter_context(nc.sbuf_tensor("fa_sb", [128, 2 * D], F32))
            bp_sb = _es.enter_context(nc.sbuf_tensor("bp_sb", [128, D], F32))
            b22_sb = _es.enter_context(nc.sbuf_tensor("b22_sb", [128, D], F32))
            cmb = _es.enter_context(nc.sbuf_tensor("cmb", [128, 2 * D], F32))
            b3b = _es.enter_context(nc.Block())
            R3SL = [S["r3"], S["r3b"]]

            @b3b.sync
            def _(s):
                s = _AD(s)
                s.wait_ge(S["cc"], 4)
                s.wait_ge(S["ld"], 16 * 12)
                s.dma_start(out=bp_sb[:, :], in_=bp_in[:, :]).then_inc(S["ld"], 16)
                s.dma_start(out=b22_sb[:, :], in_=b2_in[:, :]).then_inc(S["ld"], 16)
                s.wait_ge(S["ld"], 16 * 14)
                for _tt in range(TOKT):
                    s.wait_ge(S["f2d%d" % _tt], 16 * NSLAB)
                for t in range(TOKT):
                    if t >= 2:
                        s.wait_ge(R3SL[t % 2], 48 * (t // 2))
                        s.wait_ge(S["o3"], t - 1)
                    s.dma_start(out=rs_sb[:, (t % 2) * D:(t % 2 + 1) * D],
                                in_=rs_out[t * 128:(t + 1) * 128, :]).then_inc(R3SL[t % 2], 16)
                    s.dma_start(out=xf_sb[:, (t % 2) * D:(t % 2 + 1) * D],
                                in_=x_own[t * 128:(t + 1) * 128, :]).then_inc(R3SL[t % 2], 16)
                    s.dma_start(out=fa_sb[:, (t % 2) * D:(t % 2 + 1) * D],
                                in_=ffn_acc[t * 128:(t + 1) * 128, :]).then_inc(R3SL[t % 2], 16)

            @b3b.vector
            def _(v):
                v = _AD(v)
                v.wait_ge(S["ld"], 16 * 14)
                v.tensor_add(bp_sb[:, :], bp_sb[:, :], b22_sb[:, :])
                for t in range(TOKT):
                    v.wait_ge(R3SL[t % 2], 48 * (t // 2 + 1))
                    if t >= 2:
                        v.wait_ge(S["stq"], 16 * (t - 1))
                    sl_ = slice((t % 2) * D, (t % 2 + 1) * D)
                    v.tensor_add(cmb[:, sl_], rs_sb[:, sl_], fa_sb[:, sl_])
                    v.tensor_add(cmb[:, sl_], cmb[:, sl_], xf_sb[:, sl_])
                    v.tensor_add(cmb[:, sl_], cmb[:, sl_],
                                 bp_sb[:, :]).then_inc(S["o3"], 1)

            @b3b.gpsimd
            def _(g):
                g = _AD(g)
                for t in range(TOKT):
                    g.wait_ge(S["o3"], t + 1)
                    g.dma_start(out=out_own[t * 128:(t + 1) * 128, :],
                                in_=cmb[:, (t % 2) * D:(t % 2 + 1) * D]
                                ).then_inc(S["stq"], 16)
                g.wait_ge(S["stq"], 16 * TOKT)


def stage_inputs(x, W_qkv, b_qkv, W_proj, b_proj, w1, b1, w2, b2,
                 T=8192, DFF=8192):
    import ml_dtypes
    TOK = T // NCORE
    NFT = DFF // 128
    x2 = np.ascontiguousarray(np.asarray(x, dtype=np.float32).reshape(T, D))
    W_qkv = np.asarray(W_qkv, np.float32)
    b_qkv = np.asarray(b_qkv, np.float32)
    W_proj = np.asarray(W_proj, np.float32)
    b_proj = np.asarray(b_proj, np.float32)
    w1 = np.asarray(w1, np.float32)
    b1 = np.asarray(b1, np.float32)
    w2 = np.asarray(w2, np.float32)
    b2 = np.asarray(b2, np.float32)
    NQ = W_qkv.shape[1] // 3
    ident = np.eye(128, dtype=ml_dtypes.bfloat16)
    masks = np.zeros((4, 128, 512), np.float32)
    for j in range(4):
        kk = np.arange(128)[:, None] + j * 128
        qq = np.arange(512)[None, :]
        masks[j] = (qq >= kk).astype(np.float32)
    in_maps = []
    for c in range(NCORE):
        fo = c * FOWN
        d = {
            "x_full": x2,
            "x_own": np.ascontiguousarray(x2[c * TOK:(c + 1) * TOK]),
            "wq_in": np.ascontiguousarray(W_qkv[:, fo:fo + FOWN]),
            "wk_in": np.ascontiguousarray(W_qkv[:, NQ + fo:NQ + fo + FOWN]),
            "wv_in": np.ascontiguousarray(W_qkv[:, 2 * NQ + fo:2 * NQ + fo + FOWN]),
            "bq_in": np.ascontiguousarray(
                b_qkv[fo:fo + FOWN].reshape(2, 128).T),
            "bk_in": np.ascontiguousarray(
                b_qkv[NQ + fo:NQ + fo + FOWN].reshape(2, 128).T),
            "bv_in": np.ascontiguousarray(np.broadcast_to(
                b_qkv[2 * NQ + fo:2 * NQ + fo + FOWN][None, :], (128, FOWN))),
            "wp_in": np.ascontiguousarray(W_proj[fo:fo + FOWN, :]),
            "bp_in": np.ascontiguousarray(
                np.broadcast_to(b_proj[None, :], (128, D))),
            "b2_in": np.ascontiguousarray(
                np.broadcast_to(b2[None, :], (128, D))),
            "b1_in": np.ascontiguousarray(b1.reshape(NFT, 128).T),
            "w1_sh": np.ascontiguousarray(
                w1[c * (DFF // 8):(c + 1) * (DFF // 8), :]),
            "w2_sh": np.ascontiguousarray(
                w2[c * (D // 8):(c + 1) * (D // 8), :]),
            "id_in": ident,
            "mk_in": masks,
        }
        in_maps.append(d)
    return in_maps


def gather_output(results, T=8192):
    TOK = T // NCORE
    out = np.concatenate([np.asarray(r["out_own"]) for r in results], axis=0)
    return out.reshape(1, T, D)


from concourse.bass_utils import run_bass_kernel_spmd

_NC_CACHE = {}


def kernel(L0_attn_x, L0_attn_W_qkv, L0_attn_b_qkv, L0_attn_W_proj,
           L0_attn_b_proj, L0_ffn_w1, L0_ffn_b1, L0_ffn_w2, L0_ffn_b2):
    T, DFF = 8192, 8192
    if "nc" not in _NC_CACHE:
        _NC_CACHE["nc"] = build(T=T, DFF=DFF)
    nc = _NC_CACHE["nc"]
    in_maps = stage_inputs(L0_attn_x, L0_attn_W_qkv, L0_attn_b_qkv,
                           L0_attn_W_proj, L0_attn_b_proj,
                           L0_ffn_w1, L0_ffn_b1, L0_ffn_w2, L0_ffn_b2,
                           T=T, DFF=DFF)
    res = run_bass_kernel_spmd(nc, in_maps, core_ids=list(range(NCORE)))
    return gather_output(res.results, T=T).astype(np.float32)
